# version 07
# baseline (speedup 1.0000x reference)
import numpy as np
import ml_dtypes
import concourse.bass as bass
import concourse.mybir as mybir
from concourse.bass_utils import run_bass_kernel_spmd
from contextlib import ExitStack

F32 = mybir.dt.float32
BF16 = mybir.dt.bfloat16
AF = mybir.ActivationFunctionType
ALU = mybir.AluOpType
AX = mybir.AxisListType

T = 2048
D = 1024
KC = D // 128
NTC = T // 512
NTOK = T // 128
ARCOLS = 55296

ENGS = ['T', 'S', 'V', 'G', 'Y']


class Ctx:
    def __init__(self, nc):
        self.nc = nc
        self.es = ExitStack()
        self.q = {e: [] for e in ENGS}
        self.sems = {}
        self.semc = {}
        self.waited = {e: {p: 0 for p in ENGS} for e in ENGS}
        for e in ENGS:
            self.sems[e] = self.es.enter_context(nc.semaphore("sem" + e))
            self.semc[e] = 0

    def chan(self, name):
        if name not in self.sems:
            self.sems[name] = self.es.enter_context(self.nc.semaphore(name))
            self.semc[name] = 0
            for e in ENGS:
                self.waited[e][name] = 0
        return name

    def dma(self, eng, ch, fn, deps=()):
        self.chan(ch)
        for d in deps:
            if d is not None:
                self.wait(eng, d[0], d[1])
        sem = self.sems[ch]
        self.semc[ch] += 16
        def wrapped(e, fn=fn, sem=sem):
            fn(e).then_inc(sem, 16)
        self.q[eng].append(wrapped)
        return (ch, self.semc[ch])

    def sb(self, name, shape, dt):
        return self.es.enter_context(self.nc.sbuf_tensor(name, shape, dt))

    def ps(self, name, shape, dt):
        return self.es.enter_context(self.nc.psum_tensor(name, shape, dt))

    def wait(self, eng, prod, val):
        if val <= 0 or self.waited[eng].get(prod, 0) >= val:
            return
        self.waited[eng][prod] = val
        sem = self.sems[prod]
        self.q[eng].append(lambda e, sem=sem, v=val: e.wait_ge(sem, v))

    def op(self, eng, fn, deps=(), inc=1):
        for d in deps:
            if d is not None:
                self.wait(eng, d[0], d[1])
        if inc:
            sem = self.sems[eng]
            self.semc[eng] += inc
            def wrapped(e, fn=fn, sem=sem, n=inc):
                fn(e).then_inc(sem, n)
            self.q[eng].append(wrapped)
        else:
            self.q[eng].append(fn)
        return (eng, self.semc[eng])

    def tip(self, eng):
        return (eng, self.semc[eng])

    def phase(self):
        snap = dict(self.semc)
        for e in ENGS:
            for p in snap:
                if p != e:
                    self.wait(e, p, snap[p])

    def finish(self):
        nc = self.nc
        q = self.q
        with nc.Block() as block:
            @block.tensor
            def _(t):
                for fn in q['T']: fn(t)
            @block.scalar
            def _(s):
                for fn in q['S']: fn(s)
            @block.vector
            def _(v):
                for fn in q['V']: fn(v)
            @block.gpsimd
            def _(g):
                for fn in q['G']: fn(g)
            @block.sync
            def _(y):
                for fn in q['Y']: fn(y)
        self.es.close()
        return nc


class Bank:

    def __init__(self, c, i, bf16=False, view=None):
        self.c = c
        if view is not None:
            self.t = view
            self.f = view
        elif bf16:
            self.t = c.ps(f"bank{i}", [128, 1024], BF16)
            self.f = self.t[:, :].bitcast(F32)
        else:
            self.t = c.ps(f"bank{i}", [128, 512], F32)
            self.f = self.t[:, :]
        self.consumers = []
        self.ready = None

    def guard(self):
        for eng, cnt in self.consumers:
            self.c.wait('T', eng, cnt)
        self.consumers = []

    def consume(self, dep):
        self.consumers.append(dep)


class Pair:

    def __init__(self, c, j):
        self.t = c.ps(f"pair{j}", [128, 1024], F32)
        self.half = [Bank(c, 2 * j, view=self.t[:, 0:512]),
                     Bank(c, 2 * j + 1, view=self.t[:, 512:1024])]

    def guard(self):
        self.half[0].guard()
        self.half[1].guard()

    def consume(self, dep):
        self.half[0].consume(dep)
        self.half[1].consume(dep)


class Arena:
    def __init__(self, c, cols):
        self.c = c
        self.t = c.sb("AR", [128, cols], BF16)
        self.cols = cols
        self.off = 0
        self.stack = []

    def push(self):
        self.stack.append(self.off)

    def pop(self):
        self.off = self.stack.pop()

    def bf(self, ncols):
        o = self.off
        assert o + ncols <= self.cols, f"arena overflow {o}+{ncols}>{self.cols}"
        self.off += ncols
        return self.t[:, o:o + ncols]

    def bf3(self, n1, n2):
        return self.bf(n1 * n2).rearrange("p (a b) -> p a b", a=n1)

    def f32(self, n):
        if self.off % 2:
            self.off += 1
        o = self.off
        self.off += 2 * n
        assert self.off <= self.cols, "arena overflow (f32)"
        return self.t[:, o:o + 2 * n].bitcast(F32)

    def f32_3(self, n1, n2):
        return self.f32(n1 * n2).rearrange("p (a b) -> p a b", a=n1)


def host_consts():
    ident = np.eye(128, dtype=np.float32)
    csel = np.zeros((64, 65), np.float32); csel[:, 64] = 1.0
    tril = np.tril(np.ones((128, 128), np.float32))
    triu = np.triu(np.ones((128, 128), np.float32))
    m = np.zeros((128, 896), np.float32)
    jl = np.arange(128)[:, None]
    cc = np.arange(896)[None, :] - 384
    m[:] = np.where(jl >= cc, 0.0, -1e30)
    slopes = np.float32([2.0 ** (-0.5 * (i + 1)) for i in range(16)])
    l3slope = np.repeat(-slopes[:, None], T, axis=1).astype(np.float32)
    l3si = (-slopes[:, None] * np.arange(T, dtype=np.float32)[None, :]).astype(np.float32)
    iota = np.arange(T, dtype=np.float32)[None, :]
    iota128 = (np.arange(T) % 128).astype(np.float32)[None, :]
    itile = ((np.arange(T) // 128) * 128).astype(np.float32)[None, :]
    eyeb = np.zeros((32, 32 * 128), np.float32)
    for e in range(32):
        eyeb[e, e * 128:(e + 1) * 128] = 1.0
    tb32 = ((32.0 - np.arange(32, dtype=np.float32)) * 1e-30)[None, :]
    return {
        "c_ident": ident, "c_csel": csel, "c_tril": tril, "c_triu": triu,
        "c_l3mask": m, "c_l3slope": l3slope, "c_l3si": l3si, "c_iota": iota,
        "c_iota128": iota128, "c_itile": itile, "c_eyeb": eyeb,
        "c_tb32": tb32,
    }


def host_prep(z):
    f = np.float32
    cat = np.concatenate
    o = {}
    o["w_l0_wq"] = np.ascontiguousarray(z["L0_attn_W_Q"], f)
    o["w_l0_wk"] = np.ascontiguousarray(z["L0_attn_W_K"], f)
    o["w_l0_wv"] = np.ascontiguousarray(z["L0_attn_W_V"], f)
    o["w_l0_wg"] = np.ascontiguousarray(z["L0_ffn_Wg"], f)
    o["w_l0_w1"] = np.ascontiguousarray(
        cat([z["L0_ffn_W1"], z["L0_ffn_b1"][:, None, :]], axis=1), f)
    o["w_l0_w2"] = np.ascontiguousarray(z["L0_ffn_W2"], f)
    o["w_l0_b2"] = np.ascontiguousarray(z["L0_ffn_b2"], f)
    o["w_l0_b1c"] = np.ascontiguousarray(z["L0_ffn_b1"].reshape(8, 16, 128).transpose(0, 2, 1), f)
    o["w_l1_anw"] = np.ascontiguousarray(z["L1_attn_attn_norm_w"].reshape(KC, 128).T, f)
    o["w_l1_mnw"] = np.ascontiguousarray(z["L1_attn_mlp_norm_w"].reshape(KC, 128).T, f)
    o["w_l1_onw"] = np.ascontiguousarray(z["L1_attn_attn_out_norm_w"][:, None], f)
    for nm, key in [("wq", "L1_attn_Wq"), ("wk", "L1_attn_Wk"), ("wv", "L1_attn_Wv"),
                    ("wo", "L1_attn_Wo"), ("wgate", "L1_attn_Wgate"),
                    ("wup", "L1_attn_Wup"), ("wdown", "L1_attn_Wdown")]:
        o["w_l1_" + nm] = np.ascontiguousarray(z[key], f)
    o["w_l1_rwt"] = np.ascontiguousarray(z["L1_ffn_router_weight"].T, f)
    o["w_l1_rb"] = np.ascontiguousarray(z["L1_ffn_router_bias"][None, :], f)
    o["w_l1_gw"] = np.ascontiguousarray(z["L1_ffn_gate_w"].transpose(0, 2, 1), f)
    o["w_l1_uw"] = np.ascontiguousarray(z["L1_ffn_up_w"].transpose(0, 2, 1), f)
    o["w_l1_dw"] = np.ascontiguousarray(z["L1_ffn_down_w"].transpose(0, 2, 1), f)
    o["w_l1_sgw"] = np.ascontiguousarray(z["L1_ffn_shared_gate_w"].T, f)
    o["w_l1_suw"] = np.ascontiguousarray(z["L1_ffn_shared_up_w"].T, f)
    o["w_l1_sdw"] = np.ascontiguousarray(z["L1_ffn_shared_down_w"].T, f)
    for nm, wk, bk in [("wq", "L2_attn_Wq", "L2_attn_bq"), ("wk", "L2_attn_Wk", "L2_attn_bk"),
                       ("wv", "L2_attn_Wv", "L2_attn_bv"), ("wo", "L2_attn_Wo", "L2_attn_bo")]:
        o["w_l2_" + nm] = np.ascontiguousarray(cat([z[wk].T, z[bk][None, :]], axis=0), f)
    o["w_l2_bqc"] = np.ascontiguousarray(z["L2_attn_bq"].reshape(16, 64).T, f)
    o["w_l2_bkc"] = np.ascontiguousarray(z["L2_attn_bk"].reshape(16, 64).T, f)
    o["w_l2_rwt"] = np.ascontiguousarray(z["L2_ffn_router_w"].T, f)
    o["w_l2_rb"] = np.ascontiguousarray(z["L2_ffn_router_b"][None, :], f)
    o["w_l2_ew"] = np.ascontiguousarray(
        cat([z["L2_ffn_expert_w"].transpose(0, 2, 1),
             z["L2_ffn_expert_b"][:, None, :]], axis=1), f)
    o["w_l3_wqkv"] = np.ascontiguousarray(z["L3_attn_W_qkv"], f)
    o["w_l3_wout"] = np.ascontiguousarray(z["L3_attn_W_out"], f)
    o["w_l3_gwt"] = np.ascontiguousarray(z["L3_ffn_gate_w"].T, f)
    o["w_l3_w1"] = np.ascontiguousarray(
        cat([z["L3_ffn_w1"], z["L3_ffn_b1"][:, None, :]], axis=1), f)
    o["w_l3_w3"] = np.ascontiguousarray(
        cat([z["L3_ffn_w3"], z["L3_ffn_b3"][:, None, :]], axis=1), f)
    o["w_l3_w2"] = np.ascontiguousarray(z["L3_ffn_w2"], f)
    o["w_l3_b2"] = np.ascontiguousarray(z["L3_ffn_b2"], f)
    o["w_l3_b1c"] = np.ascontiguousarray(z["L3_ffn_b1"].reshape(8, 8, 128).transpose(0, 2, 1), f)
    o["w_l3_b3c"] = np.ascontiguousarray(z["L3_ffn_b3"].reshape(8, 8, 128).transpose(0, 2, 1), f)
    o["w_l3_sb1c"] = np.ascontiguousarray(z["L3_ffn_sb1"].reshape(8, 128).T, f)
    o["w_l3_sb3c"] = np.ascontiguousarray(z["L3_ffn_sb3"].reshape(8, 128).T, f)
    o["w_l3_sb2c"] = np.ascontiguousarray(z["L3_ffn_sb2"].reshape(8, 128).T, f)
    o["w_l3_sw1"] = np.ascontiguousarray(
        cat([z["L3_ffn_sw1"], z["L3_ffn_sb1"][None, :]], axis=0), f)
    o["w_l3_sw3"] = np.ascontiguousarray(
        cat([z["L3_ffn_sw3"], z["L3_ffn_sb3"][None, :]], axis=0), f)
    o["w_l3_sw2"] = np.ascontiguousarray(
        cat([z["L3_ffn_sw2"], z["L3_ffn_sb2"][None, :]], axis=0), f)
    return o


class St:
    pass


TC = [(tc * 512, (tc + 1) * 512) for tc in range(NTC)]


def emit_load_x(c, st, x_in):
    return c.dma('Y', 'cx', lambda y: y.dma_start(
        out=st.xF[:], in_=x_in.rearrange("(c p) t -> p c t", p=128)))


def emit_store_x(c, st, y_out):
    c.phase()
    d = c.dma('Y', 'cout', lambda y: y.dma_start(
        out=y_out.rearrange("(c p) t -> p c t", p=128), in_=st.xF[:]))
    c.wait('Y', d[0], d[1])


def emit_consts(c, st, W):
    st.dOnes = c.op('V', lambda v: v.memset(st.ones[:], 1.0))
    st.dOnesF = c.op('V', lambda v: v.memset(st.onesF[:], 1.0))
    c.dma('G', 'cconst', lambda g: g.dma_start(out=st.identB[:], in_=W["c_ident"][:, :]))
    st.dIdF = c.dma('Y', 'cconsty', lambda y: y.dma_start(out=st.identF[:], in_=W["c_ident"][:, :]))
    c.dma('G', 'cconst', lambda g: g.dma_start(out=st.cselB[:], in_=W["c_csel"][:, :]))
    c.dma('G', 'cconst', lambda g: g.dma_start(out=st.trilB[:], in_=W["c_tril"][:, :]))
    st.dTril = c.dma('G', 'cconst', lambda g: g.dma_start(out=st.triuB[:], in_=W["c_triu"][:, :]))
    st.dIdB = st.dCsel = st.dTril
    d = c.op('V', lambda v: v.memset(st.cst[0:1, 0:1], 1e-5), inc=0)
    d = c.op('V', lambda v: v.memset(st.cst[0:1, 1:2], 1e-6), inc=0)
    d = c.op('V', lambda v: v.memset(st.cst[0:1, 2:3], 1e-20), inc=0)
    st.dCst = c.op('V', lambda v: v.memset(st.cst[0:1, 3:4], 1e-35))


def emit_ln(c, st):
    bk = st.banks
    c.phase()
    st.ar.push()
    xbf = [st.ar.bf(T), st.ar.bf(T)]
    sq = [st.ar.bf(T), st.ar.bf(T)]
    for tc in range(NTC):
        bk[tc].guard()
    tRd = [None, None]
    for kc in range(KC):
        b = kc % 2
        dC = c.op('V', lambda v, kc=kc, b=b: v.tensor_copy(out=xbf[b][:], in_=st.xF[:, kc, :]),
                  deps=[tRd[b], st.dOnes])
        for tc in range(NTC):
            c.op('T', lambda t, kc=kc, tc=tc, b=b: t.matmul(
                bk[tc].t[0:1, :], st.ones[:, 0:1], xbf[b][:, TC[tc][0]:TC[tc][1]],
                start=(kc == 0), stop=(kc == KC - 1)),
                deps=[dC], inc=(1 if tc == NTC - 1 else 0))
        tRd[b] = c.tip('T')
    tSum = c.tip('T')
    dMu = None
    for tc in range(NTC):
        dMu = c.op('S', lambda s, tc=tc: s.activation(
            out=st.mu32[0:1, TC[tc][0]:TC[tc][1]], in_=bk[tc].t[0:1, :],
            func=AF.Copy, scale=1.0 / 1024.0), deps=[tSum])
        bk[tc].consume(dMu)
    dMuB = c.op('V', lambda v: v.tensor_copy(out=st.mu_bf[:, :], in_=st.mu32[:, :]), deps=[dMu])
    dXc = []
    for tc in range(NTC):
        bk[4].guard()
        tB = c.op('T', lambda t, tc=tc: t.matmul(
            bk[4].t[:, :], st.ones[0:1, 0:128], st.mu_bf[0:1, TC[tc][0]:TC[tc][1]],
            start=True, stop=True), deps=[dMuB])
        for kc in range(KC):
            d = c.op('V', lambda v, kc=kc, tc=tc: v.tensor_tensor(
                out=st.xnb[:, kc, TC[tc][0]:TC[tc][1]],
                in0=st.xF[:, kc, TC[tc][0]:TC[tc][1]], in1=bk[4].t[:, :], op=ALU.subtract),
                deps=[tB])
            bk[4].consume(d)
            dXc.append(d)
    for tc in range(NTC):
        bk[tc].guard()
    tRd2 = [None, None]
    for kc in range(KC):
        b = kc % 2
        dSq = c.op('S', lambda s, kc=kc, b=b: s.activation(
            out=sq[b][:], in_=st.xnb[:, kc, :], func=AF.Square),
            deps=[dXc[-1], tRd2[b]])
        for tc in range(NTC):
            c.op('T', lambda t, kc=kc, tc=tc, b=b: t.matmul(
                bk[tc].t[0:1, :], st.ones[:, 0:1], sq[b][:, TC[tc][0]:TC[tc][1]],
                start=(kc == 0), stop=(kc == KC - 1)),
                deps=[dSq], inc=(1 if tc == NTC - 1 else 0))
        tRd2[b] = c.tip('T')
    tVar = c.tip('T')
    dR = None
    for tc in range(NTC):
        dR = c.op('S', lambda s, tc=tc: s.activation(
            out=st.rstd32[0:1, TC[tc][0]:TC[tc][1]], in_=bk[tc].t[0:1, :],
            func=AF.Sqrt, scale=1.0 / 1024.0, bias=st.cst[0:1, 0:1]), deps=[tVar, st.dCst])
        bk[tc].consume(dR)
    dRec = c.op('V', lambda v: v.reciprocal(out=st.rstd32[:, :], in_=st.rstd32[:, :]), deps=[dR])
    dRb = c.op('V', lambda v: v.tensor_copy(out=st.rstd_bf[:, :], in_=st.rstd32[:, :]), deps=[dRec])
    sqDone = c.tip('S')
    for tc in range(NTC):
        bk[4].guard()
        tB = c.op('T', lambda t, tc=tc: t.matmul(
            bk[4].t[:, :], st.ones[0:1, 0:128], st.rstd_bf[0:1, TC[tc][0]:TC[tc][1]],
            start=True, stop=True), deps=[dRb])
        for kc in range(KC):
            d = c.op('V', lambda v, kc=kc, tc=tc: v.tensor_tensor(
                out=st.xnb[:, kc, TC[tc][0]:TC[tc][1]],
                in0=st.xnb[:, kc, TC[tc][0]:TC[tc][1]], in1=bk[4].t[:, :], op=ALU.mult),
                deps=[tB, sqDone])
            bk[4].consume(d)
    st.ar.pop()
    st.xnbReady = c.tip('V')


def emit_rstd_cols(c, st, rcol, dRstd):
    P = st.banks[7]
    last = None
    for tok in range(NTOK):
        P.guard()
        tT = c.op('T', lambda t, tok=tok: t.transpose(
            P.f[0:128, 0:1], st.rstd32[0:1, tok * 128:(tok + 1) * 128], st.identF[0:1, 0:1]),
            deps=[dRstd, st.dIdF])
        last = c.op('V', lambda v, tok=tok: v.tensor_copy(
            out=rcol[:, tok:tok + 1], in_=P.f[:, 0:1]), deps=[tT, last])
        P.consume(last)
    return last


def emit_routing(c, st, Wd, rwt, rb, E, kind, Wt):
    bk = st.banks
    st.ar.push()
    rwsb = st.ar.f32_3(KC, E)
    wsneg = st.ar.f32(E)
    rbb = st.ar.f32(E)
    rcol = st.ar.f32(NTOK)
    lg = st.ar.f32(32)
    sc = st.ar.f32(32)
    t1 = st.ar.f32(32)
    t2 = st.ar.f32(32)
    cols = st.ar.f32(8)
    dRw = c.dma('Y', 'crt', lambda y: y.dma_start(out=rwsb[:], in_=rwt.rearrange("(c p) e -> p c e", p=128)))
    dRbRow = None
    if rb is not None:
        rbrow = st.ar.f32(E)
        dRbRow = c.dma('Y', 'crt', lambda y: y.dma_start(out=rbrow[0:1, 0:E], in_=rb[:, :]))
        if kind == 'l1':
            tbrow = st.ar.f32(E)
            dTb = c.dma('Y', 'crt', lambda y: y.dma_start(out=tbrow[0:1, 0:E], in_=Wd["c_tb32"][:, 0:E]))
            dTb = ('crt', c.semc['crt'])
        dRbRow = ('crt', c.semc['crt'])
    dRw = ('crt', c.semc['crt'])
    dRcol = emit_rstd_cols(c, st, rcol, st.xnbReady)
    bk[6].guard()
    for kc in range(KC):
        c.op('T', lambda t, kc=kc: t.matmul(
            bk[6].t[0:1, 0:E], st.onesF[:, 0:1], rwsb[:, kc, :],
            start=(kc == 0), stop=(kc == KC - 1)),
            deps=[dRw, st.dOnesF], inc=(1 if kc == KC - 1 else 0))
    tWs = c.tip('T')
    dWs = c.op('S', lambda s: s.activation(out=wsneg[0:1, 0:E], in_=bk[6].t[0:1, 0:E],
                                           func=AF.Copy, scale=-1.0), deps=[tWs])
    bk[6].consume(dWs)
    dRbb = None
    if rb is not None:
        if kind == 'l1':
            dRbRow = c.op('V', lambda v: v.tensor_tensor(
                out=rbrow[0:1, 0:E], in0=rbrow[0:1, 0:E], in1=tbrow[0:1, 0:E], op=ALU.add),
                deps=[dRbRow, dTb])
        bk[6].guard()
        tRb = c.op('T', lambda t: t.matmul(bk[6].t[:, 0:E], st.onesF[0:1, 0:128], rbrow[0:1, 0:E],
                                           start=True, stop=True), deps=[dRbRow, dWs])
        dRbb = c.op('V', lambda v: v.tensor_copy(out=rbb[:, 0:E], in_=bk[6].t[:, 0:E]), deps=[tRb])
        bk[6].consume(dRbb)

    B = bk[5]
    for tok in range(NTOK):
        B.guard()
        for kc in range(KC):
            c.op('T', lambda t, kc=kc, tok=tok: t.matmul(
                B.t[:, 0:E], st.xF[:, kc, tok * 128:(tok + 1) * 128], rwsb[:, kc, :],
                start=(kc == 0), stop=False), deps=[dRw], inc=0)
        tL = c.op('T', lambda t, tok=tok: t.matmul(
            B.t[:, 0:E], st.mu32[0:1, tok * 128:(tok + 1) * 128], wsneg[0:1, 0:E],
            start=False, stop=True), deps=[dWs])
        d = c.op('V', lambda v, tok=tok: v.tensor_scalar(
            out=lg[:, 0:E], in0=B.t[:, 0:E], scalar1=rcol[:, tok:tok + 1], scalar2=None,
            op0=ALU.mult), deps=[tL, dRcol])
        B.consume(d)
        Ws = Wt[:, tok, :]
        if kind in ('l0', 'l2', 'l3'):
            if kind == 'l2':
                d = c.op('V', lambda v: v.tensor_tensor(out=lg[:, 0:E], in0=lg[:, 0:E],
                                                        in1=rbb[:, 0:E], op=ALU.add), deps=[d, dRbb])
            d = c.op('V', lambda v: v.tensor_reduce(out=cols[:, 0:1], in_=lg[:, 0:E], axis=AX.X, op=ALU.max), deps=[d])
            d = c.op('V', lambda v: v.tensor_scalar(out=cols[:, 1:2], in0=cols[:, 0:1], scalar1=-1.0, scalar2=None, op0=ALU.mult), deps=[d])
            d = c.op('S', lambda s: s.activation(out=sc[:, 0:E], in_=lg[:, 0:E], func=AF.Exp, bias=cols[:, 1:2]), deps=[d])
            d = c.op('V', lambda v: v.tensor_reduce(out=cols[:, 2:3], in_=sc[:, 0:E], axis=AX.X, op=ALU.max), deps=[d])
            d = c.op('V', lambda v: v.tensor_scalar(out=t1[:, 0:E], in0=sc[:, 0:E], scalar1=cols[:, 2:3], scalar2=None, op0=ALU.is_equal), deps=[d])
            if kind == 'l2':
                d = c.op('V', lambda v: v.tensor_reduce(out=cols[:, 3:4], in_=sc[:, 0:E], axis=AX.X, op=ALU.add), deps=[d])
                d = c.op('V', lambda v: v.reciprocal(out=cols[:, 4:5], in_=cols[:, 3:4]), deps=[d])
                d = c.op('V', lambda v, Ws=Ws: v.tensor_scalar(
                    out=Ws, in0=t1[:, 0:E], scalar1=cols[:, 2:3], scalar2=cols[:, 4:5],
                    op0=ALU.mult, op1=ALU.mult), deps=[d])
            else:
                d = c.op('V', lambda v: v.scalar_tensor_tensor(out=t2[:, 0:E], in0=t1[:, 0:E], scalar=-1e30, in1=sc[:, 0:E], op0=ALU.mult, op1=ALU.add), deps=[d])
                d = c.op('V', lambda v: v.tensor_reduce(out=cols[:, 3:4], in_=t2[:, 0:E], axis=AX.X, op=ALU.max), deps=[d])
                d = c.op('V', lambda v: v.tensor_scalar(out=t2[:, 0:E], in0=t2[:, 0:E], scalar1=cols[:, 3:4], scalar2=None, op0=ALU.is_equal), deps=[d])
                if kind == 'l0':
                    d = c.op('V', lambda v: v.tensor_tensor(out=cols[:, 4:5], in0=cols[:, 2:3], in1=cols[:, 3:4], op=ALU.add), deps=[d])
                else:
                    d = c.op('V', lambda v: v.tensor_reduce(out=cols[:, 4:5], in_=sc[:, 0:E], axis=AX.X, op=ALU.add), deps=[d])
                d = c.op('V', lambda v: v.reciprocal(out=cols[:, 5:6], in_=cols[:, 4:5]), deps=[d])
                d = c.op('V', lambda v: v.tensor_scalar(out=t1[:, 0:E], in0=t1[:, 0:E], scalar1=cols[:, 2:3], scalar2=None, op0=ALU.mult), deps=[d])
                d = c.op('V', lambda v: v.tensor_scalar(out=t2[:, 0:E], in0=t2[:, 0:E], scalar1=cols[:, 3:4], scalar2=None, op0=ALU.mult), deps=[d])
                d = c.op('V', lambda v: v.tensor_tensor(out=t1[:, 0:E], in0=t1[:, 0:E], in1=t2[:, 0:E], op=ALU.add), deps=[d])
                d = c.op('V', lambda v, Ws=Ws: v.tensor_scalar(
                    out=Ws, in0=t1[:, 0:E], scalar1=cols[:, 5:6], scalar2=None, op0=ALU.mult), deps=[d])
        elif kind == 'l1':
            g3 = lg[:, 0:E].rearrange("p (a b) -> p a b", a=4)
            t13 = t1[:, 0:E].rearrange("p (a b) -> p a b", a=4)
            d = c.op('S', lambda s: s.activation(out=sc[:, 0:E], in_=lg[:, 0:E], func=AF.Sigmoid), deps=[d])
            d = c.op('V', lambda v: v.tensor_tensor(out=lg[:, 0:E], in0=sc[:, 0:E], in1=rbb[:, 0:E], op=ALU.add), deps=[d, dRbb])
            d = c.op('V', lambda v: v.tensor_reduce(out=cols[:, 0:4], in_=g3, axis=AX.X, op=ALU.max), deps=[d])
            m1b = cols[:, 0:4].unsqueeze(2).broadcast_to([128, 4, 8])
            d = c.op('V', lambda v, m1b=m1b: v.tensor_tensor(out=t13, in0=g3, in1=m1b, op=ALU.is_equal), deps=[d])
            d = c.op('V', lambda v: v.scalar_tensor_tensor(out=t2[:, 0:E], in0=t1[:, 0:E], scalar=-1e30, in1=lg[:, 0:E], op0=ALU.mult, op1=ALU.add), deps=[d])
            d = c.op('V', lambda v: v.tensor_reduce(out=cols[:, 4:8], in_=t2[:, 0:E].rearrange("p (a b) -> p a b", a=4), axis=AX.X, op=ALU.max), deps=[d])
            d = c.op('V', lambda v: v.tensor_tensor(out=cols[:, 0:4], in0=cols[:, 0:4], in1=cols[:, 4:8], op=ALU.add), deps=[d])
            d = c.op('V', lambda v: v.tensor_reduce(out=cols[:, 4:5], in_=cols[:, 0:4], axis=AX.X, op=ALU.max), deps=[d])
            d = c.op('V', lambda v: v.tensor_scalar(out=cols[:, 5:6], in0=cols[:, 4:5], scalar1=-1e30, scalar2=None, op0=ALU.mult), deps=[d])
            d = c.op('V', lambda v: v.tensor_scalar(out=t2[:, 0:4], in0=cols[:, 0:4], scalar1=cols[:, 4:5], scalar2=None, op0=ALU.is_equal), deps=[d])
            d = c.op('V', lambda v: v.scalar_tensor_tensor(out=t2[:, 0:4], in0=t2[:, 0:4], scalar=-1e30, in1=cols[:, 0:4], op0=ALU.mult, op1=ALU.add), deps=[d])
            d = c.op('V', lambda v: v.tensor_reduce(out=cols[:, 5:6], in_=t2[:, 0:4], axis=AX.X, op=ALU.max), deps=[d])
            d = c.op('V', lambda v: v.tensor_scalar(out=t2[:, 0:4], in0=cols[:, 0:4], scalar1=cols[:, 5:6], scalar2=None, op0=ALU.is_ge), deps=[d])
            gmb = t2[:, 0:4].unsqueeze(2).broadcast_to([128, 4, 8])
            d = c.op('V', lambda v, gmb=gmb: v.tensor_tensor(out=sc[:, 0:E].rearrange("p (a b) -> p a b", a=4), in0=g3, in1=gmb, op=ALU.mult), deps=[d])
            for kk in range(4):
                d = c.op('V', lambda v: v.tensor_reduce(out=cols[:, 6:7], in_=sc[:, 0:E], axis=AX.X, op=ALU.max), deps=[d])
                d = c.op('V', lambda v: v.tensor_scalar(out=t2[:, 0:E], in0=sc[:, 0:E], scalar1=cols[:, 6:7], scalar2=None, op0=ALU.is_equal), deps=[d])
                if kk == 0:
                    d = c.op('V', lambda v: v.tensor_copy(out=t1[:, 0:E], in_=t2[:, 0:E]), deps=[d])
                else:
                    d = c.op('V', lambda v: v.tensor_tensor(out=t1[:, 0:E], in0=t1[:, 0:E], in1=t2[:, 0:E], op=ALU.add), deps=[d])
                if kk < 3:
                    d = c.op('V', lambda v: v.scalar_tensor_tensor(out=sc[:, 0:E], in0=t2[:, 0:E], scalar=-1e30, in1=sc[:, 0:E], op0=ALU.mult, op1=ALU.add), deps=[d])
            d = c.op('V', lambda v: v.tensor_tensor(out=t2[:, 0:E], in0=lg[:, 0:E], in1=rbb[:, 0:E], op=ALU.subtract), deps=[d])
            d = c.op('V', lambda v: v.tensor_tensor(out=t1[:, 0:E], in0=t1[:, 0:E], in1=t2[:, 0:E], op=ALU.mult), deps=[d])
            d = c.op('V', lambda v: v.tensor_reduce(out=cols[:, 6:7], in_=t1[:, 0:E], axis=AX.X, op=ALU.add), deps=[d])
            d = c.op('V', lambda v: v.tensor_scalar(out=cols[:, 6:7], in0=cols[:, 6:7], scalar1=1e-20, scalar2=None, op0=ALU.add), deps=[d])
            d = c.op('V', lambda v: v.reciprocal(out=cols[:, 7:8], in_=cols[:, 6:7]), deps=[d])
            d = c.op('V', lambda v, Ws=Ws: v.tensor_scalar(
                out=Ws, in0=t1[:, 0:E], scalar1=cols[:, 7:8], scalar2=2.5,
                op0=ALU.mult, op1=ALU.mult), deps=[d])
        else:
            raise ValueError(kind)
    st.ar.pop()
    return c.tip('V')


def emit_wt_transpose(c, st, Wt, WeT, E, dWt):
    P = st.banks[7]
    st.ar.push()
    wtb = st.ar.bf(32)
    d = dWt
    last = None
    for tok in range(NTOK):
        d = c.op('V', lambda v, tok=tok: v.tensor_copy(out=wtb[:, 0:E], in_=Wt[:, tok, :]), deps=[dWt, d])
        P.guard()
        tT = c.op('T', lambda t, tok=tok: t.transpose(P.t[0:E, 0:128], wtb[:, 0:E], st.identB[:]),
                  deps=[d, st.dIdB])
        last = c.op('V', lambda v, tok=tok: v.tensor_copy(
            out=WeT[0:E, tok * 128:(tok + 1) * 128], in_=P.t[0:E, 0:128]), deps=[tT, last])
        P.consume(last)
    st.ar.pop()
    return last


def emit_l0_ffn(c, st, Wd):
    E = 8
    bk = st.banks
    c.phase()
    st.ar.push()
    Wt = st.ar.f32_3(NTOK, E)
    dWt = emit_routing(c, st, Wd, Wd["w_l0_wg"], None, E, 'l0', Wt)
    WeT = st.ar.bf(T)
    dWeT = emit_wt_transpose(c, st, Wt, WeT, E, dWt)
    b2sb = st.ar.bf(1024)
    dB2 = c.dma('G', 'cmisc', lambda g: g.dma_start(out=b2sb[0:8, :], in_=Wd["w_l0_b2"][:, :]),
                deps=[dWeT])
    eye = st.ar.bf(1024)
    eyev = eye.rearrange("p (e m) -> p e m", e=8)
    dEye = c.dma('G', 'cmisc', lambda g: g.dma_start(out=eyev[0:8], in_=Wd["c_eyeb"][0:8, 0:1024].rearrange("e (f m) -> e f m", f=8)),
                 deps=[dWeT])
    wb = st.ar.bf(T)
    b1c0 = st.ar.f32_3(8, 16)
    dBc0 = c.dma('Y', 'cl0bc', lambda y: y.dma_start(
        out=b1c0[:], in_=Wd["w_l0_b1c"].rearrange("e p i -> p e i")))
    mst = st.ar.bf3(16, 1024)
    w1s = [st.ar.bf3(KC, 128) for _ in range(2)]
    w1r = [st.ar.bf(128) for _ in range(2)]
    w2s = [st.ar.bf3(16, 128) for _ in range(2)]
    relu_sb = st.ar.bf(512)
    w1t = [None, None]
    w2t = [None, None]
    mprev = None
    for e in range(E):
        dwb = None
        for tc in range(NTC):
            bk[6].guard()
            tB = c.op('T', lambda t, tc=tc, e=e: t.matmul(
                bk[6].t[:, :], eyev[0:8, e, :], WeT[0:8, TC[tc][0]:TC[tc][1]],
                start=True, stop=True), deps=[dWeT, dEye])
            dwb = c.op('V', lambda v, tc=tc: v.tensor_copy(
                out=wb[:, TC[tc][0]:TC[tc][1]], in_=bk[6].t[:, :]), deps=[tB, dwb])
            bk[6].consume(dwb)
        for half in range(2):
            tcs = [2 * half, 2 * half + 1]
            dm = mprev
            for ic in range(16):
                sl = ic % 2
                ld = c.dma('G', f'cw1_{sl}', lambda g, e=e, ic=ic, sl=sl: g.dma_start(
                    out=w1s[sl][:], in_=Wd["w_l0_w1"][e, 0:1024, ic * 128:(ic + 1) * 128]
                    .rearrange("(c p) m -> p c m", p=128)), deps=[w1t[sl]])
                for ti, tc in enumerate(tcs):
                    hb = bk[ti]
                    hb.guard()
                    for kc in range(KC):
                        c.op('T', lambda t, kc=kc, tc=tc, sl=sl, hb=hb: t.matmul(
                            hb.t[:, :], w1s[sl][:, kc, :], st.xnb[:, kc, TC[tc][0]:TC[tc][1]],
                            start=(kc == 0), stop=(kc == KC - 1)),
                            deps=[ld, st.xnbReady], inc=(1 if kc == KC - 1 else 0))
                    tH = c.tip('T')
                    dRelu = c.op('S', lambda s, hb=hb, e=e, ic=ic: s.activation(
                        out=relu_sb[:, :], in_=hb.t[:, :], func=AF.Relu,
                        bias=b1c0[:, e, ic:ic + 1]), deps=[tH, dm, dBc0])
                    hb.consume(dRelu)
                    dm = c.op('V', lambda v, ic=ic, tc=tc, ti=ti: v.tensor_tensor(
                        out=mst[:, ic, ti * 512:(ti + 1) * 512], in0=relu_sb[:, :],
                        in1=wb[:, TC[tc][0]:TC[tc][1]], op=ALU.mult), deps=[dRelu, dwb, mprev])
                w1t[sl] = c.tip('T')
            for dc in range(KC):
                sl = dc % 2
                ld2 = c.dma('G', f'cw2_{sl}', lambda g, e=e, dc=dc, sl=sl: g.dma_start(
                    out=w2s[sl][:], in_=Wd["w_l0_w2"][e, :, dc * 128:(dc + 1) * 128]
                    .rearrange("(c p) m -> p c m", p=128)), deps=[w2t[sl]])
                for ti, tc in enumerate(tcs):
                    ob = bk[2 + ti]
                    ob.guard()
                    for ic in range(16):
                        last = (ic == 15) and (e != 0)
                        c.op('T', lambda t, ic=ic, ti=ti, sl=sl, last=last, ob=ob: t.matmul(
                            ob.t[:, :], w2s[sl][:, ic, :], mst[:, ic, ti * 512:(ti + 1) * 512],
                            start=(ic == 0), stop=last), deps=[ld2, dm], inc=(1 if last else 0))
                    if e == 0:
                        tD = c.op('T', lambda t, dc=dc, tc=tc, ob=ob: t.matmul(
                            ob.t[:, :], b2sb[0:8, dc * 128:(dc + 1) * 128], WeT[0:8, TC[tc][0]:TC[tc][1]],
                            start=False, stop=True), deps=[dB2, dWeT])
                    else:
                        tD = c.tip('T')
                    dAdd = c.op('V', lambda v, dc=dc, tc=tc, ob=ob: v.tensor_tensor(
                        out=st.xF[:, dc, TC[tc][0]:TC[tc][1]], in0=st.xF[:, dc, TC[tc][0]:TC[tc][1]],
                        in1=ob.t[:, :], op=ALU.add), deps=[tD])
                    ob.consume(dAdd)
                w2t[sl] = c.tip('T')
            mprev = c.tip('T')
    st.ar.pop()


def emit_l0_attn(c, st, Wd):
    bk = st.banks
    c.phase()
    st.ar.push()
    Vt = st.ar.bf3(NTOK, 1040)
    st.ar.push()
    wv = st.ar.bf(KC * 1024).rearrange("p (c h m) -> p c h m", c=KC, h=16)
    dWv = None
    for h in range(16):
        dWv = c.dma('G', 'cwv', lambda g, h=h: g.dma_start(
            out=wv[:, :, h, :], in_=Wd["w_l0_wv"][h].rearrange("(c p) m -> p c m", p=128)))
    dVm = c.op('V', lambda v: v.memset(Vt[:, :, :], 1.0))
    tVdone = None
    for jt in range(NTOK):
        for vc in range(2):
            b = bk[vc]
            b.guard()
            for kc in range(KC):
                c.op('T', lambda t, jt=jt, vc=vc, kc=kc, b=b: t.matmul(
                    b.t[:, :], st.xnb[:, kc, jt * 128:(jt + 1) * 128], wv[:, kc, vc * 8:(vc + 1) * 8, :],
                    start=(kc == 0), stop=(kc == KC - 1)),
                    deps=[dWv, st.xnbReady], inc=(1 if kc == KC - 1 else 0))
            tV = c.tip('T')
            dcp = None
            for h8 in range(8):
                hh = vc * 8 + h8
                dcp = c.op('V', lambda v, jt=jt, hh=hh, h8=h8, b=b: v.tensor_copy(
                    out=Vt[:, jt, hh * 65:hh * 65 + 64], in_=b.t[:, h8 * 64:(h8 + 1) * 64]),
                    deps=[tV, dVm, dcp])
                b.consume(dcp)
    tVdone = c.tip('T')
    st.ar.pop()
    st.ar.push()
    qk = st.ar.bf(KC * 256).rearrange("p (c w m) -> p c w m", c=KC, w=4)
    Qs = [st.ar.bf(T), st.ar.bf(T)]
    Ks = [st.ar.bf(T), st.ar.bf(T)]
    Ptd = [st.ar.bf3(8, 1024), st.ar.bf3(8, 1024)]
    qsq = st.ar.bf(T)
    rowS = st.ar.f32(T)
    k2m = st.ar.f32(8)
    Osb = st.ar.f32(512)
    OsbB = st.ar.bf(512)
    recb = OsbB
    dWqk = None
    tPrevScores = None
    prevB5 = None
    prevA = None
    prevSh = None
    dBs = [None, None]
    for hg in range(8):
        dWqkL = []
        for w, (wkey, hh2) in enumerate([("w_l0_wq", 0), ("w_l0_wq", 1), ("w_l0_wk", 0), ("w_l0_wk", 1)]):
            dWqkL.append(c.dma('G', f'cqk_{w}', lambda g, hg=hg, w=w, wkey=wkey, hh2=hh2: g.dma_start(
                out=qk[:, :, w, :], in_=Wd[wkey][hg * 2 + hh2]
                .rearrange("(c p) m -> p c m", p=128)), deps=[tPrevScores, tVdone]))
        for hh in range(2):
            h = hg * 2 + hh
            dQe = dKe = None
            for tc in range(NTC):
                b = bk[tc % 2]
                b.guard()
                for kc in range(KC):
                    c.op('T', lambda t, kc=kc, tc=tc, hh=hh, b=b: t.matmul(
                        b.t[0:64, :], qk[:, kc, hh, :], st.xnb[:, kc, TC[tc][0]:TC[tc][1]],
                        start=(kc == 0), stop=(kc == KC - 1)),
                        deps=[dWqkL[hh], st.xnbReady], inc=(1 if kc == KC - 1 else 0))
                tQ = c.tip('T')
                dQe = c.op('S', lambda s, tc=tc, hh=hh, b=b: s.activation(
                    out=Qs[hh][0:64, TC[tc][0]:TC[tc][1]], in_=b.t[0:64, :],
                    func=AF.Copy, scale=0.125), deps=[tQ, dQe])
                b.consume(dQe)
                b2_ = bk[2 + tc % 2]
                b2_.guard()
                for kc in range(KC):
                    c.op('T', lambda t, kc=kc, tc=tc, hh=hh, b2_=b2_: t.matmul(
                        b2_.t[0:64, :], qk[:, kc, 2 + hh, :],
                        st.xnb[:, kc, TC[tc][0]:TC[tc][1]],
                        start=(kc == 0), stop=(kc == KC - 1)),
                        deps=[dWqkL[2 + hh]], inc=(1 if kc == KC - 1 else 0))
                tK = c.tip('T')
                dKe = c.op('S', lambda s, tc=tc, hh=hh, b2_=b2_: s.copy(
                    out=Ks[hh][0:64, TC[tc][0]:TC[tc][1]], in_=b2_.t[0:64, :]), deps=[tK, dKe])
                b2_.consume(dKe)
            dKone = c.op('V', lambda v, hh=hh: v.memset(Ks[hh][64:65, :], 1.0), deps=[tVdone])
            dSqQ = c.op('S', lambda s, hh=hh: s.activation(
                out=qsq[0:64, :], in_=Qs[hh][0:64, :], func=AF.Square), deps=[dQe])
            dRowQ = None
            for tc in range(NTC):
                b = bk[4]
                b.guard()
                tR = c.op('T', lambda t, tc=tc, b=b: t.matmul(
                    b.t[0:65, :], st.cselB[:, :], qsq[0:64, TC[tc][0]:TC[tc][1]],
                    start=True, stop=True), deps=[dSqQ, st.dCsel])
                dRowQ = c.op('S', lambda s, tc=tc, b=b: s.copy(
                    out=rowS[64:65, TC[tc][0]:TC[tc][1]], in_=b.t[64:65, :]),
                    deps=[tR, dRowQ, prevA, dBs[1 - hh]])
                b.consume(dRowQ)
            dSqK = c.op('S', lambda s, hh=hh: s.activation(
                out=qsq[0:64, :], in_=Ks[hh][0:64, :], func=AF.Square), deps=[dKe, dRowQ])
            dKm = None
            for tc in range(NTC):
                b = bk[4]
                b.guard()
                tR = c.op('T', lambda t, tc=tc, b=b: t.matmul(
                    b.t[0:65, :], st.cselB[:, :], qsq[0:64, TC[tc][0]:TC[tc][1]],
                    start=True, stop=True), deps=[dSqK])
                dKm = c.op('V', lambda v, tc=tc, b=b: v.tensor_reduce(
                    out=k2m[64:65, tc:tc + 1], in_=b.t[64:65, :], axis=AX.X, op=ALU.max),
                    deps=[tR, dKm])
                b.consume(dKm)
            dKm = c.op('V', lambda v: v.tensor_reduce(
                out=k2m[64:65, 4:5], in_=k2m[64:65, 0:4], axis=AX.X, op=ALU.max), deps=[dKm])
            dB = c.op('V', lambda v: v.tensor_scalar(
                out=rowS[64:65, :], in0=rowS[64:65, :], scalar1=k2m[64:65, 4:5], scalar2=None,
                op0=ALU.mult), deps=[dKm, dRowQ])
            dB = c.op('S', lambda s: s.activation(
                out=rowS[64:65, :], in_=rowS[64:65, :], func=AF.Sqrt), deps=[dB])
            dB = c.op('V', lambda v, hh=hh: v.tensor_scalar(
                out=Qs[hh][64:65, :], in0=rowS[64:65, :], scalar1=-1.0, scalar2=None,
                op0=ALU.mult), deps=[dB])
            dBs[hh] = dB
        units = [(hh, ib) for ib in range(NTC) for hh in range(2)]
        NU = len(units)
        dExpU = [None] * NU
        tPVU = [None] * NU
        dE = None
        for step in range(NU + 1):
            if step < NU:
                hh, ib = units[step]
                par = step % 2
                for jp in range(8):
                    P = st.pairs[jp % 2]
                    P.guard()
                    tS = None
                    for hf in range(2):
                        jt = 2 * jp + hf
                        tS = c.op('T', lambda t, jt=jt, ib=ib, hh=hh, P=P, hf=hf: t.matmul(
                            P.half[hf].t[:, :], Ks[hh][0:65, jt * 128:(jt + 1) * 128],
                            Qs[hh][0:65, TC[ib][0]:TC[ib][1]],
                            start=True, stop=True),
                            deps=[dBs[hh], dKone, tPrevScores,
                                  tPVU[step - 2] if step >= 2 else None])
                    dE = c.op('S', lambda s, jp=jp, par=par, P=P: s.activation(
                        out=Ptd[par][:, jp, :], in_=P.t[:, :], func=AF.Exp),
                        deps=[tS, dE, tPVU[step - 2] if step >= 2 else None])
                    P.consume(dE)
                dExpU[step] = dE
            if step > 0:
                hh2, ib2 = units[step - 1]
                h = hg * 2 + hh2
                par2 = (step - 1) % 2
                b = bk[4 + (step - 1) % 2]
                b.guard()
                for jt in range(NTOK):
                    c.op('T', lambda t, jt=jt, h=h, b=b, par2=par2: t.matmul(
                        b.t[0:65, :], Vt[:, jt, h * 65:(h + 1) * 65],
                        Ptd[par2][:, jt // 2, (jt % 2) * 512:(jt % 2) * 512 + 512],
                        start=(jt == 0), stop=(jt == NTOK - 1)),
                        deps=[dExpU[step - 1]], inc=(1 if jt == NTOK - 1 else 0))
                tO = c.tip('T')
                tPVU[step - 1] = tO
                tPrevScores = tO
                dD = c.op('V', lambda v, b=b: v.tensor_scalar(
                    out=rowS[64:65, 0:512], in0=b.t[64:65, :], scalar1=1e-35, scalar2=None,
                    op0=ALU.add), deps=[tO, dBs[hh2]])
                b.consume(dD)
                dD = c.op('V', lambda v: v.reciprocal(
                    out=rowS[64:65, 512:1024], in_=rowS[64:65, 0:512]), deps=[dD])
                dD = c.op('V', lambda v: v.tensor_copy(
                    out=recb[64:65, :], in_=rowS[64:65, 512:1024]), deps=[dD, prevB5])
                p0 = (h % 2) * 64
                b6 = bk[6]
                b6.guard()
                tB5 = c.op('T', lambda t, p0=p0, b6=b6: t.matmul(
                    b6.t[p0:p0 + 64, :], st.ones[64:65, 0:64], recb[64:65, 0:512],
                    start=True, stop=True), deps=[dD, st.dOnes])
                if p0 == 0:
                    dO = c.op('S', lambda s, b=b: s.copy(out=Osb[0:64, :], in_=b.t[0:64, :]),
                              deps=[tO, prevA, dD])
                    b.consume(dO)
                else:
                    dOb = c.op('S', lambda s, b=b: s.copy(out=OsbB[0:64, :], in_=b.t[0:64, :]),
                               deps=[tO, prevA, prevSh, dD])
                    b.consume(dOb)
                    b7 = bk[7]
                    b7.guard()
                    tSh = c.op('T', lambda t, b7=b7: t.matmul(
                        b7.f[64:128, 0:512], st.identB[0:64, 0:64], OsbB[0:64, :],
                        start=True, stop=True), deps=[dOb, st.dIdB])
                    prevSh = tSh
                    dO = c.op('S', lambda s, b7=b7: s.copy(out=Osb[64:128, :], in_=b7.f[64:128, 0:512]),
                              deps=[tSh, prevA])
                    b7.consume(dO)
                dN = c.op('V', lambda v, p0=p0, b6=b6: v.tensor_tensor(
                    out=Osb[p0:p0 + 64, :], in0=Osb[p0:p0 + 64, :], in1=b6.t[p0:p0 + 64, :],
                    op=ALU.mult), deps=[tB5, dO])
                b6.consume(dN)
                dA = c.op('V', lambda v, h=h, ib2=ib2, p0=p0: v.tensor_tensor(
                    out=st.xF[p0:p0 + 64, h // 2, TC[ib2][0]:TC[ib2][1]],
                    in0=st.xF[p0:p0 + 64, h // 2, TC[ib2][0]:TC[ib2][1]],
                    in1=Osb[p0:p0 + 64, :], op=ALU.add), deps=[dN])
                prevB5 = tB5
                prevA = dA
    st.ar.pop()
    st.ar.pop()



def mk_glu_rings(c, st, inter, tag, with_up=True):
    r = {}
    r['tag'] = tag
    r['w1s'] = [st.ar.bf3(KC, 128) for _ in range(2)]
    r['w1r'] = [st.ar.bf(128) for _ in range(2)]
    r['w1t'] = [None, None]
    if with_up:
        r['w3s'] = [st.ar.bf3(KC, 128) for _ in range(2)]
        r['w3r'] = [st.ar.bf(128) for _ in range(2)]
        r['w3t'] = [None, None]
    r['w2s'] = [st.ar.bf3(inter // 128, 128) for _ in range(2)]
    r['w2t'] = [None, None]
    r['gsb'] = st.ar.bf(512)
    r['usb'] = st.ar.bf(512)
    r['mst'] = None
    r['mprev'] = None
    return r


def emit_glu(c, st, r, xsrc, xdep, w1ap, w3ap, w2ap, inter, tparts, wbv=None,
             dwb=None, extra_down=None, first_down_group=True,
             b1c=None, b3c=None, b2c=None, dBc=None):
    bk = st.banks
    nic = inter // 128
    tw = T // tparts
    ncc = tw // 512
    if r['mst'] is None:
        r['mst'] = st.ar.bf3(nic, tw)
    mst = r['mst']
    tag = r['tag']
    w1b = w1ap.shape[0] > 1024 and b1c is None
    w3b = w3ap is not None and w3ap.shape[0] > 1024 and b3c is None
    w2b = w2ap.shape[0] > inter and b2c is None
    mprev = r['mprev']
    for part in range(tparts):
        col0 = part * tw
        dm = mprev
        for ic in range(nic):
            sl = ic % 2
            ld1 = c.dma('G', f'c{tag}1_{sl}', lambda g, ic=ic, sl=sl: g.dma_start(
                out=r['w1s'][sl][:], in_=w1ap[0:1024, ic * 128:(ic + 1) * 128]
                .rearrange("(c p) m -> p c m", p=128)), deps=[r['w1t'][sl]])
            if w1b:
                ld1r = c.dma('G', f'c{tag}1r_{sl}', lambda g, ic=ic, sl=sl: g.dma_start(
                    out=r['w1r'][sl][0:1, :], in_=w1ap[1024:1025, ic * 128:(ic + 1) * 128]),
                    deps=[r['w1t'][sl]])
            if w3ap is not None:
                ld3 = c.dma('G', f'c{tag}3_{sl}', lambda g, ic=ic, sl=sl: g.dma_start(
                    out=r['w3s'][sl][:], in_=w3ap[0:1024, ic * 128:(ic + 1) * 128]
                    .rearrange("(c p) m -> p c m", p=128)), deps=[r['w3t'][sl]])
                if w3b:
                    ld3r = c.dma('G', f'c{tag}3r_{sl}', lambda g, ic=ic, sl=sl: g.dma_start(
                        out=r['w3r'][sl][0:1, :], in_=w3ap[1024:1025, ic * 128:(ic + 1) * 128]),
                        deps=[r['w3t'][sl]])
            for cc in range(ncc):
                lo = col0 + cc * 512
                ga = bk[cc % 2]
                ga.guard()
                for kc in range(KC):
                    c.op('T', lambda t, kc=kc, lo=lo, sl=sl, ga=ga: t.matmul(
                        ga.t[:, :], r['w1s'][sl][:, kc, :], xsrc[:, kc, lo:lo + 512],
                        start=(kc == 0), stop=(kc == KC - 1 and not w1b)),
                        deps=[ld1, xdep], inc=(0 if w1b else (1 if kc == KC - 1 else 0)))
                if w1b:
                    c.op('T', lambda t, sl=sl, ga=ga: t.matmul(
                        ga.t[:, :], r['w1r'][sl][0:1, :], st.ones[0:1, 0:512],
                        start=False, stop=True), deps=[ld1r, st.dOnes])
                tG = c.tip('T')
                if w3ap is not None:
                    ua = bk[2 + cc % 2]
                    ua.guard()
                    for kc in range(KC):
                        c.op('T', lambda t, kc=kc, lo=lo, sl=sl, ua=ua: t.matmul(
                            ua.t[:, :], r['w3s'][sl][:, kc, :], xsrc[:, kc, lo:lo + 512],
                            start=(kc == 0), stop=(kc == KC - 1 and not w3b)),
                            deps=[ld3, xdep], inc=(0 if w3b else (1 if kc == KC - 1 else 0)))
                    if w3b:
                        c.op('T', lambda t, sl=sl, ua=ua: t.matmul(
                            ua.t[:, :], r['w3r'][sl][0:1, :], st.ones[0:1, 0:512],
                            start=False, stop=True), deps=[ld3r, st.dOnes])
                    tU = c.tip('T')
                    if b1c is not None:
                        dSi = c.op('S', lambda s, ga=ga, ic=ic: s.activation(
                            out=r['gsb'][:, :], in_=ga.t[:, :], func=AF.Silu,
                            bias=b1c[:, ic:ic + 1]), deps=[tG, dm, dBc])
                    else:
                        dSi = c.op('S', lambda s, ga=ga: s.activation(
                            out=r['gsb'][:, :], in_=ga.t[:, :], func=AF.Silu), deps=[tG, dm])
                    ga.consume(dSi)
                    if wbv is not None:
                        if b3c is not None:
                            dT = c.op('V', lambda v, ua=ua, ic=ic: v.scalar_tensor_tensor(
                                out=r['usb'][:, :], in0=ua.t[:, :], scalar=b3c[:, ic:ic + 1],
                                in1=r['gsb'][:, :], op0=ALU.add, op1=ALU.mult),
                                deps=[dSi, tU, mprev, dBc])
                        else:
                            dT = c.op('V', lambda v, ua=ua: v.tensor_tensor(
                                out=r['usb'][:, :], in0=r['gsb'][:, :], in1=ua.t[:, :],
                                op=ALU.mult), deps=[dSi, tU, mprev])
                        ua.consume(dT)
                        dm = c.op('V', lambda v, ic=ic, cc=cc, lo=lo: v.tensor_tensor(
                            out=mst[:, ic, cc * 512:(cc + 1) * 512], in0=r['usb'][:, :],
                            in1=wbv[:, lo:lo + 512], op=ALU.mult), deps=[dT, dwb])
                    else:
                        if b3c is not None:
                            dm = c.op('V', lambda v, ic=ic, cc=cc, ua=ua: v.scalar_tensor_tensor(
                                out=mst[:, ic, cc * 512:(cc + 1) * 512], in0=ua.t[:, :],
                                scalar=b3c[:, ic:ic + 1], in1=r['gsb'][:, :],
                                op0=ALU.add, op1=ALU.mult), deps=[dSi, tU, mprev, dBc])
                        else:
                            dm = c.op('V', lambda v, ic=ic, cc=cc, ua=ua: v.tensor_tensor(
                                out=mst[:, ic, cc * 512:(cc + 1) * 512], in0=r['gsb'][:, :],
                                in1=ua.t[:, :], op=ALU.mult), deps=[dSi, tU, mprev])
                        ua.consume(dm)
                else:
                    dSi = c.op('S', lambda s, ga=ga: s.activation(
                        out=r['gsb'][:, :], in_=ga.t[:, :], func=AF.Relu), deps=[tG, dm])
                    ga.consume(dSi)
                    dm = c.op('V', lambda v, ic=ic, cc=cc: v.tensor_tensor(
                        out=mst[:, ic, cc * 512:(cc + 1) * 512], in0=r['gsb'][:, :],
                        in1=wbv[:, col0 + cc * 512:col0 + (cc + 1) * 512], op=ALU.mult),
                        deps=[dSi, dwb, mprev])
            r['w1t'][sl] = c.tip('T')
            if w3ap is not None:
                r['w3t'][sl] = c.tip('T')
        for dc in range(KC):
            sl = dc % 2
            ld2 = c.dma('G', f'c{tag}2_{sl}', lambda g, dc=dc, sl=sl: g.dma_start(
                out=r['w2s'][sl][:], in_=w2ap[0:inter, dc * 128:(dc + 1) * 128]
                .rearrange("(c p) m -> p c m", p=128)), deps=[r['w2t'][sl]])
            if w2b:
                ld2r = c.dma('G', f'c{tag}2r_{sl}', lambda g, dc=dc, sl=sl: g.dma_start(
                    out=r['w1r'][sl][0:1, :], in_=w2ap[inter:inter + 1, dc * 128:(dc + 1) * 128]),
                    deps=[r['w1t'][sl], r['w2t'][sl]])
            for cc in range(ncc):
                lo = col0 + cc * 512
                tc = lo // 512
                ob = bk[4 + cc % 2]
                ob.guard()
                needs_tail = w2b or (extra_down is not None)
                for ic in range(nic):
                    last = (ic == nic - 1) and not needs_tail
                    c.op('T', lambda t, ic=ic, cc=cc, sl=sl, ob=ob, last=last: t.matmul(
                        ob.t[:, :], r['w2s'][sl][:, ic, :], mst[:, ic, cc * 512:(cc + 1) * 512],
                        start=(ic == 0), stop=last), deps=[ld2, dm],
                        inc=(1 if last else 0))
                if w2b:
                    c.op('T', lambda t, sl=sl, ob=ob: t.matmul(
                        ob.t[:, :], r['w1r'][sl][0:1, :], st.ones[0:1, 0:512],
                        start=False, stop=(extra_down is None)),
                        deps=[ld2r, st.dOnes], inc=(1 if extra_down is None else 0))
                if extra_down is not None:
                    extra_down(ob, dc, tc)
                tD = c.tip('T')
                if b2c is not None:
                    dAdd = c.op('V', lambda v, dc=dc, tc=tc, ob=ob: v.scalar_tensor_tensor(
                        out=st.xF[:, dc, TC[tc][0]:TC[tc][1]], in0=ob.t[:, :],
                        scalar=b2c[:, dc:dc + 1], in1=st.xF[:, dc, TC[tc][0]:TC[tc][1]],
                        op0=ALU.add, op1=ALU.add), deps=[tD, dBc])
                else:
                    dAdd = c.op('V', lambda v, dc=dc, tc=tc, ob=ob: v.tensor_tensor(
                        out=st.xF[:, dc, TC[tc][0]:TC[tc][1]], in0=st.xF[:, dc, TC[tc][0]:TC[tc][1]],
                        in1=ob.t[:, :], op=ALU.add), deps=[tD])
                ob.consume(dAdd)
            r['w2t'][sl] = c.tip('T')
            if w2b:
                r['w1t'][sl] = c.tip('T')
        mprev = c.tip('T')
    r['mprev'] = mprev


def emit_l3_ffn(c, st, Wd):
    E = 8
    bk = st.banks
    c.phase()
    st.ar.push()
    Wt = st.ar.f32_3(NTOK, E)
    dWt = emit_routing(c, st, Wd, Wd["w_l3_gwt"], None, E, 'l3', Wt)
    WeT = st.ar.bf(T)
    dWeT = emit_wt_transpose(c, st, Wt, WeT, E, dWt)
    b2sb = st.ar.bf(1024)
    dB2 = c.dma('G', 'cmisc', lambda g: g.dma_start(out=b2sb[0:8, :], in_=Wd["w_l3_b2"][:, :]),
                deps=[dWeT])
    eye = st.ar.bf(1024)
    eyev = eye.rearrange("p (e m) -> p e m", e=8)
    dEye = c.dma('G', 'cmisc', lambda g: g.dma_start(
        out=eyev[0:8], in_=Wd["c_eyeb"][0:8, 0:1024].rearrange("e (f m) -> e f m", f=8)),
        deps=[dWeT])
    wb = st.ar.bf(T)
    b1c = st.ar.f32_3(8, 8)
    b3c = st.ar.f32_3(8, 8)
    sbc = st.ar.f32(24)
    dBc = c.dma('Y', 'cl3bc', lambda y: y.dma_start(
        out=b1c[:], in_=Wd["w_l3_b1c"].rearrange("e p i -> p e i")))
    dBc = c.dma('Y', 'cl3bc', lambda y: y.dma_start(
        out=b3c[:], in_=Wd["w_l3_b3c"].rearrange("e p i -> p e i")))
    dBc = c.dma('Y', 'cl3bc', lambda y: y.dma_start(out=sbc[:, 0:8], in_=Wd["w_l3_sb1c"][:, :]))
    dBc = c.dma('Y', 'cl3bc', lambda y: y.dma_start(out=sbc[:, 8:16], in_=Wd["w_l3_sb3c"][:, :]))
    dBc = c.dma('Y', 'cl3bc', lambda y: y.dma_start(out=sbc[:, 16:24], in_=Wd["w_l3_sb2c"][:, :]))
    dBc = ('cl3bc', c.semc['cl3bc'])
    rings = mk_glu_rings(c, st, 1024, 'l3f')
    for e in range(E):
        dwb = None
        for tc in range(NTC):
            bk[6].guard()
            tB = c.op('T', lambda t, tc=tc, e=e: t.matmul(
                bk[6].t[:, :], eyev[0:8, e, :], WeT[0:8, TC[tc][0]:TC[tc][1]],
                start=True, stop=True), deps=[dWeT, dEye])
            dwb = c.op('V', lambda v, tc=tc: v.tensor_copy(
                out=wb[:, TC[tc][0]:TC[tc][1]], in_=bk[6].t[:, :]), deps=[tB, dwb])
            bk[6].consume(dwb)
        if e == 0:
            def extra_down(ob, dc, tc):
                c.op('T', lambda t, dc=dc, tc=tc, ob=ob: t.matmul(
                    ob.t[:, :], b2sb[0:8, dc * 128:(dc + 1) * 128], WeT[0:8, TC[tc][0]:TC[tc][1]],
                    start=False, stop=True), deps=[dB2, dWeT])
        else:
            extra_down = None
        emit_glu(c, st, rings, st.xnb, st.xnbReady, Wd["w_l3_w1"][e], Wd["w_l3_w3"][e],
                 Wd["w_l3_w2"][e], 1024, 2, wbv=wb, dwb=dwb, extra_down=extra_down,
                 b1c=b1c[:, e, :], b3c=b3c[:, e, :], dBc=dBc)
    emit_glu(c, st, rings, st.xnb, st.xnbReady, Wd["w_l3_sw1"], Wd["w_l3_sw3"],
             Wd["w_l3_sw2"], 1024, 2, wbv=None, dwb=None, extra_down=None,
             b1c=sbc[:, 0:8], b3c=sbc[:, 8:16], b2c=sbc[:, 16:24], dBc=dBc)
    st.ar.pop()



L3F = {"mask": True, "wo": True, "rows": 67, "shift": True}


def emit_l3_attn(c, st, Wd):
    bk = st.banks
    c.phase()
    st.ar.push()
    Vt = st.ar.bf3(NTOK, 1040)
    st.ar.push()
    wv = st.ar.bf(KC * 1024).rearrange("p (c h m) -> p c h m", c=KC, h=16)
    dWv = None
    for h in range(16):
        dWv = c.dma('G', 'cwv', lambda g, h=h: g.dma_start(
            out=wv[:, :, h, :], in_=Wd["w_l3_wqkv"][:, 2048 + h * 64:2048 + (h + 1) * 64]
            .rearrange("(c p) m -> p c m", p=128)))
    dVm = c.op('V', lambda v: v.memset(Vt[:, :, :], 1.0))
    for jt in range(NTOK):
        for vc in range(2):
            b = bk[vc]
            b.guard()
            for kc in range(KC):
                c.op('T', lambda t, jt=jt, vc=vc, kc=kc, b=b: t.matmul(
                    b.t[:, :], st.xnb[:, kc, jt * 128:(jt + 1) * 128], wv[:, kc, vc * 8:(vc + 1) * 8, :],
                    start=(kc == 0), stop=(kc == KC - 1)),
                    deps=[dWv, st.xnbReady], inc=(1 if kc == KC - 1 else 0))
            tV = c.tip('T')
            dcp = None
            for h8 in range(8):
                hh = vc * 8 + h8
                dcp = c.op('V', lambda v, jt=jt, hh=hh, h8=h8, b=b: v.tensor_copy(
                    out=Vt[:, jt, hh * 65:hh * 65 + 64], in_=b.t[:, h8 * 64:(h8 + 1) * 64]),
                    deps=[tV, dVm, dcp])
                b.consume(dcp)
    tVdone = c.tip('T')
    st.ar.pop()
    st.ar.push()
    qkraw = st.ar.bf(KC * 256)
    qk = qkraw.rearrange("p (c w m) -> p c w m", c=KC, w=4)
    wos = qkraw[:, 0:1024]
    Qs = [st.ar.bf(T), st.ar.bf(T)]
    Ks = [st.ar.bf(T), st.ar.bf(T)]
    Ptd = [st.ar.bf3(8, 1024), st.ar.bf3(8, 1024)]
    qsq = st.ar.bf(T)
    siRow = qsq
    rowD = st.ar.f32(512)
    k2m = st.ar.f32(8)
    Osb = st.ar.bf(512)
    OsbB = st.ar.bf(512)
    recb = OsbB
    l3m = st.ar.bf(896)
    OnG = st.ar.bf(T)
    dM = c.dma('G', 'cmisc2', lambda g: g.dma_start(out=l3m[:, :], in_=Wd["c_l3mask"][:, :]),
               deps=[tVdone])
    tPrevScores = None
    prevB5 = None
    prevA = None
    prevSh = None
    dOn = None
    tWoRead = None
    dBs = [None, None]
    dRows = [None, None]
    for hg in range(8):
        dWqkL = []
        for w in range(4):
            col0 = [0, 64, 1024, 1024 + 64][w] + hg * 128
            dWqkL.append(c.dma('G', f'cqk_{w}', lambda g, col0=col0, w=w: g.dma_start(
                out=qk[:, :, w, :], in_=Wd["w_l3_wqkv"][:, col0:col0 + 64]
                .rearrange("(c p) m -> p c m", p=128)), deps=[tPrevScores, tVdone, tWoRead]))
        for hh in range(2):
            h = hg * 2 + hh
            dQe = dKe = None
            for tc in range(NTC):
                b = bk[tc % 2]
                b.guard()
                for kc in range(KC):
                    c.op('T', lambda t, kc=kc, tc=tc, hh=hh, b=b: t.matmul(
                        b.t[0:64, :], qk[:, kc, hh, :], st.xnb[:, kc, TC[tc][0]:TC[tc][1]],
                        start=(kc == 0), stop=(kc == KC - 1)),
                        deps=[dWqkL[hh], st.xnbReady], inc=(1 if kc == KC - 1 else 0))
                tQ = c.tip('T')
                dQe = c.op('S', lambda s, tc=tc, hh=hh, b=b: s.activation(
                    out=Qs[hh][0:64, TC[tc][0]:TC[tc][1]], in_=b.t[0:64, :],
                    func=AF.Copy, scale=0.125), deps=[tQ, dQe])
                b.consume(dQe)
                b2_ = bk[2 + tc % 2]
                b2_.guard()
                for kc in range(KC):
                    c.op('T', lambda t, kc=kc, tc=tc, hh=hh, b2_=b2_: t.matmul(
                        b2_.t[0:64, :], qk[:, kc, 2 + hh, :],
                        st.xnb[:, kc, TC[tc][0]:TC[tc][1]],
                        start=(kc == 0), stop=(kc == KC - 1)),
                        deps=[dWqkL[2 + hh]], inc=(1 if kc == KC - 1 else 0))
                tK = c.tip('T')
                dKe = c.op('S', lambda s, tc=tc, hh=hh, b2_=b2_: s.copy(
                    out=Ks[hh][0:64, TC[tc][0]:TC[tc][1]], in_=b2_.t[0:64, :]), deps=[tK, dKe])
                b2_.consume(dKe)
            dKone = c.op('V', lambda v, hh=hh: v.memset(Ks[hh][64:65, :], 1.0), deps=[tVdone])
            dIot = c.dma('G', f'crow0_{hh}', lambda g, hh=hh: g.dma_start(
                out=Ks[hh][65:66, :], in_=Wd["c_iota128"][:, :]), deps=[tPrevScores])
            dIot = c.dma('G', f'crow3_{hh}', lambda g, hh=hh: g.dma_start(
                out=Ks[hh][66:67, :], in_=Wd["c_itile"][:, :]), deps=[tPrevScores, dIot])
            dSlp = c.dma('G', f'crow1_{hh}', lambda g, hh=hh, h=h: g.dma_start(
                out=Qs[hh][65:66, :], in_=Wd["c_l3slope"][h:h + 1, :]), deps=[tPrevScores])
            dSlp = c.dma('G', f'crow4_{hh}', lambda g, hh=hh, h=h: g.dma_start(
                out=Qs[hh][66:67, :], in_=Wd["c_l3slope"][h:h + 1, :]), deps=[tPrevScores, dSlp])
            dSi = c.dma('G', f'crow2_{hh}', lambda g, h=h: g.dma_start(
                out=siRow[64:65, :], in_=Wd["c_l3si"][h:h + 1, :]),
                deps=[tPrevScores, prevA, dBs[1 - hh]])
            dSqQ = c.op('S', lambda s, hh=hh: s.activation(
                out=qsq[0:64, :], in_=Qs[hh][0:64, :], func=AF.Square), deps=[dQe])
            dRowQ = None
            for tc in range(NTC):
                b = bk[4]
                b.guard()
                tR = c.op('T', lambda t, tc=tc, b=b: t.matmul(
                    b.t[0:65, :], st.cselB[:, :], qsq[0:64, TC[tc][0]:TC[tc][1]],
                    start=True, stop=True), deps=[dSqQ, st.dCsel, tPrevScores])
                dRowQ = c.op('S', lambda s, tc=tc, hh=hh, b=b: s.copy(
                    out=Qs[hh][64:65, TC[tc][0]:TC[tc][1]], in_=b.t[64:65, :]),
                    deps=[tR, dRowQ, prevA])
                b.consume(dRowQ)
            dSqK = c.op('S', lambda s, hh=hh: s.activation(
                out=qsq[0:64, :], in_=Ks[hh][0:64, :], func=AF.Square), deps=[dKe, dRowQ])
            dKm = None
            for tc in range(NTC):
                b = bk[4]
                b.guard()
                tR = c.op('T', lambda t, tc=tc, b=b: t.matmul(
                    b.t[0:65, :], st.cselB[:, :], qsq[0:64, TC[tc][0]:TC[tc][1]],
                    start=True, stop=True), deps=[dSqK])
                dKm = c.op('V', lambda v, tc=tc, b=b: v.tensor_reduce(
                    out=k2m[64:65, tc:tc + 1], in_=b.t[64:65, :], axis=AX.X, op=ALU.max),
                    deps=[tR, dKm])
                b.consume(dKm)
            dKm = c.op('V', lambda v: v.tensor_reduce(
                out=k2m[64:65, 4:5], in_=k2m[64:65, 0:4], axis=AX.X, op=ALU.max), deps=[dKm])
            dB = c.op('V', lambda v, hh=hh: v.tensor_scalar(
                out=Qs[hh][64:65, :], in0=Qs[hh][64:65, :], scalar1=k2m[64:65, 4:5], scalar2=None,
                op0=ALU.mult), deps=[dKm, dRowQ])
            dB = c.op('S', lambda s, hh=hh: s.activation(
                out=Qs[hh][64:65, :], in_=Qs[hh][64:65, :], func=AF.Sqrt), deps=[dB])
            dB = c.op('V', lambda v, hh=hh: v.tensor_tensor(
                out=Qs[hh][64:65, :], in0=Qs[hh][64:65, :], in1=siRow[64:65, :], op=ALU.add),
                deps=[dB, dSi])
            dB = c.op('V', lambda v, hh=hh: v.tensor_scalar(
                out=Qs[hh][64:65, :], in0=Qs[hh][64:65, :], scalar1=-1.0, scalar2=None,
                op0=ALU.mult), deps=[dB])
            dBs[hh] = dB
            dRows[hh] = (dIot, dSlp)
        units = [(hh, ib) for ib in range(NTC) for hh in range(2)]
        NU = len(units)
        dExpU = [None] * NU
        tPVU = [None] * NU
        dE = None
        for step in range(NU + 1):
            if step < NU:
                hh, ib = units[step]
                par = step % 2
                for jp in range(2 * ib, 8):
                    P = st.pairs[jp % 2]
                    P.guard()
                    tS = None
                    for hf in range(2):
                        jt = 2 * jp + hf
                        tS = c.op('T', lambda t, jt=jt, ib=ib, hh=hh, P=P, hf=hf: t.matmul(
                            P.half[hf].t[:, :], Ks[hh][0:67, jt * 128:(jt + 1) * 128],
                            Qs[hh][0:67, TC[ib][0]:TC[ib][1]],
                            start=True, stop=True),
                            deps=[dBs[hh], dKone, dRows[hh][0], dRows[hh][1], tPrevScores,
                                  tPVU[step - 2] if step >= 2 else None])
                    if jp < 2 * ib + 2:
                        for hf in range(2):
                            jt = 2 * jp + hf
                            delta = jt * 128 - ib * 512
                            tS = c.op('V', lambda v, P=P, hf=hf, delta=delta: v.tensor_tensor(
                                out=P.half[hf].t[:, :], in0=P.half[hf].t[:, :],
                                in1=l3m[:, 384 - delta:896 - delta], op=ALU.add),
                                deps=[tS, dM])
                    dE = c.op('S', lambda s, jp=jp, par=par, P=P: s.activation(
                        out=Ptd[par][:, jp, :], in_=P.t[:, :], func=AF.Exp),
                        deps=[tS, dE, tPVU[step - 2] if step >= 2 else None])
                    P.consume(dE)
                dExpU[step] = dE
            if step > 0:
                hh2, ib2 = units[step - 1]
                h = hg * 2 + hh2
                par2 = (step - 1) % 2
                b = bk[4 + (step - 1) % 2]
                b.guard()
                njt = NTOK - ib2 * 4
                for idx, jt in enumerate(range(ib2 * 4, NTOK)):
                    c.op('T', lambda t, jt=jt, h=h, b=b, par2=par2, idx=idx, njt=njt: t.matmul(
                        b.t[0:65, :], Vt[:, jt, h * 65:(h + 1) * 65],
                        Ptd[par2][:, jt // 2, (jt % 2) * 512:(jt % 2) * 512 + 512],
                        start=(idx == 0), stop=(idx == njt - 1)),
                        deps=[dExpU[step - 1]], inc=(1 if idx == njt - 1 else 0))
                tO = c.tip('T')
                tPVU[step - 1] = tO
                tPrevScores = tO
                dD = c.op('V', lambda v, b=b: v.tensor_scalar(
                    out=rowD[64:65, 0:512], in0=b.t[64:65, :], scalar1=1e-35, scalar2=None,
                    op0=ALU.add), deps=[tO, dBs[hh2]])
                b.consume(dD)
                dD = c.op('V', lambda v: v.reciprocal(
                    out=rowD[64:65, 0:512], in_=rowD[64:65, 0:512]), deps=[dD])
                dD = c.op('V', lambda v: v.tensor_copy(
                    out=recb[64:65, :], in_=rowD[64:65, 0:512]), deps=[dD, prevB5])
                p0 = (h % 2) * 64
                b6 = bk[6]
                b6.guard()
                tB5 = c.op('T', lambda t, p0=p0, b6=b6: t.matmul(
                    b6.t[p0:p0 + 64, :], st.ones[64:65, 0:64], recb[64:65, 0:512],
                    start=True, stop=True), deps=[dD, st.dOnes])
                if p0 == 0:
                    dO = c.op('S', lambda s, b=b: s.copy(out=Osb[0:64, :], in_=b.t[0:64, :]),
                              deps=[tO, prevA, dD])
                    b.consume(dO)
                else:
                    dOb = c.op('S', lambda s, b=b: s.copy(out=OsbB[0:64, :], in_=b.t[0:64, :]),
                               deps=[tO, prevA, prevSh, dD])
                    b.consume(dOb)
                    b7 = bk[7]
                    b7.guard()
                    tSh = c.op('T', lambda t, b7=b7: t.matmul(
                        b7.f[64:128, 0:512], st.identB[0:64, 0:64], OsbB[0:64, :],
                        start=True, stop=True), deps=[dOb, st.dIdB])
                    prevSh = tSh
                    dO = c.op('S', lambda s, b7=b7: s.copy(out=Osb[64:128, :], in_=b7.f[64:128, 0:512]),
                              deps=[tSh, prevA])
                    b7.consume(dO)
                dOn = c.op('V', lambda v, p0=p0, ib2=ib2, b6=b6: v.tensor_tensor(
                    out=OnG[p0:p0 + 64, TC[ib2][0]:TC[ib2][1]], in0=Osb[p0:p0 + 64, :],
                    in1=b6.t[p0:p0 + 64, :], op=ALU.mult), deps=[tB5, dO, tWoRead, dOn])
                b6.consume(dOn)
                prevB5 = tB5
                prevA = dOn
        if not L3F["wo"]:
            tWoRead = c.tip('T')
            continue
        dWo = c.dma('G', 'cwo', lambda g, hg=hg: g.dma_start(
            out=wos[:, :], in_=Wd["w_l3_wout"][hg * 128:(hg + 1) * 128, :]),
            deps=[tWoRead, c.tip('T')])
        for mc in range(KC):
            for tc in range(NTC):
                ob = bk[6]
                ob.guard()
                c.op('T', lambda t, mc=mc, tc=tc, ob=ob: t.matmul(
                    ob.t[:, :], wos[:, mc * 128:(mc + 1) * 128],
                    OnG[:, TC[tc][0]:TC[tc][1]],
                    start=True, stop=True), deps=[dWo, dOn])
                tD = c.tip('T')
                dAdd = c.op('V', lambda v, mc=mc, tc=tc, ob=ob: v.tensor_tensor(
                    out=st.xF[:, mc, TC[tc][0]:TC[tc][1]], in0=st.xF[:, mc, TC[tc][0]:TC[tc][1]],
                    in1=ob.t[:, :], op=ALU.add), deps=[tD])
                ob.consume(dAdd)
        tWoRead = c.tip('T')
    st.ar.pop()
    st.ar.pop()



def emit_l2_ffn(c, st, Wd):
    E = 8
    bk = st.banks
    c.phase()
    st.ar.push()
    Wt = st.ar.f32_3(NTOK, E)
    dWt = emit_routing(c, st, Wd, Wd["w_l2_rwt"], Wd["w_l2_rb"], E, 'l2', Wt)
    acc = st.ar.f32_3(NTOK, 1024)
    ews = [st.ar.bf3(KC, 512) for _ in range(2)]
    ewr = [st.ar.bf(512) for _ in range(2)]
    ewt = [None, None]
    accDep = dWt
    for e in range(E):
        for oc in range(2):
            sl = (e * 2 + oc) % 2
            ld = c.dma('G', f'cew_{sl}', lambda g, e=e, oc=oc, sl=sl: g.dma_start(
                out=ews[sl][:], in_=Wd["w_l2_ew"][e, 0:1024, oc * 512:(oc + 1) * 512]
                .rearrange("(c p) m -> p c m", p=128)), deps=[ewt[sl]])
            ldr = c.dma('G', f'cewr_{sl}', lambda g, e=e, oc=oc, sl=sl: g.dma_start(
                out=ewr[sl][0:1, :], in_=Wd["w_l2_ew"][e, 1024:1025, oc * 512:(oc + 1) * 512]),
                deps=[ewt[sl]])
            for tok in range(NTOK):
                ob = bk[tok % 2]
                ob.guard()
                for kc in range(KC):
                    c.op('T', lambda t, kc=kc, tok=tok, sl=sl, ob=ob: t.matmul(
                        ob.t[:, :], st.xnb[:, kc, tok * 128:(tok + 1) * 128], ews[sl][:, kc, :],
                        start=(kc == 0), stop=False), deps=[ld, st.xnbReady], inc=0)
                c.op('T', lambda t, sl=sl, ob=ob: t.matmul(
                    ob.t[:, :], st.ones[0:1, 0:128], ewr[sl][0:1, :],
                    start=False, stop=True), deps=[ldr, st.dOnes])
                tY = c.tip('T')
                if e == 0:
                    accDep = c.op('V', lambda v, tok=tok, oc=oc, e=e, ob=ob: v.tensor_scalar(
                        out=acc[:, tok, oc * 512:(oc + 1) * 512], in0=ob.t[:, :],
                        scalar1=Wt[:, tok, e:e + 1], scalar2=None, op0=ALU.mult),
                        deps=[tY, dWt, accDep])
                else:
                    accDep = c.op('V', lambda v, tok=tok, oc=oc, e=e, ob=ob: v.scalar_tensor_tensor(
                        out=acc[:, tok, oc * 512:(oc + 1) * 512], in0=ob.t[:, :],
                        scalar=Wt[:, tok, e:e + 1], in1=acc[:, tok, oc * 512:(oc + 1) * 512],
                        op0=ALU.mult, op1=ALU.add), deps=[tY, accDep])
                ob.consume(accDep)
            ewt[sl] = c.tip('T')
    tb = st.ar.bf(128)
    P = bk[7]
    dCast = None
    for tok in range(NTOK):
        for dc in range(KC):
            dCast = c.op('V', lambda v, tok=tok, dc=dc: v.tensor_copy(
                out=tb[:, :], in_=acc[:, tok, dc * 128:(dc + 1) * 128]), deps=[accDep, dCast])
            P.guard()
            tT = c.op('T', lambda t: t.transpose(P.t[:, 0:128], tb[:, :], st.identB[:]),
                      deps=[dCast, st.dIdB])
            dAdd = c.op('V', lambda v, tok=tok, dc=dc: v.tensor_tensor(
                out=st.xF[:, dc, tok * 128:(tok + 1) * 128],
                in0=st.xF[:, dc, tok * 128:(tok + 1) * 128], in1=P.t[:, 0:128], op=ALU.add),
                deps=[tT])
            P.consume(dAdd)
            dCast = dAdd
    st.ar.pop()


L2F = {"inter": True, "state": True, "wo": True, "nheads": 16, "core": True, "tmaj": True,
       "phi": True, "bias": True}


def emit_l2_attn(c, st, Wd):
    bk = st.banks
    c.phase()
    st.ar.push()
    phiq = [st.ar.bf(T), st.ar.bf(T)]
    phik = [st.ar.bf(T), st.ar.bf(T)]
    ktp = [st.ar.bf3(NTOK, 64), st.ar.bf3(NTOK, 64)]
    vtp = [st.ar.bf3(NTOK, 65), st.ar.bf3(NTOK, 65)]
    Sbf = st.ar.bf(160)
    OnG = st.ar.bf(T)
    Oraw = [st.ar.bf(T), st.ar.bf(T)]
    rowD2 = st.ar.f32(T)
    recb2 = st.ar.bf(T)
    Osb2 = st.ar.bf(512)
    wsl = [st.ar.bf3(KC, 64), st.ar.bf3(KC, 64)]
    wos = st.ar.bf(1024)
    worow = st.ar.bf(1024)
    vrow = st.ar.bf(64)
    rsc = st.ar.bf(512)
    msc = [st.ar.bf(512), st.ar.bf(512)]
    bqkc = st.ar.f32(32)
    zc64 = st.ar.f32(1)
    dZc = c.op('V', lambda v: v.memset(zc64[:, 0:1], 0.0))
    dBqk = c.dma('Y', 'cl2bc', lambda y: y.dma_start(out=bqkc[0:64, 0:16], in_=Wd["w_l2_bqc"][:, :]))
    dBqk = c.dma('Y', 'cl2bc', lambda y: y.dma_start(out=bqkc[0:64, 16:32], in_=Wd["w_l2_bkc"][:, :]))
    dBqk = ('cl2bc', c.semc['cl2bc'])
    dWoR = c.dma('G', 'cworow', lambda g: g.dma_start(
        out=worow[0:1, :], in_=Wd["w_l2_wo"][1024:1025, :]))
    tPrev = None
    tWoRead = None
    dOn = None
    prevPhi = None
    dD0 = None
    prevB6 = None
    for pr in range(8):
        dPq = [None, None]
        dPk = [None, None]
        dKt = [None, None]
        dVt = [None, None]
        for hh in range(2):
            h = 2 * pr + hh
            def proj_f(wkey, dst, dPrev, bcol, h=h, hh=hh):
                dSlab = c.dma('G', f'cl2w{hh}', lambda g, wkey=wkey, h=h, hh=hh: g.dma_start(
                    out=wsl[hh][:], in_=Wd[wkey][0:1024, h * 64:(h + 1) * 64]
                    .rearrange("(c p) m -> p c m", p=128)), deps=[c.tip('T')])
                dE = dPrev
                for tc in range(NTC):
                    b = bk[tc % 2]
                    b.guard()
                    for kc in range(KC):
                        c.op('T', lambda t, kc=kc, tc=tc, b=b, hh=hh: t.matmul(
                            b.t[0:64, :], wsl[hh][:, kc, :], st.xnb[:, kc, TC[tc][0]:TC[tc][1]],
                            start=(kc == 0), stop=(kc == KC - 1)),
                            deps=[dSlab, st.xnbReady], inc=(1 if kc == KC - 1 else 0))
                    tU = c.tip('T')
                    dR = c.op('S', lambda s, b=b, bcol=bcol: s.activation(
                        out=rsc[0:64, :], in_=b.t[0:64, :], func=AF.Relu,
                        bias=bcol), deps=[tU, dE, dBqk])
                    dMn = c.op('V', lambda v, b=b, bcol=bcol, hh=hh: v.tensor_scalar(
                        out=msc[hh][0:64, :], in0=b.t[0:64, :], scalar1=bcol, scalar2=zc64[0:64, 0:1],
                        op0=ALU.add, op1=ALU.min), deps=[tU, dE, dR, dBqk, dZc])
                    b.consume(dR)
                    b.consume(dMn)
                    dX = c.op('S', lambda s, hh=hh: s.activation(
                        out=msc[hh][0:64, :], in_=msc[hh][0:64, :], func=AF.Exp), deps=[dMn, dR])
                    dE = c.op('V', lambda v, tc=tc, dst=dst, hh=hh: v.tensor_tensor(
                        out=dst[0:64, TC[tc][0]:TC[tc][1]], in0=rsc[0:64, :], in1=msc[hh][0:64, :],
                        op=ALU.add), deps=[dX])
                return dE
            dPq[hh] = proj_f("w_l2_wq", phiq[hh], prevPhi if hh == 0 else dPk[0],
                             bqkc[0:64, h:h + 1])
            dPk[hh] = proj_f("w_l2_wk", phik[hh], dPq[hh], bqkc[0:64, 16 + h:17 + h])
            dKt[hh] = dPk[hh]
            b7 = bk[7]
            for ct in range(NTOK):
                b7.guard()
                tT = c.op('T', lambda t, ct=ct, b7=b7, hh=hh: t.transpose(
                    b7.t[0:128, 0:64], phik[hh][0:64, ct * 128:(ct + 1) * 128],
                    st.identB[0:64, 0:64]), deps=[dPk[hh], st.dIdB])
                dKt[hh] = c.op('V', lambda v, ct=ct, b7=b7, hh=hh: v.tensor_copy(
                    out=ktp[hh][:, ct, :], in_=b7.t[0:128, 0:64]), deps=[tT, dKt[hh]])
                b7.consume(dKt[hh])
            dSlab = c.dma('G', f'cl2w{hh}', lambda g, h=h, hh=hh: g.dma_start(
                out=wsl[hh][:], in_=Wd["w_l2_wv"][0:1024, h * 64:(h + 1) * 64]
                .rearrange("(c p) m -> p c m", p=128)), deps=[c.tip('T'), dKt[hh]])
            dRow2 = c.dma('G', f'cl2wr{hh}', lambda g, h=h, hh=hh: g.dma_start(
                out=vrow[0:1, :], in_=Wd["w_l2_wv"][1024:1025, h * 64:(h + 1) * 64]),
                deps=[c.tip('T')])
            dVm = c.op('V', lambda v, hh=hh: v.memset(vtp[hh][:, :, :], 1.0),
                       deps=[dKt[hh], tPrev])
            dVt[hh] = dVm
            for jt in range(NTOK):
                b = bk[2 + jt % 2]
                b.guard()
                for kc in range(KC):
                    c.op('T', lambda t, kc=kc, jt=jt, b=b, hh=hh: t.matmul(
                        b.t[:, 0:64], st.xnb[:, kc, jt * 128:(jt + 1) * 128], wsl[hh][:, kc, :],
                        start=(kc == 0), stop=False), deps=[dSlab, st.xnbReady], inc=0)
                c.op('T', lambda t, b=b: t.matmul(
                    b.t[:, 0:64], st.ones[0:1, 0:128], vrow[0:1, :],
                    start=False, stop=True), deps=[dRow2, st.dOnes])
                tU = c.tip('T')
                dVt[hh] = c.op('V', lambda v, jt=jt, b=b, hh=hh: v.tensor_copy(
                    out=vtp[hh][:, jt, 0:64], in_=b.t[:, 0:64]), deps=[tU, dVm, dVt[hh]])
                b.consume(dVt[hh])
        SOFF = [0, 80]
        ABANK = [bk[0], bk[1]]
        OBANK = [bk[2], bk[4]]
        SBANK = [bk[5], bk[3]]
        dS0 = [None, None]
        dSe = [None, None]
        prevO = [None, None]
        dOh = [None, None]
        for hh in range(2):
            dS0[hh] = c.op('V', lambda v, hh=hh: v.memset(Sbf[0:64, SOFF[hh]:SOFF[hh] + 65], 0.0),
                           deps=[tPrev, dVt[hh], dPq[hh], dD0])
            dSe[hh] = dS0[hh]
        for ct in range(NTOK):
            for hh in range(2):
                ba = ABANK[hh]
                ba.guard()
                tA = c.op('T', lambda t, ct=ct, ba=ba, hh=hh: t.matmul(
                    ba.t[:, 0:128], phik[hh][0:64, ct * 128:(ct + 1) * 128],
                    phiq[hh][0:64, ct * 128:(ct + 1) * 128],
                    start=True, stop=True), deps=[dPq[hh], dPk[hh], dVt[hh]])
                dAm = c.op('V', lambda v, ba=ba, hh=hh: v.tensor_tensor(
                    out=msc[hh][:, 0:128], in0=ba.t[:, 0:128], in1=st.triuB[:, :],
                    op=ALU.mult), deps=[tA, st.dTril, prevO[hh]])
                ba.consume(dAm)
                bo = OBANK[hh]
                bo.guard()
                c.op('T', lambda t, ct=ct, bo=bo, hh=hh: t.matmul(
                    bo.t[0:65, 0:128], vtp[hh][:, ct, :], msc[hh][:, 0:128],
                    start=True, stop=(ct == 0)), deps=[dAm], inc=(1 if ct == 0 else 0))
                if ct > 0:
                    c.op('T', lambda t, ct=ct, bo=bo, hh=hh: t.matmul(
                        bo.t[0:65, 0:128], Sbf[0:64, SOFF[hh]:SOFF[hh] + 65],
                        phiq[hh][0:64, ct * 128:(ct + 1) * 128],
                        start=False, stop=True), deps=[dSe[hh]])
                tO = c.tip('T')
                bs = SBANK[hh]
                if ct == 0:
                    bs.guard()
                tS = c.op('T', lambda t, ct=ct, bs=bs, hh=hh: t.matmul(
                    bs.t[0:64, 0:65], ktp[hh][:, ct, :], vtp[hh][:, ct, :],
                    start=(ct == 0), stop=True, skip_group_check=True),
                    deps=[dKt[hh], dVt[hh], dSe[hh]])
                if ct < NTOK - 1:
                    dSe[hh] = c.op('V', lambda v, bs=bs, hh=hh: v.tensor_copy(
                        out=Sbf[0:64, SOFF[hh]:SOFF[hh] + 65], in_=bs.t[0:64, 0:65]),
                        deps=[tS, dSe[hh]])
                    bs.consume(dSe[hh])
                dOh[hh] = c.op('S', lambda s, ct=ct, bo=bo, hh=hh: s.copy(
                    out=Oraw[hh][0:65, ct * 128:(ct + 1) * 128], in_=bo.t[0:65, 0:128]),
                    deps=[tO, prevO[hh], dD0])
                bo.consume(dOh[hh])
                prevO[hh] = dOh[hh]
        for hh in range(2):
            p0 = hh * 64
            dD0 = c.op('V', lambda v, hh=hh: v.tensor_scalar(
                out=rowD2[64:65, :], in0=Oraw[hh][64:65, :], scalar1=st.cst[0:1, 1:2],
                scalar2=None, op0=ALU.add), deps=[prevO[hh], st.dCst, dD0])
            dD0 = c.op('V', lambda v: v.reciprocal(
                out=rowD2[64:65, :], in_=rowD2[64:65, :]), deps=[dD0])
            dD0 = c.op('V', lambda v: v.tensor_copy(
                out=recb2[64:65, :], in_=rowD2[64:65, :]), deps=[dD0, prevB6])
            for tc in range(NTC):
                b6 = bk[6]
                b6.guard()
                tB6 = c.op('T', lambda t, tc=tc, p0=p0, b6=b6: t.matmul(
                    b6.t[p0:p0 + 64, :], st.ones[64:65, 0:64], recb2[64:65, TC[tc][0]:TC[tc][1]],
                    start=True, stop=True), deps=[dD0, st.dOnes])
                prevB6 = tB6
                if p0 == 0:
                    dOn = c.op('V', lambda v, tc=tc, b6=b6, hh=hh: v.tensor_tensor(
                        out=OnG[0:64, TC[tc][0]:TC[tc][1]], in0=Oraw[hh][0:64, TC[tc][0]:TC[tc][1]],
                        in1=b6.t[0:64, :], op=ALU.mult), deps=[tB6, tWoRead, dOn, prevO[hh]])
                    b6.consume(dOn)
                else:
                    b3 = bk[3]
                    b3.guard()
                    tSh = c.op('T', lambda t, tc=tc, b3=b3, hh=hh: t.matmul(
                        b3.t[64:128, :], st.identB[0:64, 0:64], Oraw[hh][0:64, TC[tc][0]:TC[tc][1]],
                        start=True, stop=True), deps=[prevO[hh], st.dIdB])
                    dSh = c.op('S', lambda s, tc=tc, b3=b3: s.copy(
                        out=Osb2[64:128, :], in_=b3.t[64:128, :]), deps=[tSh, dOn])
                    b3.consume(dSh)
                    dOn = c.op('V', lambda v, tc=tc, b6=b6: v.tensor_tensor(
                        out=OnG[64:128, TC[tc][0]:TC[tc][1]], in0=Osb2[64:128, :],
                        in1=b6.t[64:128, :], op=ALU.mult), deps=[tB6, dSh, tWoRead, dOn])
                    b6.consume(dOn)
        tPrev = c.tip('T')
        prevPhi = dOn
        dWo = c.dma('G', 'cwo2', lambda g, pr=pr: g.dma_start(
            out=wos[:, :], in_=Wd["w_l2_wo"][pr * 128:(pr + 1) * 128, :]),
            deps=[tWoRead, c.tip('T')])
        for mc in range(KC):
            for tc in range(NTC):
                ob = bk[4]
                ob.guard()
                c.op('T', lambda t, mc=mc, tc=tc, ob=ob, pr=pr: t.matmul(
                    ob.t[:, :], wos[:, mc * 128:(mc + 1) * 128], OnG[:, TC[tc][0]:TC[tc][1]],
                    start=True, stop=(pr != 0)), deps=[dWo, dOn],
                    inc=(1 if pr != 0 else 0))
                if pr == 0:
                    c.op('T', lambda t, mc=mc, tc=tc, ob=ob: t.matmul(
                        ob.t[:, :], worow[0:1, mc * 128:(mc + 1) * 128], st.ones[0:1, 0:512],
                        start=False, stop=True), deps=[dWoR, st.dOnes])
                tD = c.tip('T')
                dAdd = c.op('V', lambda v, mc=mc, tc=tc, ob=ob: v.tensor_tensor(
                    out=st.xF[:, mc, TC[tc][0]:TC[tc][1]], in0=st.xF[:, mc, TC[tc][0]:TC[tc][1]],
                    in1=ob.t[:, :], op=ALU.add), deps=[tD])
                ob.consume(dAdd)
        tWoRead = c.tip('T')
        tPrev = tWoRead
    st.ar.pop()


def emit_rms(c, st, chunks, inv_n, prevDep, wdep=None, scr=None):
    bk = st.banks
    if scr is None:
        st.ar.push()
        sq = [st.ar.bf(T), st.ar.bf(T)]
        r32 = st.ar.f32(T)[0:1, :]
        rbf = st.ar.bf(T)[0:1, :]
    else:
        sq = [scr[:, 0:2048], scr[:, 2048:4096]]
        r32 = scr[:, 4096:8192].bitcast(F32)[0:1, :]
        rbf = scr[:, 0:2048][0:1, :]
    for tc in range(NTC):
        bk[tc].guard()
    tRd = [None, None]
    nch = len(chunks)
    for i, (srcv, dstv, wcol) in enumerate(chunks):
        b = i % 2
        dSq = c.op('S', lambda s, srcv=srcv, b=b: s.activation(
            out=sq[b][:], in_=srcv, func=AF.Square), deps=[prevDep, tRd[b]])
        for tc in range(NTC):
            c.op('T', lambda t, tc=tc, b=b, i=i: t.matmul(
                bk[tc].t[0:1, :], st.ones[:, 0:1], sq[b][:, TC[tc][0]:TC[tc][1]],
                start=(i == 0), stop=(i == nch - 1)),
                deps=[dSq, st.dOnes], inc=(1 if tc == NTC - 1 else 0))
        tRd[b] = c.tip('T')
    tVar = c.tip('T')
    dR = None
    for tc in range(NTC):
        dR = c.op('S', lambda s, tc=tc: s.activation(
            out=r32[0:1, TC[tc][0]:TC[tc][1]], in_=bk[tc].t[0:1, :],
            func=AF.Sqrt, scale=inv_n, bias=st.cst[0:1, 0:1]), deps=[tVar, st.dCst])
        bk[tc].consume(dR)
    dRec = c.op('V', lambda v: v.reciprocal(out=r32[:, :], in_=r32[:, :]), deps=[dR])
    dRb = c.op('V', lambda v: v.tensor_copy(out=rbf[:, :], in_=r32[:, :]), deps=[dRec])
    out_dep = None
    for tc in range(NTC):
        bk[4].guard()
        tB = c.op('T', lambda t, tc=tc: t.matmul(
            bk[4].t[:, :], st.ones[0:1, 0:128], rbf[0:1, TC[tc][0]:TC[tc][1]],
            start=True, stop=True), deps=[dRb])
        for i, (srcv, dstv, wcol) in enumerate(chunks):
            d = c.op('V', lambda v, tc=tc, srcv=srcv, dstv=dstv, wcol=wcol: v.scalar_tensor_tensor(
                out=dstv[:, TC[tc][0]:TC[tc][1]], in0=srcv[:, TC[tc][0]:TC[tc][1]],
                scalar=wcol, in1=bk[4].t[:, :], op0=ALU.mult, op1=ALU.mult),
                deps=[tB, wdep, c.tip('S')])
            bk[4].consume(d)
            out_dep = d
    if scr is None:
        st.ar.pop()
    return out_dep


def emit_l1_ffn(c, st, Wd):
    E = 32
    bk = st.banks
    c.phase()
    st.ar.push()
    Wt = st.ar.f32_3(NTOK, E)
    dWt = emit_routing(c, st, Wd, Wd["w_l1_rwt"], Wd["w_l1_rb"], E, 'l1', Wt)
    WeT = st.ar.bf(T)
    dWeT = emit_wt_transpose(c, st, Wt, WeT, E, dWt)
    eye = st.ar.bf(4096)
    eyev = eye.rearrange("p (e m) -> p e m", e=32)
    dEye = c.dma('G', 'cmisc', lambda g: g.dma_start(
        out=eyev[0:32], in_=Wd["c_eyeb"][:, :].rearrange("e (f m) -> e f m", f=32)),
        deps=[dWeT])
    wb = st.ar.bf(T)
    st.ar.push()
    rs = mk_glu_rings(c, st, 1024, 'l1s')
    emit_glu(c, st, rs, st.xnb, st.xnbReady, Wd["w_l1_sgw"], Wd["w_l1_suw"],
             Wd["w_l1_sdw"], 1024, 1)
    st.ar.pop()
    c.phase()
    st.ar.push()
    re_ = mk_glu_rings(c, st, 512, 'l1e')
    for e in range(E):
        dwb = None
        for tc in range(NTC):
            bk[6].guard()
            tB = c.op('T', lambda t, tc=tc, e=e: t.matmul(
                bk[6].t[:, :], eyev[0:32, e, :], WeT[0:32, TC[tc][0]:TC[tc][1]],
                start=True, stop=True), deps=[dWeT, dEye])
            dwb = c.op('V', lambda v, tc=tc: v.tensor_copy(
                out=wb[:, TC[tc][0]:TC[tc][1]], in_=bk[6].t[:, :]), deps=[tB, dwb])
            bk[6].consume(dwb)
        emit_glu(c, st, re_, st.xnb, st.xnbReady, Wd["w_l1_gw"][e], Wd["w_l1_uw"][e],
                 Wd["w_l1_dw"][e], 512, 1, wbv=wb, dwb=dwb)
    st.ar.pop()
    st.ar.pop()


def emit_l1_attn(c, st, Wd):
    bk = st.banks
    c.phase()
    st.ar.push()
    awcol = st.ar.f32(8)
    mwcol = st.ar.f32(8)
    owcol = st.ar.f32(1)
    dAw = c.dma('Y', 'cnw', lambda y: y.dma_start(out=awcol[:, :], in_=Wd["w_l1_anw"][:, :]))
    dMw = c.dma('Y', 'cnw', lambda y: y.dma_start(out=mwcol[:, :], in_=Wd["w_l1_mnw"][:, :]))
    dOw = c.dma('Y', 'cnw', lambda y: y.dma_start(out=owcol[:, :], in_=Wd["w_l1_onw"][:, :]))
    dNw = ('cnw', c.semc['cnw'])
    h1 = st.ar.bf3(KC, T)
    h1v = [h1[:, kc, :] for kc in range(KC)]
    dH1 = emit_rms(c, st, [(st.xnb[:, kc, :], h1v[kc], awcol[:, kc:kc + 1]) for kc in range(KC)],
                   1.0 / 1024.0, st.xnbReady, wdep=dNw)
    st.ar.push()
    x1b = st.ar.bf3(KC, T)
    OnP = st.ar.bf3(2, T)
    pk4 = st.ar.bf(8192)
    qf = pk4[:, 0:2048]
    kf = pk4[:, 2048:4096]
    kt = pk4[:, 4096:6144].rearrange("p (a b) -> p a b", a=NTOK)
    vt = pk4[:, 6144:8192].rearrange("p (a b) -> p a b", a=NTOK)
    wsl = [st.ar.bf3(KC, 128) for _ in range(2)]
    wop = st.ar.bf(2048)
    wopv = wop.rearrange("p (w m) -> p w m", w=2)
    Sbf = st.ar.bf(128)
    Am = st.ar.bf(128)
    dX1 = None
    for kc in range(KC):
        dX1 = c.op('V', lambda v, kc=kc: v.tensor_copy(
            out=x1b[:, kc, :], in_=st.xnb[:, kc, :]), deps=[st.xnbReady, dX1, c.tip('T')])
    wslT = [None, None]
    dOnP = None
    tWoRead = None
    for h in range(8):
        pp = h % 2
        dSl = c.dma('G', 'cl1w_0', lambda g, h=h: g.dma_start(
            out=wsl[0][:], in_=Wd["w_l1_wq"][:, h * 128:(h + 1) * 128]
            .rearrange("(c p) m -> p c m", p=128)), deps=[wslT[0]])
        dQf = None
        for tc in range(NTC):
            b = bk[tc % 2]
            b.guard()
            for kc in range(KC):
                c.op('T', lambda t, kc=kc, tc=tc, b=b: t.matmul(
                    b.t[:, :], wsl[0][:, kc, :], h1[:, kc, TC[tc][0]:TC[tc][1]],
                    start=(kc == 0), stop=(kc == KC - 1)),
                    deps=[dSl, dH1], inc=(1 if kc == KC - 1 else 0))
            tQ = c.tip('T')
            dQf = c.op('S', lambda s, tc=tc, b=b: s.copy(
                out=qf[:, TC[tc][0]:TC[tc][1]], in_=b.t[:, :]), deps=[tQ, dQf])
            b.consume(dQf)
        wslT[0] = c.tip('T')
        dSk = c.dma('G', 'cl1w_1', lambda g, h=h: g.dma_start(
            out=wsl[1][:], in_=Wd["w_l1_wk"][:, h * 128:(h + 1) * 128]
            .rearrange("(c p) m -> p c m", p=128)), deps=[wslT[1]])
        dKf = None
        for tc in range(NTC):
            b = bk[2 + tc % 2]
            b.guard()
            for kc in range(KC):
                c.op('T', lambda t, kc=kc, tc=tc, b=b: t.matmul(
                    b.t[:, :], wsl[1][:, kc, :], h1[:, kc, TC[tc][0]:TC[tc][1]],
                    start=(kc == 0), stop=(kc == KC - 1)),
                    deps=[dSk, dH1], inc=(1 if kc == KC - 1 else 0))
            tK = c.tip('T')
            dKf = c.op('S', lambda s, tc=tc, b=b: s.copy(
                out=kf[:, TC[tc][0]:TC[tc][1]], in_=b.t[:, :]), deps=[tK, dKf])
            b.consume(dKf)
        wslT[1] = c.tip('T')
        dSkt = c.dma('G', 'cl1w_0', lambda g, h=h: g.dma_start(
            out=wsl[0][:], in_=Wd["w_l1_wk"][:, h * 128:(h + 1) * 128]
            .rearrange("(c p) m -> p c m", p=128)), deps=[wslT[0]])
        dKt = None
        for jt in range(NTOK):
            b = bk[jt % 2]
            b.guard()
            for kc in range(KC):
                c.op('T', lambda t, kc=kc, jt=jt, b=b: t.matmul(
                    b.t[:, 0:128], h1[:, kc, jt * 128:(jt + 1) * 128], wsl[0][:, kc, :],
                    start=(kc == 0), stop=(kc == KC - 1)),
                    deps=[dSkt, dH1], inc=(1 if kc == KC - 1 else 0))
            tU = c.tip('T')
            dKt = c.op('V', lambda v, jt=jt, b=b: v.tensor_copy(
                out=kt[:, jt, :], in_=b.t[:, 0:128]), deps=[tU, dKt])
            b.consume(dKt)
        wslT[0] = c.tip('T')
        dSv = c.dma('G', 'cl1w_1', lambda g, h=h: g.dma_start(
            out=wsl[1][:], in_=Wd["w_l1_wv"][:, h * 128:(h + 1) * 128]
            .rearrange("(c p) m -> p c m", p=128)), deps=[wslT[1]])
        dVt = None
        for jt in range(NTOK):
            b = bk[2 + jt % 2]
            b.guard()
            for kc in range(KC):
                c.op('T', lambda t, kc=kc, jt=jt, b=b: t.matmul(
                    b.t[:, 0:128], h1[:, kc, jt * 128:(jt + 1) * 128], wsl[1][:, kc, :],
                    start=(kc == 0), stop=(kc == KC - 1)),
                    deps=[dSv, dH1], inc=(1 if kc == KC - 1 else 0))
            tU = c.tip('T')
            dVt = c.op('V', lambda v, jt=jt, b=b: v.tensor_copy(
                out=vt[:, jt, :], in_=b.t[:, 0:128]), deps=[tU, dVt])
            b.consume(dVt)
        wslT[1] = c.tip('T')
        dS0 = c.op('V', lambda v: v.memset(Sbf[:, :], 0.0), deps=[c.tip('T'), dOnP])
        dSe = dS0
        prevO = None
        dOh = None
        for ct in range(NTOK):
            ba = bk[ct % 2]
            ba.guard()
            tA = c.op('T', lambda t, ct=ct, ba=ba: t.matmul(
                ba.t[:, 0:128], kf[:, ct * 128:(ct + 1) * 128], qf[:, ct * 128:(ct + 1) * 128],
                start=True, stop=True), deps=[dQf, dKf, dVt, dKt])
            dAm = c.op('V', lambda v, ba=ba: v.tensor_tensor(
                out=Am[:, :], in0=ba.t[:, 0:128], in1=st.triuB[:, :],
                op=ALU.mult), deps=[tA, st.dTril, prevO])
            ba.consume(dAm)
            bo = bk[4]
            bo.guard()
            c.op('T', lambda t, ct=ct, bo=bo: t.matmul(
                bo.t[:, 0:128], vt[:, ct, :], Am[:, :],
                start=True, stop=(ct == 0)), deps=[dAm], inc=(1 if ct == 0 else 0))
            if ct > 0:
                c.op('T', lambda t, ct=ct, bo=bo: t.matmul(
                    bo.t[:, 0:128], Sbf[:, :], qf[:, ct * 128:(ct + 1) * 128],
                    start=False, stop=True), deps=[dSe])
            tO = c.tip('T')
            bs = bk[5]
            if ct == 0:
                bs.guard()
            tS = c.op('T', lambda t, ct=ct, bs=bs: t.matmul(
                bs.t[:, 0:128], kt[:, ct, :], vt[:, ct, :],
                start=(ct == 0), stop=True, skip_group_check=True), deps=[dKt, dVt, dSe])
            if ct < NTOK - 1:
                dSe = c.op('V', lambda v, bs=bs: v.tensor_copy(
                    out=Sbf[:, :], in_=bs.t[:, 0:128]), deps=[tS, dSe])
                bs.consume(dSe)
            dOh = c.op('S', lambda s, ct=ct, pp=pp, bo=bo: s.copy(
                out=OnP[:, pp, ct * 128:(ct + 1) * 128], in_=bo.t[:, 0:128]),
                deps=[tO, dOh, tWoRead, dOnP])
            bo.consume(dOh)
            prevO = dOh
        dOnP = emit_rms(c, st, [(OnP[:, pp, :], OnP[:, pp, :], owcol[:, 0:1])],
                        1.0 / 128.0, dOh, wdep=dNw, scr=pk4)
        if pp == 1:
            pr = h // 2
            dWo = c.dma('G', 'cl1wo', lambda g, pr=pr: g.dma_start(
                out=wopv[:], in_=Wd["w_l1_wo"][pr * 256:(pr + 1) * 256, :]
                .rearrange("(w p) m -> p w m", p=128)), deps=[tWoRead, c.tip('T')])
            for mc in range(KC):
                for tc in range(NTC):
                    ob = bk[6]
                    ob.guard()
                    for w in range(2):
                        c.op('T', lambda t, w=w, mc=mc, tc=tc, ob=ob: t.matmul(
                            ob.t[:, :], wopv[:, w, mc * 128:(mc + 1) * 128],
                            OnP[:, w, TC[tc][0]:TC[tc][1]],
                            start=(w == 0), stop=(w == 1)),
                            deps=[dWo, dOnP], inc=(1 if w == 1 else 0))
                    tD = c.tip('T')
                    dAdd = c.op('V', lambda v, mc=mc, tc=tc, ob=ob: v.tensor_tensor(
                        out=x1b[:, mc, TC[tc][0]:TC[tc][1]], in0=x1b[:, mc, TC[tc][0]:TC[tc][1]],
                        in1=ob.t[:, :], op=ALU.add), deps=[tD, dX1])
                    ob.consume(dAdd)
                    dX1 = dAdd
            tWoRead = c.tip('T')
    dXF = None
    for kc in range(KC):
        for tc in range(NTC):
            dXF = c.op('V', lambda v, kc=kc, tc=tc: v.tensor_tensor(
                out=st.xF[:, kc, TC[tc][0]:TC[tc][1]], in0=st.xF[:, kc, TC[tc][0]:TC[tc][1]],
                in1=x1b[:, kc, TC[tc][0]:TC[tc][1]], op=ALU.add), deps=[dX1, dXF])
    dH2 = emit_rms(c, st, [(x1b[:, kc, :], h1v[kc], mwcol[:, kc:kc + 1]) for kc in range(KC)],
                   1.0 / 1024.0, dXF, wdep=dNw, scr=pk4)
    st.ar.pop()
    c.phase()
    st.ar.push()
    rm = mk_glu_rings(c, st, 4096, 'l1m')
    emit_glu(c, st, rm, h1, dH2, Wd["w_l1_wgate"], Wd["w_l1_wup"],
             Wd["w_l1_wdown"], 4096, 4)
    st.ar.pop()
    st.ar.pop()


BLOCK_EMITTERS = {}


def emit_rows(c, st):
    st.ar.push()
    st.mu32 = st.ar.f32(T)[0:1, :]
    st.rstd32 = st.ar.f32(T)[0:1, :]
    st.mu_bf = st.ar.bf(T)[0:1, :]
    st.rstd_bf = st.ar.bf(T)[0:1, :]


def emit_rows_pop(c, st):
    st.ar.pop()


def emit_block0(c, st, Wd):
    emit_rows(c, st)
    emit_ln(c, st)
    emit_l0_ffn(c, st, Wd)
    emit_rows_pop(c, st)
    emit_l0_attn(c, st, Wd)


BLOCK_EMITTERS[0] = (emit_block0, ("w_l0_",))


def emit_block3(c, st, Wd):
    emit_rows(c, st)
    emit_ln(c, st)
    emit_l3_ffn(c, st, Wd)
    emit_rows_pop(c, st)
    emit_l3_attn(c, st, Wd)


BLOCK_EMITTERS[3] = (emit_block3, ("w_l3_",))


def emit_block2(c, st, Wd):
    emit_rows(c, st)
    emit_ln(c, st)
    emit_l2_ffn(c, st, Wd)
    emit_rows_pop(c, st)
    emit_l2_attn(c, st, Wd)


def emit_block2f(c, st, Wd):
    emit_rows(c, st)
    emit_ln(c, st)
    emit_l2_ffn(c, st, Wd)
    emit_rows_pop(c, st)


def emit_block2a(c, st, Wd):
    emit_rows(c, st)
    emit_ln(c, st)
    emit_rows_pop(c, st)
    emit_l2_attn(c, st, Wd)


BLOCK_EMITTERS[2] = (emit_block2, ("w_l2_",))


def emit_block1(c, st, Wd):
    emit_rows(c, st)
    emit_ln(c, st)
    emit_l1_ffn(c, st, Wd)
    emit_rows_pop(c, st)
    emit_l1_attn(c, st, Wd)


def emit_block1f(c, st, Wd):
    emit_rows(c, st)
    emit_ln(c, st)
    emit_l1_ffn(c, st, Wd)
    emit_rows_pop(c, st)


def emit_block1a(c, st, Wd):
    emit_rows(c, st)
    emit_ln(c, st)
    emit_rows_pop(c, st)
    emit_l1_attn(c, st, Wd)


BLOCK_EMITTERS[1] = (emit_block1, ("w_l1_",))
BLOCK_EMITTERS[10] = (emit_block1f, ("w_l1_",))
BLOCK_EMITTERS[11] = (emit_block1a, ("w_l1_",))
BLOCK_EMITTERS[20] = (emit_block2f, ("w_l2_",))
BLOCK_EMITTERS[21] = (emit_block2a, ("w_l2_",))


def emit_block3f(c, st, Wd):
    emit_rows(c, st)
    emit_ln(c, st)
    emit_l3_ffn(c, st, Wd)
    emit_rows_pop(c, st)


def emit_block3a(c, st, Wd):
    emit_rows(c, st)
    emit_ln(c, st)
    emit_rows_pop(c, st)
    emit_l3_attn(c, st, Wd)


BLOCK_EMITTERS[30] = (emit_block3f, ("w_l3_",))
BLOCK_EMITTERS[31] = (emit_block3a, ("w_l3_",))


def build_program(prep, consts, blocks):
    nc = bass.Bass("TRN2", target_bir_lowering=False)
    x_in = nc.dram_tensor("x_in", [D, T], F32, kind="ExternalInput").ap()
    y_out = nc.dram_tensor("y_out", [D, T], F32, kind="ExternalOutput").ap()
    Wd = {}
    need = set()
    for b in blocks:
        need.update(BLOCK_EMITTERS[b][1])
    for k, arr in list(consts.items()) + list(prep.items()):
        if k.startswith("c_") or any(k.startswith(p) for p in need):
            Wd[k] = nc.dram_tensor(k, list(arr.shape), F32, kind="ExternalInput").ap()
    c = Ctx(nc)
    st = St()
    st.xF = c.sb("xF", [128, KC, T], F32)
    st.xnb = c.sb("xnb", [128, KC, T], BF16)
    st.ones = c.sb("ones", [128, 512], BF16)
    st.onesF = c.sb("onesF", [128, 128], F32)
    st.identB = c.sb("identB", [128, 128], BF16)
    st.identF = c.sb("identF", [128, 128], F32)
    st.cselB = c.sb("cselB", [64, 65], BF16)
    st.trilB = c.sb("trilB", [128, 128], BF16)
    st.triuB = c.sb("triuB", [128, 128], BF16)
    st.cst = c.sb("cst", [1, 16], F32)
    st.pairs = [Pair(c, j) for j in range(3)]
    st.banks = [st.pairs[0].half[0], st.pairs[0].half[1],
                st.pairs[1].half[0], st.pairs[1].half[1],
                st.pairs[2].half[0], st.pairs[2].half[1],
                Bank(c, 6), Bank(c, 7, bf16=True)]
    st.ar = Arena(c, ARCOLS)
    emit_consts(c, st, Wd)
    emit_load_x(c, st, x_in)
    for b in blocks:
        BLOCK_EMITTERS[b][0](c, st, Wd)
    emit_store_x(c, st, y_out)
    return c.finish()


def _in_maps(prep, consts, blocks, xs):
    need = set()
    for b in blocks:
        need.update(BLOCK_EMITTERS[b][1])
    base = {}
    for k, arr in list(consts.items()) + list(prep.items()):
        if k.startswith("c_") or any(k.startswith(p) for p in need):
            base[k] = arr
    return [dict(base, x_in=x) for x in xs]


STAGES = [[0, 1, 2, 3]]


def kernel(**inputs):
    z = inputs
    prep = host_prep(z)
    consts = host_consts()
    x = np.asarray(z["L0_attn_x"], np.float32)
    B = x.shape[0]
    xs = [np.ascontiguousarray(x[b].T) for b in range(B)]
    for blocks in STAGES:
        nc = build_program(prep, consts, blocks)
        res = run_bass_kernel_spmd(nc, _in_maps(prep, consts, blocks, xs),
                                   core_ids=list(range(B)))
        xs = [np.asarray(r["y_out"], np.float32) for r in res.results]
    out = np.stack([xf.T for xf in xs], axis=0)
    return np.ascontiguousarray(out)


# version 08
# speedup vs baseline: 1.1930x; 1.1930x over previous
import numpy as np
import ml_dtypes
import concourse.bass as bass
import concourse.mybir as mybir
from concourse.bass_utils import run_bass_kernel_spmd
from contextlib import ExitStack

F32 = mybir.dt.float32
BF16 = mybir.dt.bfloat16
AF = mybir.ActivationFunctionType
ALU = mybir.AluOpType
AX = mybir.AxisListType

T = 2048
D = 1024
KC = D // 128
NTC = T // 512
NTOK = T // 128
ARCOLS = 55296

ENGS = ['T', 'S', 'V', 'G', 'Y']


class Ctx:
    def __init__(self, nc):
        self.nc = nc
        self.es = ExitStack()
        self.q = {e: [] for e in ENGS}
        self.sems = {}
        self.semc = {}
        self.waited = {e: {p: 0 for p in ENGS} for e in ENGS}
        for e in ENGS:
            self.sems[e] = self.es.enter_context(nc.semaphore("sem" + e))
            self.semc[e] = 0

    def chan(self, name):
        if name not in self.sems:
            self.sems[name] = self.es.enter_context(self.nc.semaphore(name))
            self.semc[name] = 0
            for e in ENGS:
                self.waited[e][name] = 0
        return name

    def dma(self, eng, ch, fn, deps=()):
        self.chan(ch)
        for d in deps:
            if d is not None:
                self.wait(eng, d[0], d[1])
        sem = self.sems[ch]
        self.semc[ch] += 16
        def wrapped(e, fn=fn, sem=sem):
            fn(e).then_inc(sem, 16)
        self.q[eng].append(wrapped)
        return (ch, self.semc[ch])

    def sb(self, name, shape, dt):
        return self.es.enter_context(self.nc.sbuf_tensor(name, shape, dt))

    def ps(self, name, shape, dt):
        return self.es.enter_context(self.nc.psum_tensor(name, shape, dt))

    def wait(self, eng, prod, val):
        if val <= 0 or self.waited[eng].get(prod, 0) >= val:
            return
        self.waited[eng][prod] = val
        sem = self.sems[prod]
        self.q[eng].append(lambda e, sem=sem, v=val: e.wait_ge(sem, v))

    def op(self, eng, fn, deps=(), inc=1):
        for d in deps:
            if d is not None:
                self.wait(eng, d[0], d[1])
        if inc:
            sem = self.sems[eng]
            self.semc[eng] += inc
            def wrapped(e, fn=fn, sem=sem, n=inc):
                fn(e).then_inc(sem, n)
            self.q[eng].append(wrapped)
        else:
            self.q[eng].append(fn)
        return (eng, self.semc[eng])

    def tip(self, eng):
        return (eng, self.semc[eng])

    def phase(self):
        snap = dict(self.semc)
        for e in ENGS:
            for p in snap:
                if p != e:
                    self.wait(e, p, snap[p])

    def finish(self):
        nc = self.nc
        q = self.q
        with nc.Block() as block:
            @block.tensor
            def _(t):
                for fn in q['T']: fn(t)
            @block.scalar
            def _(s):
                for fn in q['S']: fn(s)
            @block.vector
            def _(v):
                for fn in q['V']: fn(v)
            @block.gpsimd
            def _(g):
                for fn in q['G']: fn(g)
            @block.sync
            def _(y):
                for fn in q['Y']: fn(y)
        self.es.close()
        return nc


class Bank:

    def __init__(self, c, i, bf16=False, view=None):
        self.c = c
        if view is not None:
            self.t = view
            self.f = view
        elif bf16:
            self.t = c.ps(f"bank{i}", [128, 1024], BF16)
            self.f = self.t[:, :].bitcast(F32)
        else:
            self.t = c.ps(f"bank{i}", [128, 512], F32)
            self.f = self.t[:, :]
        self.consumers = []
        self.ready = None

    def guard(self):
        for eng, cnt in self.consumers:
            self.c.wait('T', eng, cnt)
        self.consumers = []

    def consume(self, dep):
        self.consumers.append(dep)


class Pair:

    def __init__(self, c, j):
        self.t = c.ps(f"pair{j}", [128, 1024], F32)
        self.half = [Bank(c, 2 * j, view=self.t[:, 0:512]),
                     Bank(c, 2 * j + 1, view=self.t[:, 512:1024])]

    def guard(self):
        self.half[0].guard()
        self.half[1].guard()

    def consume(self, dep):
        self.half[0].consume(dep)
        self.half[1].consume(dep)


class Arena:
    def __init__(self, c, cols):
        self.c = c
        self.t = c.sb("AR", [128, cols], BF16)
        self.cols = cols
        self.off = 0
        self.stack = []

    def push(self):
        self.stack.append(self.off)

    def pop(self):
        self.off = self.stack.pop()

    def bf(self, ncols):
        o = self.off
        assert o + ncols <= self.cols, f"arena overflow {o}+{ncols}>{self.cols}"
        self.off += ncols
        return self.t[:, o:o + ncols]

    def bf3(self, n1, n2):
        return self.bf(n1 * n2).rearrange("p (a b) -> p a b", a=n1)

    def f32(self, n):
        if self.off % 2:
            self.off += 1
        o = self.off
        self.off += 2 * n
        assert self.off <= self.cols, "arena overflow (f32)"
        return self.t[:, o:o + 2 * n].bitcast(F32)

    def f32_3(self, n1, n2):
        return self.f32(n1 * n2).rearrange("p (a b) -> p a b", a=n1)


def host_consts():
    ident = np.eye(128, dtype=np.float32)
    csel = np.zeros((64, 65), np.float32); csel[:, 64] = 1.0
    tril = np.tril(np.ones((128, 128), np.float32))
    triu = np.triu(np.ones((128, 128), np.float32))
    m = np.zeros((128, 896), np.float32)
    jl = np.arange(128)[:, None]
    cc = np.arange(896)[None, :] - 384
    m[:] = np.where(jl >= cc, 0.0, -1e30)
    slopes = np.float32([2.0 ** (-0.5 * (i + 1)) for i in range(16)])
    l3slope = np.repeat(-slopes[:, None], T, axis=1).astype(np.float32)
    l3si = (-slopes[:, None] * np.arange(T, dtype=np.float32)[None, :]).astype(np.float32)
    iota = np.arange(T, dtype=np.float32)[None, :]
    iota128 = (np.arange(T) % 128).astype(np.float32)[None, :]
    itile = ((np.arange(T) // 128) * 128).astype(np.float32)[None, :]
    eyeb = np.zeros((32, 32 * 128), np.float32)
    for e in range(32):
        eyeb[e, e * 128:(e + 1) * 128] = 1.0
    tb32 = ((32.0 - np.arange(32, dtype=np.float32)) * 1e-30)[None, :]
    return {
        "c_ident": ident, "c_csel": csel, "c_tril": tril, "c_triu": triu,
        "c_l3mask": m, "c_l3slope": l3slope, "c_l3si": l3si, "c_iota": iota,
        "c_iota128": iota128, "c_itile": itile, "c_eyeb": eyeb,
        "c_tb32": tb32,
    }


def host_prep(z):
    f = np.float32
    cat = np.concatenate
    o = {}
    o["w_l0_wq"] = np.ascontiguousarray(z["L0_attn_W_Q"], f)
    o["w_l0_wk"] = np.ascontiguousarray(z["L0_attn_W_K"], f)
    o["w_l0_wv"] = np.ascontiguousarray(z["L0_attn_W_V"], f)
    o["w_l0_wg"] = np.ascontiguousarray(z["L0_ffn_Wg"], f)
    o["w_l0_w1"] = np.ascontiguousarray(
        cat([z["L0_ffn_W1"], z["L0_ffn_b1"][:, None, :]], axis=1), f)
    o["w_l0_w2"] = np.ascontiguousarray(z["L0_ffn_W2"], f)
    o["w_l0_b2"] = np.ascontiguousarray(z["L0_ffn_b2"], f)
    o["w_l0_b1c"] = np.ascontiguousarray(z["L0_ffn_b1"].reshape(8, 16, 128).transpose(0, 2, 1), f)
    o["w_l1_anw"] = np.ascontiguousarray(z["L1_attn_attn_norm_w"].reshape(KC, 128).T, f)
    o["w_l1_mnw"] = np.ascontiguousarray(z["L1_attn_mlp_norm_w"].reshape(KC, 128).T, f)
    o["w_l1_onw"] = np.ascontiguousarray(z["L1_attn_attn_out_norm_w"][:, None], f)
    for nm, key in [("wq", "L1_attn_Wq"), ("wk", "L1_attn_Wk"), ("wv", "L1_attn_Wv"),
                    ("wo", "L1_attn_Wo"), ("wgate", "L1_attn_Wgate"),
                    ("wup", "L1_attn_Wup"), ("wdown", "L1_attn_Wdown")]:
        o["w_l1_" + nm] = np.ascontiguousarray(z[key], f)
    o["w_l1_rwt"] = np.ascontiguousarray(z["L1_ffn_router_weight"].T, f)
    o["w_l1_rb"] = np.ascontiguousarray(z["L1_ffn_router_bias"][None, :], f)
    o["w_l1_gw"] = np.ascontiguousarray(z["L1_ffn_gate_w"].transpose(0, 2, 1), f)
    o["w_l1_uw"] = np.ascontiguousarray(z["L1_ffn_up_w"].transpose(0, 2, 1), f)
    o["w_l1_dw"] = np.ascontiguousarray(z["L1_ffn_down_w"].transpose(0, 2, 1), f)
    o["w_l1_sgw"] = np.ascontiguousarray(z["L1_ffn_shared_gate_w"].T, f)
    o["w_l1_suw"] = np.ascontiguousarray(z["L1_ffn_shared_up_w"].T, f)
    o["w_l1_sdw"] = np.ascontiguousarray(z["L1_ffn_shared_down_w"].T, f)
    for nm, wk, bk in [("wq", "L2_attn_Wq", "L2_attn_bq"), ("wk", "L2_attn_Wk", "L2_attn_bk"),
                       ("wv", "L2_attn_Wv", "L2_attn_bv"), ("wo", "L2_attn_Wo", "L2_attn_bo")]:
        o["w_l2_" + nm] = np.ascontiguousarray(cat([z[wk].T, z[bk][None, :]], axis=0), f)
    o["w_l2_bqc"] = np.ascontiguousarray(z["L2_attn_bq"].reshape(16, 64).T, f)
    o["w_l2_bkc"] = np.ascontiguousarray(z["L2_attn_bk"].reshape(16, 64).T, f)
    o["w_l2_rwt"] = np.ascontiguousarray(z["L2_ffn_router_w"].T, f)
    o["w_l2_rb"] = np.ascontiguousarray(z["L2_ffn_router_b"][None, :], f)
    o["w_l2_ew"] = np.ascontiguousarray(
        cat([z["L2_ffn_expert_w"].transpose(0, 2, 1),
             z["L2_ffn_expert_b"][:, None, :]], axis=1), f)
    o["w_l3_wqkv"] = np.ascontiguousarray(z["L3_attn_W_qkv"], f)
    o["w_l3_wout"] = np.ascontiguousarray(z["L3_attn_W_out"], f)
    o["w_l3_gwt"] = np.ascontiguousarray(z["L3_ffn_gate_w"].T, f)
    o["w_l3_w1"] = np.ascontiguousarray(
        cat([z["L3_ffn_w1"], z["L3_ffn_b1"][:, None, :]], axis=1), f)
    o["w_l3_w3"] = np.ascontiguousarray(
        cat([z["L3_ffn_w3"], z["L3_ffn_b3"][:, None, :]], axis=1), f)
    o["w_l3_w2"] = np.ascontiguousarray(z["L3_ffn_w2"], f)
    o["w_l3_b2"] = np.ascontiguousarray(z["L3_ffn_b2"], f)
    o["w_l3_b1c"] = np.ascontiguousarray(z["L3_ffn_b1"].reshape(8, 8, 128).transpose(0, 2, 1), f)
    o["w_l3_b3c"] = np.ascontiguousarray(z["L3_ffn_b3"].reshape(8, 8, 128).transpose(0, 2, 1), f)
    o["w_l3_sb1c"] = np.ascontiguousarray(z["L3_ffn_sb1"].reshape(8, 128).T, f)
    o["w_l3_sb3c"] = np.ascontiguousarray(z["L3_ffn_sb3"].reshape(8, 128).T, f)
    o["w_l3_sb2c"] = np.ascontiguousarray(z["L3_ffn_sb2"].reshape(8, 128).T, f)
    o["w_l3_sw1"] = np.ascontiguousarray(
        cat([z["L3_ffn_sw1"], z["L3_ffn_sb1"][None, :]], axis=0), f)
    o["w_l3_sw3"] = np.ascontiguousarray(
        cat([z["L3_ffn_sw3"], z["L3_ffn_sb3"][None, :]], axis=0), f)
    o["w_l3_sw2"] = np.ascontiguousarray(
        cat([z["L3_ffn_sw2"], z["L3_ffn_sb2"][None, :]], axis=0), f)
    return o


class St:
    pass


TC = [(tc * 512, (tc + 1) * 512) for tc in range(NTC)]


def emit_load_x(c, st, x_in):
    return c.dma('Y', 'cx', lambda y: y.dma_start(
        out=st.xF[:], in_=x_in.rearrange("(c p) t -> p c t", p=128)))


def emit_store_x(c, st, y_out):
    c.phase()
    d = c.dma('Y', 'cout', lambda y: y.dma_start(
        out=y_out.rearrange("(c p) t -> p c t", p=128), in_=st.xF[:]))
    c.wait('Y', d[0], d[1])


def emit_consts(c, st, W):
    st.dOnes = c.op('V', lambda v: v.memset(st.ones[:], 1.0))
    st.dOnesF = c.op('V', lambda v: v.memset(st.onesF[:], 1.0))
    c.dma('G', 'cconst', lambda g: g.dma_start(out=st.identB[:], in_=W["c_ident"][:, :]))
    st.dIdF = c.dma('Y', 'cconsty', lambda y: y.dma_start(out=st.identF[:], in_=W["c_ident"][:, :]))
    c.dma('G', 'cconst', lambda g: g.dma_start(out=st.cselB[:], in_=W["c_csel"][:, :]))
    c.dma('G', 'cconst', lambda g: g.dma_start(out=st.trilB[:], in_=W["c_tril"][:, :]))
    st.dTril = c.dma('G', 'cconst', lambda g: g.dma_start(out=st.triuB[:], in_=W["c_triu"][:, :]))
    st.dIdB = st.dCsel = st.dTril
    d = c.op('V', lambda v: v.memset(st.cst[0:1, 0:1], 1e-5), inc=0)
    d = c.op('V', lambda v: v.memset(st.cst[0:1, 1:2], 1e-6), inc=0)
    d = c.op('V', lambda v: v.memset(st.cst[0:1, 2:3], 1e-20), inc=0)
    st.dCst = c.op('V', lambda v: v.memset(st.cst[0:1, 3:4], 1e-35))


def emit_ln(c, st):
    bk = st.banks
    c.phase()
    st.ar.push()
    xbf = [st.ar.bf(T), st.ar.bf(T)]
    sq = [st.ar.bf(T), st.ar.bf(T)]
    for tc in range(NTC):
        bk[tc].guard()
    tRd = [None, None]
    for kc in range(KC):
        b = kc % 2
        dC = c.op('V', lambda v, kc=kc, b=b: v.tensor_copy(out=xbf[b][:], in_=st.xF[:, kc, :]),
                  deps=[tRd[b], st.dOnes])
        for tc in range(NTC):
            c.op('T', lambda t, kc=kc, tc=tc, b=b: t.matmul(
                bk[tc].t[0:1, :], st.ones[:, 0:1], xbf[b][:, TC[tc][0]:TC[tc][1]],
                start=(kc == 0), stop=(kc == KC - 1)),
                deps=[dC], inc=(1 if tc == NTC - 1 else 0))
        tRd[b] = c.tip('T')
    tSum = c.tip('T')
    dMu = None
    for tc in range(NTC):
        dMu = c.op('S', lambda s, tc=tc: s.activation(
            out=st.mu32[0:1, TC[tc][0]:TC[tc][1]], in_=bk[tc].t[0:1, :],
            func=AF.Copy, scale=1.0 / 1024.0), deps=[tSum])
        bk[tc].consume(dMu)
    dMuB = c.op('V', lambda v: v.tensor_copy(out=st.mu_bf[:, :], in_=st.mu32[:, :]), deps=[dMu])
    dXc = []
    for tc in range(NTC):
        bk[4].guard()
        tB = c.op('T', lambda t, tc=tc: t.matmul(
            bk[4].t[:, :], st.ones[0:1, 0:128], st.mu_bf[0:1, TC[tc][0]:TC[tc][1]],
            start=True, stop=True), deps=[dMuB])
        for kc in range(KC):
            d = c.op('V', lambda v, kc=kc, tc=tc: v.tensor_tensor(
                out=st.xnb[:, kc, TC[tc][0]:TC[tc][1]],
                in0=st.xF[:, kc, TC[tc][0]:TC[tc][1]], in1=bk[4].t[:, :], op=ALU.subtract),
                deps=[tB])
            bk[4].consume(d)
            dXc.append(d)
    for tc in range(NTC):
        bk[tc].guard()
    tRd2 = [None, None]
    for kc in range(KC):
        b = kc % 2
        dSq = c.op('S', lambda s, kc=kc, b=b: s.activation(
            out=sq[b][:], in_=st.xnb[:, kc, :], func=AF.Square),
            deps=[dXc[-1], tRd2[b]])
        for tc in range(NTC):
            c.op('T', lambda t, kc=kc, tc=tc, b=b: t.matmul(
                bk[tc].t[0:1, :], st.ones[:, 0:1], sq[b][:, TC[tc][0]:TC[tc][1]],
                start=(kc == 0), stop=(kc == KC - 1)),
                deps=[dSq], inc=(1 if tc == NTC - 1 else 0))
        tRd2[b] = c.tip('T')
    tVar = c.tip('T')
    dR = None
    for tc in range(NTC):
        dR = c.op('S', lambda s, tc=tc: s.activation(
            out=st.rstd32[0:1, TC[tc][0]:TC[tc][1]], in_=bk[tc].t[0:1, :],
            func=AF.Sqrt, scale=1.0 / 1024.0, bias=st.cst[0:1, 0:1]), deps=[tVar, st.dCst])
        bk[tc].consume(dR)
    dRec = c.op('V', lambda v: v.reciprocal(out=st.rstd32[:, :], in_=st.rstd32[:, :]), deps=[dR])
    dRb = c.op('V', lambda v: v.tensor_copy(out=st.rstd_bf[:, :], in_=st.rstd32[:, :]), deps=[dRec])
    sqDone = c.tip('S')
    for tc in range(NTC):
        bk[4].guard()
        tB = c.op('T', lambda t, tc=tc: t.matmul(
            bk[4].t[:, :], st.ones[0:1, 0:128], st.rstd_bf[0:1, TC[tc][0]:TC[tc][1]],
            start=True, stop=True), deps=[dRb])
        for kc in range(KC):
            d = c.op('V', lambda v, kc=kc, tc=tc: v.tensor_tensor(
                out=st.xnb[:, kc, TC[tc][0]:TC[tc][1]],
                in0=st.xnb[:, kc, TC[tc][0]:TC[tc][1]], in1=bk[4].t[:, :], op=ALU.mult),
                deps=[tB, sqDone])
            bk[4].consume(d)
    st.ar.pop()
    st.xnbReady = c.tip('V')


def emit_rstd_cols(c, st, rcol, dRstd):
    P = st.banks[7]
    last = None
    for tok in range(NTOK):
        P.guard()
        tT = c.op('T', lambda t, tok=tok: t.transpose(
            P.f[0:128, 0:1], st.rstd32[0:1, tok * 128:(tok + 1) * 128], st.identF[0:1, 0:1]),
            deps=[dRstd, st.dIdF])
        last = c.op('V', lambda v, tok=tok: v.tensor_copy(
            out=rcol[:, tok:tok + 1], in_=P.f[:, 0:1]), deps=[tT, last])
        P.consume(last)
    return last


def emit_routing(c, st, Wd, rwt, rb, E, kind, Wt):
    bk = st.banks
    st.ar.push()
    rwsb = st.ar.f32_3(KC, E)
    wsneg = st.ar.f32(E)
    rbb = st.ar.f32(E)
    rcol = st.ar.f32(NTOK)
    lg = st.ar.f32(32)
    sc = st.ar.f32(32)
    t1 = st.ar.f32(32)
    t2 = st.ar.f32(32)
    cols = st.ar.f32(8)
    dRw = c.dma('Y', 'crt', lambda y: y.dma_start(out=rwsb[:], in_=rwt.rearrange("(c p) e -> p c e", p=128)))
    dRbRow = None
    if rb is not None:
        rbrow = st.ar.f32(E)
        dRbRow = c.dma('Y', 'crt', lambda y: y.dma_start(out=rbrow[0:1, 0:E], in_=rb[:, :]))
        if kind == 'l1':
            tbrow = st.ar.f32(E)
            dTb = c.dma('Y', 'crt', lambda y: y.dma_start(out=tbrow[0:1, 0:E], in_=Wd["c_tb32"][:, 0:E]))
            dTb = ('crt', c.semc['crt'])
        dRbRow = ('crt', c.semc['crt'])
    dRw = ('crt', c.semc['crt'])
    dRcol = emit_rstd_cols(c, st, rcol, st.xnbReady)
    bk[6].guard()
    for kc in range(KC):
        c.op('T', lambda t, kc=kc: t.matmul(
            bk[6].t[0:1, 0:E], st.onesF[:, 0:1], rwsb[:, kc, :],
            start=(kc == 0), stop=(kc == KC - 1)),
            deps=[dRw, st.dOnesF], inc=(1 if kc == KC - 1 else 0))
    tWs = c.tip('T')
    dWs = c.op('S', lambda s: s.activation(out=wsneg[0:1, 0:E], in_=bk[6].t[0:1, 0:E],
                                           func=AF.Copy, scale=-1.0), deps=[tWs])
    bk[6].consume(dWs)
    dRbb = None
    if rb is not None:
        if kind == 'l1':
            dRbRow = c.op('V', lambda v: v.tensor_tensor(
                out=rbrow[0:1, 0:E], in0=rbrow[0:1, 0:E], in1=tbrow[0:1, 0:E], op=ALU.add),
                deps=[dRbRow, dTb])
        bk[6].guard()
        tRb = c.op('T', lambda t: t.matmul(bk[6].t[:, 0:E], st.onesF[0:1, 0:128], rbrow[0:1, 0:E],
                                           start=True, stop=True), deps=[dRbRow, dWs])
        dRbb = c.op('V', lambda v: v.tensor_copy(out=rbb[:, 0:E], in_=bk[6].t[:, 0:E]), deps=[tRb])
        bk[6].consume(dRbb)

    B = bk[5]
    for tok in range(NTOK):
        B.guard()
        for kc in range(KC):
            c.op('T', lambda t, kc=kc, tok=tok: t.matmul(
                B.t[:, 0:E], st.xF[:, kc, tok * 128:(tok + 1) * 128], rwsb[:, kc, :],
                start=(kc == 0), stop=False), deps=[dRw], inc=0)
        tL = c.op('T', lambda t, tok=tok: t.matmul(
            B.t[:, 0:E], st.mu32[0:1, tok * 128:(tok + 1) * 128], wsneg[0:1, 0:E],
            start=False, stop=True), deps=[dWs])
        d = c.op('V', lambda v, tok=tok: v.tensor_scalar(
            out=lg[:, 0:E], in0=B.t[:, 0:E], scalar1=rcol[:, tok:tok + 1], scalar2=None,
            op0=ALU.mult), deps=[tL, dRcol])
        B.consume(d)
        Ws = Wt[:, tok, :]
        if kind in ('l0', 'l2', 'l3'):
            if kind == 'l2':
                d = c.op('V', lambda v: v.tensor_tensor(out=lg[:, 0:E], in0=lg[:, 0:E],
                                                        in1=rbb[:, 0:E], op=ALU.add), deps=[d, dRbb])
            d = c.op('V', lambda v: v.tensor_reduce(out=cols[:, 0:1], in_=lg[:, 0:E], axis=AX.X, op=ALU.max), deps=[d])
            d = c.op('V', lambda v: v.tensor_scalar(out=cols[:, 1:2], in0=cols[:, 0:1], scalar1=-1.0, scalar2=None, op0=ALU.mult), deps=[d])
            d = c.op('S', lambda s: s.activation(out=sc[:, 0:E], in_=lg[:, 0:E], func=AF.Exp, bias=cols[:, 1:2]), deps=[d])
            d = c.op('V', lambda v: v.tensor_reduce(out=cols[:, 2:3], in_=sc[:, 0:E], axis=AX.X, op=ALU.max), deps=[d])
            d = c.op('V', lambda v: v.tensor_scalar(out=t1[:, 0:E], in0=sc[:, 0:E], scalar1=cols[:, 2:3], scalar2=None, op0=ALU.is_equal), deps=[d])
            if kind == 'l2':
                d = c.op('V', lambda v: v.tensor_reduce(out=cols[:, 3:4], in_=sc[:, 0:E], axis=AX.X, op=ALU.add), deps=[d])
                d = c.op('V', lambda v: v.reciprocal(out=cols[:, 4:5], in_=cols[:, 3:4]), deps=[d])
                d = c.op('V', lambda v, Ws=Ws: v.tensor_scalar(
                    out=Ws, in0=t1[:, 0:E], scalar1=cols[:, 2:3], scalar2=cols[:, 4:5],
                    op0=ALU.mult, op1=ALU.mult), deps=[d])
            else:
                d = c.op('V', lambda v: v.scalar_tensor_tensor(out=t2[:, 0:E], in0=t1[:, 0:E], scalar=-1e30, in1=sc[:, 0:E], op0=ALU.mult, op1=ALU.add), deps=[d])
                d = c.op('V', lambda v: v.tensor_reduce(out=cols[:, 3:4], in_=t2[:, 0:E], axis=AX.X, op=ALU.max), deps=[d])
                d = c.op('V', lambda v: v.tensor_scalar(out=t2[:, 0:E], in0=t2[:, 0:E], scalar1=cols[:, 3:4], scalar2=None, op0=ALU.is_equal), deps=[d])
                if kind == 'l0':
                    d = c.op('V', lambda v: v.tensor_tensor(out=cols[:, 4:5], in0=cols[:, 2:3], in1=cols[:, 3:4], op=ALU.add), deps=[d])
                else:
                    d = c.op('V', lambda v: v.tensor_reduce(out=cols[:, 4:5], in_=sc[:, 0:E], axis=AX.X, op=ALU.add), deps=[d])
                d = c.op('V', lambda v: v.reciprocal(out=cols[:, 5:6], in_=cols[:, 4:5]), deps=[d])
                d = c.op('V', lambda v: v.tensor_scalar(out=t1[:, 0:E], in0=t1[:, 0:E], scalar1=cols[:, 2:3], scalar2=None, op0=ALU.mult), deps=[d])
                d = c.op('V', lambda v: v.tensor_scalar(out=t2[:, 0:E], in0=t2[:, 0:E], scalar1=cols[:, 3:4], scalar2=None, op0=ALU.mult), deps=[d])
                d = c.op('V', lambda v: v.tensor_tensor(out=t1[:, 0:E], in0=t1[:, 0:E], in1=t2[:, 0:E], op=ALU.add), deps=[d])
                d = c.op('V', lambda v, Ws=Ws: v.tensor_scalar(
                    out=Ws, in0=t1[:, 0:E], scalar1=cols[:, 5:6], scalar2=None, op0=ALU.mult), deps=[d])
        elif kind == 'l1':
            g3 = lg[:, 0:E].rearrange("p (a b) -> p a b", a=4)
            t13 = t1[:, 0:E].rearrange("p (a b) -> p a b", a=4)
            d = c.op('S', lambda s: s.activation(out=sc[:, 0:E], in_=lg[:, 0:E], func=AF.Sigmoid), deps=[d])
            d = c.op('V', lambda v: v.tensor_tensor(out=lg[:, 0:E], in0=sc[:, 0:E], in1=rbb[:, 0:E], op=ALU.add), deps=[d, dRbb])
            d = c.op('V', lambda v: v.tensor_reduce(out=cols[:, 0:4], in_=g3, axis=AX.X, op=ALU.max), deps=[d])
            m1b = cols[:, 0:4].unsqueeze(2).broadcast_to([128, 4, 8])
            d = c.op('V', lambda v, m1b=m1b: v.tensor_tensor(out=t13, in0=g3, in1=m1b, op=ALU.is_equal), deps=[d])
            d = c.op('V', lambda v: v.scalar_tensor_tensor(out=t2[:, 0:E], in0=t1[:, 0:E], scalar=-1e30, in1=lg[:, 0:E], op0=ALU.mult, op1=ALU.add), deps=[d])
            d = c.op('V', lambda v: v.tensor_reduce(out=cols[:, 4:8], in_=t2[:, 0:E].rearrange("p (a b) -> p a b", a=4), axis=AX.X, op=ALU.max), deps=[d])
            d = c.op('V', lambda v: v.tensor_tensor(out=cols[:, 0:4], in0=cols[:, 0:4], in1=cols[:, 4:8], op=ALU.add), deps=[d])
            d = c.op('V', lambda v: v.tensor_reduce(out=cols[:, 4:5], in_=cols[:, 0:4], axis=AX.X, op=ALU.max), deps=[d])
            d = c.op('V', lambda v: v.tensor_scalar(out=cols[:, 5:6], in0=cols[:, 4:5], scalar1=-1e30, scalar2=None, op0=ALU.mult), deps=[d])
            d = c.op('V', lambda v: v.tensor_scalar(out=t2[:, 0:4], in0=cols[:, 0:4], scalar1=cols[:, 4:5], scalar2=None, op0=ALU.is_equal), deps=[d])
            d = c.op('V', lambda v: v.scalar_tensor_tensor(out=t2[:, 0:4], in0=t2[:, 0:4], scalar=-1e30, in1=cols[:, 0:4], op0=ALU.mult, op1=ALU.add), deps=[d])
            d = c.op('V', lambda v: v.tensor_reduce(out=cols[:, 5:6], in_=t2[:, 0:4], axis=AX.X, op=ALU.max), deps=[d])
            d = c.op('V', lambda v: v.tensor_scalar(out=t2[:, 0:4], in0=cols[:, 0:4], scalar1=cols[:, 5:6], scalar2=None, op0=ALU.is_ge), deps=[d])
            gmb = t2[:, 0:4].unsqueeze(2).broadcast_to([128, 4, 8])
            d = c.op('V', lambda v, gmb=gmb: v.tensor_tensor(out=sc[:, 0:E].rearrange("p (a b) -> p a b", a=4), in0=g3, in1=gmb, op=ALU.mult), deps=[d])
            for kk in range(4):
                d = c.op('V', lambda v: v.tensor_reduce(out=cols[:, 6:7], in_=sc[:, 0:E], axis=AX.X, op=ALU.max), deps=[d])
                d = c.op('V', lambda v: v.tensor_scalar(out=t2[:, 0:E], in0=sc[:, 0:E], scalar1=cols[:, 6:7], scalar2=None, op0=ALU.is_equal), deps=[d])
                if kk == 0:
                    d = c.op('V', lambda v: v.tensor_copy(out=t1[:, 0:E], in_=t2[:, 0:E]), deps=[d])
                else:
                    d = c.op('V', lambda v: v.tensor_tensor(out=t1[:, 0:E], in0=t1[:, 0:E], in1=t2[:, 0:E], op=ALU.add), deps=[d])
                if kk < 3:
                    d = c.op('V', lambda v: v.scalar_tensor_tensor(out=sc[:, 0:E], in0=t2[:, 0:E], scalar=-1e30, in1=sc[:, 0:E], op0=ALU.mult, op1=ALU.add), deps=[d])
            d = c.op('V', lambda v: v.tensor_tensor(out=t2[:, 0:E], in0=lg[:, 0:E], in1=rbb[:, 0:E], op=ALU.subtract), deps=[d])
            d = c.op('V', lambda v: v.tensor_tensor(out=t1[:, 0:E], in0=t1[:, 0:E], in1=t2[:, 0:E], op=ALU.mult), deps=[d])
            d = c.op('V', lambda v: v.tensor_reduce(out=cols[:, 6:7], in_=t1[:, 0:E], axis=AX.X, op=ALU.add), deps=[d])
            d = c.op('V', lambda v: v.tensor_scalar(out=cols[:, 6:7], in0=cols[:, 6:7], scalar1=1e-20, scalar2=None, op0=ALU.add), deps=[d])
            d = c.op('V', lambda v: v.reciprocal(out=cols[:, 7:8], in_=cols[:, 6:7]), deps=[d])
            d = c.op('V', lambda v, Ws=Ws: v.tensor_scalar(
                out=Ws, in0=t1[:, 0:E], scalar1=cols[:, 7:8], scalar2=2.5,
                op0=ALU.mult, op1=ALU.mult), deps=[d])
        else:
            raise ValueError(kind)
    st.ar.pop()
    return c.tip('V')


def emit_wt_transpose(c, st, Wt, WeT, E, dWt):
    P = st.banks[7]
    st.ar.push()
    wtb = st.ar.bf(32)
    d = dWt
    last = None
    for tok in range(NTOK):
        d = c.op('V', lambda v, tok=tok: v.tensor_copy(out=wtb[:, 0:E], in_=Wt[:, tok, :]), deps=[dWt, d])
        P.guard()
        tT = c.op('T', lambda t, tok=tok: t.transpose(P.t[0:E, 0:128], wtb[:, 0:E], st.identB[:]),
                  deps=[d, st.dIdB])
        last = c.op('V', lambda v, tok=tok: v.tensor_copy(
            out=WeT[0:E, tok * 128:(tok + 1) * 128], in_=P.t[0:E, 0:128]), deps=[tT, last])
        P.consume(last)
    st.ar.pop()
    return last


def emit_l0_ffn(c, st, Wd):
    E = 8
    bk = st.banks
    c.phase()
    st.ar.push()
    Wt = st.ar.f32_3(NTOK, E)
    dWt = emit_routing(c, st, Wd, Wd["w_l0_wg"], None, E, 'l0', Wt)
    WeT = st.ar.bf(T)
    dWeT = emit_wt_transpose(c, st, Wt, WeT, E, dWt)
    b2sb = st.ar.bf(1024)
    dB2 = c.dma('G', 'cmisc', lambda g: g.dma_start(out=b2sb[0:8, :], in_=Wd["w_l0_b2"][:, :]),
                deps=[dWeT])
    eye = st.ar.bf(1024)
    eyev = eye.rearrange("p (e m) -> p e m", e=8)
    dEye = c.dma('G', 'cmisc', lambda g: g.dma_start(out=eyev[0:8], in_=Wd["c_eyeb"][0:8, 0:1024].rearrange("e (f m) -> e f m", f=8)),
                 deps=[dWeT])
    wb = st.ar.bf(T)
    b1c0 = st.ar.f32_3(8, 16)
    dBc0 = c.dma('Y', 'cl0bc', lambda y: y.dma_start(
        out=b1c0[:], in_=Wd["w_l0_b1c"].rearrange("e p i -> p e i")))
    mst = st.ar.bf3(16, 1024)
    w1s = [st.ar.bf3(KC, 128) for _ in range(2)]
    w1r = [st.ar.bf(128) for _ in range(2)]
    w2s = [st.ar.bf3(16, 128) for _ in range(2)]
    relu_sb = st.ar.bf(512)
    w1t = [None, None]
    w2t = [None, None]
    mprev = None
    for e in range(E):
        dwb = None
        for tc in range(NTC):
            bk[6].guard()
            tB = c.op('T', lambda t, tc=tc, e=e: t.matmul(
                bk[6].t[:, :], eyev[0:8, e, :], WeT[0:8, TC[tc][0]:TC[tc][1]],
                start=True, stop=True), deps=[dWeT, dEye])
            dwb = c.op('V', lambda v, tc=tc: v.tensor_copy(
                out=wb[:, TC[tc][0]:TC[tc][1]], in_=bk[6].t[:, :]), deps=[tB, dwb])
            bk[6].consume(dwb)
        for half in range(2):
            tcs = [2 * half, 2 * half + 1]
            dm = mprev
            for ic in range(16):
                sl = ic % 2
                ld = c.dma('G', f'cw1_{sl}', lambda g, e=e, ic=ic, sl=sl: g.dma_start(
                    out=w1s[sl][:], in_=Wd["w_l0_w1"][e, 0:1024, ic * 128:(ic + 1) * 128]
                    .rearrange("(c p) m -> p c m", p=128)), deps=[w1t[sl]])
                for ti, tc in enumerate(tcs):
                    hb = bk[ti]
                    hb.guard()
                    for kc in range(KC):
                        c.op('T', lambda t, kc=kc, tc=tc, sl=sl, hb=hb: t.matmul(
                            hb.t[:, :], w1s[sl][:, kc, :], st.xnb[:, kc, TC[tc][0]:TC[tc][1]],
                            start=(kc == 0), stop=(kc == KC - 1)),
                            deps=[ld, st.xnbReady], inc=(1 if kc == KC - 1 else 0))
                    tH = c.tip('T')
                    dRelu = c.op('S', lambda s, hb=hb, e=e, ic=ic: s.activation(
                        out=relu_sb[:, :], in_=hb.t[:, :], func=AF.Relu,
                        bias=b1c0[:, e, ic:ic + 1]), deps=[tH, dm, dBc0])
                    hb.consume(dRelu)
                    dm = c.op('V', lambda v, ic=ic, tc=tc, ti=ti: v.tensor_tensor(
                        out=mst[:, ic, ti * 512:(ti + 1) * 512], in0=relu_sb[:, :],
                        in1=wb[:, TC[tc][0]:TC[tc][1]], op=ALU.mult), deps=[dRelu, dwb, mprev])
                w1t[sl] = c.tip('T')
            for dc in range(KC):
                sl = dc % 2
                ld2 = c.dma('G', f'cw2_{sl}', lambda g, e=e, dc=dc, sl=sl: g.dma_start(
                    out=w2s[sl][:], in_=Wd["w_l0_w2"][e, :, dc * 128:(dc + 1) * 128]
                    .rearrange("(c p) m -> p c m", p=128)), deps=[w2t[sl]])
                for ti, tc in enumerate(tcs):
                    ob = bk[2 + ti]
                    ob.guard()
                    for ic in range(16):
                        last = (ic == 15) and (e != 0)
                        c.op('T', lambda t, ic=ic, ti=ti, sl=sl, last=last, ob=ob: t.matmul(
                            ob.t[:, :], w2s[sl][:, ic, :], mst[:, ic, ti * 512:(ti + 1) * 512],
                            start=(ic == 0), stop=last), deps=[ld2, dm], inc=(1 if last else 0))
                    if e == 0:
                        tD = c.op('T', lambda t, dc=dc, tc=tc, ob=ob: t.matmul(
                            ob.t[:, :], b2sb[0:8, dc * 128:(dc + 1) * 128], WeT[0:8, TC[tc][0]:TC[tc][1]],
                            start=False, stop=True), deps=[dB2, dWeT])
                    else:
                        tD = c.tip('T')
                    dAdd = c.op('V', lambda v, dc=dc, tc=tc, ob=ob: v.tensor_tensor(
                        out=st.xF[:, dc, TC[tc][0]:TC[tc][1]], in0=st.xF[:, dc, TC[tc][0]:TC[tc][1]],
                        in1=ob.t[:, :], op=ALU.add), deps=[tD])
                    ob.consume(dAdd)
                w2t[sl] = c.tip('T')
            mprev = c.tip('T')
    st.ar.pop()


def emit_l0_attn(c, st, Wd):
    bk = st.banks
    c.phase()
    st.ar.push()
    Vt = st.ar.bf3(NTOK, 1040)
    st.ar.push()
    wv = st.ar.bf(KC * 1024).rearrange("p (c h m) -> p c h m", c=KC, h=16)
    dWv = None
    for h in range(16):
        dWv = c.dma('G', 'cwv', lambda g, h=h: g.dma_start(
            out=wv[:, :, h, :], in_=Wd["w_l0_wv"][h].rearrange("(c p) m -> p c m", p=128)))
    dVm = c.op('V', lambda v: v.memset(Vt[:, :, :], 1.0))
    tVdone = None
    for jt in range(NTOK):
        for vc in range(2):
            b = bk[vc]
            b.guard()
            for kc in range(KC):
                c.op('T', lambda t, jt=jt, vc=vc, kc=kc, b=b: t.matmul(
                    b.t[:, :], st.xnb[:, kc, jt * 128:(jt + 1) * 128], wv[:, kc, vc * 8:(vc + 1) * 8, :],
                    start=(kc == 0), stop=(kc == KC - 1)),
                    deps=[dWv, st.xnbReady], inc=(1 if kc == KC - 1 else 0))
            tV = c.tip('T')
            dcp = None
            for h8 in range(8):
                hh = vc * 8 + h8
                dcp = c.op('V', lambda v, jt=jt, hh=hh, h8=h8, b=b: v.tensor_copy(
                    out=Vt[:, jt, hh * 65:hh * 65 + 64], in_=b.t[:, h8 * 64:(h8 + 1) * 64]),
                    deps=[tV, dVm, dcp])
                b.consume(dcp)
    tVdone = c.tip('T')
    st.ar.pop()
    st.ar.push()
    qk = st.ar.bf(KC * 256).rearrange("p (c w m) -> p c w m", c=KC, w=4)
    Qs = [st.ar.bf(T), st.ar.bf(T)]
    Ks = [st.ar.bf(T), st.ar.bf(T)]
    Ptd = [st.ar.bf3(8, 1024), st.ar.bf3(8, 1024)]
    qsq = st.ar.bf(T)
    rowS = st.ar.f32(T)
    k2m = st.ar.f32(8)
    Osb = st.ar.f32(512)
    OsbB = st.ar.bf(512)
    recb = OsbB
    dWqk = None
    tPrevScores = None
    prevB5 = None
    prevA = None
    prevSh = None
    dBs = [None, None]
    for hg in range(8):
        dWqkL = []
        for w, (wkey, hh2) in enumerate([("w_l0_wq", 0), ("w_l0_wq", 1), ("w_l0_wk", 0), ("w_l0_wk", 1)]):
            dWqkL.append(c.dma('G', f'cqk_{w}', lambda g, hg=hg, w=w, wkey=wkey, hh2=hh2: g.dma_start(
                out=qk[:, :, w, :], in_=Wd[wkey][hg * 2 + hh2]
                .rearrange("(c p) m -> p c m", p=128)), deps=[tPrevScores, tVdone]))
        for hh in range(2):
            h = hg * 2 + hh
            dQe = dKe = None
            for tc in range(NTC):
                b = bk[tc % 2]
                b.guard()
                for kc in range(KC):
                    c.op('T', lambda t, kc=kc, tc=tc, hh=hh, b=b: t.matmul(
                        b.t[0:64, :], qk[:, kc, hh, :], st.xnb[:, kc, TC[tc][0]:TC[tc][1]],
                        start=(kc == 0), stop=(kc == KC - 1)),
                        deps=[dWqkL[hh], st.xnbReady], inc=(1 if kc == KC - 1 else 0))
                tQ = c.tip('T')
                dQe = c.op('S', lambda s, tc=tc, hh=hh, b=b: s.activation(
                    out=Qs[hh][0:64, TC[tc][0]:TC[tc][1]], in_=b.t[0:64, :],
                    func=AF.Copy, scale=0.125), deps=[tQ, dQe])
                b.consume(dQe)
                b2_ = bk[2 + tc % 2]
                b2_.guard()
                for kc in range(KC):
                    c.op('T', lambda t, kc=kc, tc=tc, hh=hh, b2_=b2_: t.matmul(
                        b2_.t[0:64, :], qk[:, kc, 2 + hh, :],
                        st.xnb[:, kc, TC[tc][0]:TC[tc][1]],
                        start=(kc == 0), stop=(kc == KC - 1)),
                        deps=[dWqkL[2 + hh]], inc=(1 if kc == KC - 1 else 0))
                tK = c.tip('T')
                dKe = c.op('S', lambda s, tc=tc, hh=hh, b2_=b2_: s.copy(
                    out=Ks[hh][0:64, TC[tc][0]:TC[tc][1]], in_=b2_.t[0:64, :]), deps=[tK, dKe])
                b2_.consume(dKe)
            dKone = c.op('V', lambda v, hh=hh: v.memset(Ks[hh][64:65, :], 1.0), deps=[tVdone])
            dSqQ = c.op('S', lambda s, hh=hh: s.activation(
                out=qsq[0:64, :], in_=Qs[hh][0:64, :], func=AF.Square), deps=[dQe])
            dRowQ = None
            for tc in range(NTC):
                b = bk[4]
                b.guard()
                tR = c.op('T', lambda t, tc=tc, b=b: t.matmul(
                    b.t[0:65, :], st.cselB[:, :], qsq[0:64, TC[tc][0]:TC[tc][1]],
                    start=True, stop=True), deps=[dSqQ, st.dCsel])
                dRowQ = c.op('S', lambda s, tc=tc, b=b: s.copy(
                    out=rowS[64:65, TC[tc][0]:TC[tc][1]], in_=b.t[64:65, :]),
                    deps=[tR, dRowQ, prevA, dBs[1 - hh]])
                b.consume(dRowQ)
            dSqK = c.op('S', lambda s, hh=hh: s.activation(
                out=qsq[0:64, :], in_=Ks[hh][0:64, :], func=AF.Square), deps=[dKe, dRowQ])
            dKm = None
            for tc in range(NTC):
                b = bk[4]
                b.guard()
                tR = c.op('T', lambda t, tc=tc, b=b: t.matmul(
                    b.t[0:65, :], st.cselB[:, :], qsq[0:64, TC[tc][0]:TC[tc][1]],
                    start=True, stop=True), deps=[dSqK])
                dKm = c.op('V', lambda v, tc=tc, b=b: v.tensor_reduce(
                    out=k2m[64:65, tc:tc + 1], in_=b.t[64:65, :], axis=AX.X, op=ALU.max),
                    deps=[tR, dKm])
                b.consume(dKm)
            dKm = c.op('V', lambda v: v.tensor_reduce(
                out=k2m[64:65, 4:5], in_=k2m[64:65, 0:4], axis=AX.X, op=ALU.max), deps=[dKm])
            dB = c.op('V', lambda v: v.tensor_scalar(
                out=rowS[64:65, :], in0=rowS[64:65, :], scalar1=k2m[64:65, 4:5], scalar2=None,
                op0=ALU.mult), deps=[dKm, dRowQ])
            dB = c.op('S', lambda s: s.activation(
                out=rowS[64:65, :], in_=rowS[64:65, :], func=AF.Sqrt), deps=[dB])
            dB = c.op('V', lambda v, hh=hh: v.tensor_scalar(
                out=Qs[hh][64:65, :], in0=rowS[64:65, :], scalar1=-1.0, scalar2=None,
                op0=ALU.mult), deps=[dB])
            dBs[hh] = dB
        units = [(hh, ib) for ib in range(NTC) for hh in range(2)]
        NU = len(units)
        dExpU = [None] * NU
        tPVU = [None] * NU
        dE = None
        for step in range(NU + 1):
            if step < NU:
                hh, ib = units[step]
                par = step % 2
                for jp in range(8):
                    P = st.pairs[jp % 2]
                    P.guard()
                    tS = None
                    for hf in range(2):
                        jt = 2 * jp + hf
                        tS = c.op('T', lambda t, jt=jt, ib=ib, hh=hh, P=P, hf=hf: t.matmul(
                            P.half[hf].t[:, :], Ks[hh][0:65, jt * 128:(jt + 1) * 128],
                            Qs[hh][0:65, TC[ib][0]:TC[ib][1]],
                            start=True, stop=True),
                            deps=[dBs[hh], dKone, tPrevScores,
                                  tPVU[step - 2] if step >= 2 else None])
                    dE = c.op('S', lambda s, jp=jp, par=par, P=P: s.activation(
                        out=Ptd[par][:, jp, :], in_=P.t[:, :], func=AF.Exp),
                        deps=[tS, dE, tPVU[step - 2] if step >= 2 else None])
                    P.consume(dE)
                dExpU[step] = dE
            if step > 0:
                hh2, ib2 = units[step - 1]
                h = hg * 2 + hh2
                par2 = (step - 1) % 2
                b = bk[4 + (step - 1) % 2]
                b.guard()
                for jt in range(NTOK):
                    c.op('T', lambda t, jt=jt, h=h, b=b, par2=par2: t.matmul(
                        b.t[0:65, :], Vt[:, jt, h * 65:(h + 1) * 65],
                        Ptd[par2][:, jt // 2, (jt % 2) * 512:(jt % 2) * 512 + 512],
                        start=(jt == 0), stop=(jt == NTOK - 1)),
                        deps=[dExpU[step - 1]], inc=(1 if jt == NTOK - 1 else 0))
                tO = c.tip('T')
                tPVU[step - 1] = tO
                tPrevScores = tO
                dD = c.op('V', lambda v, b=b: v.tensor_scalar(
                    out=rowS[64:65, 0:512], in0=b.t[64:65, :], scalar1=1e-35, scalar2=None,
                    op0=ALU.add), deps=[tO, dBs[hh2]])
                b.consume(dD)
                dD = c.op('V', lambda v: v.reciprocal(
                    out=rowS[64:65, 512:1024], in_=rowS[64:65, 0:512]), deps=[dD])
                dD = c.op('V', lambda v: v.tensor_copy(
                    out=recb[64:65, :], in_=rowS[64:65, 512:1024]), deps=[dD, prevB5])
                p0 = (h % 2) * 64
                b6 = bk[6]
                b6.guard()
                tB5 = c.op('T', lambda t, p0=p0, b6=b6: t.matmul(
                    b6.t[p0:p0 + 64, :], st.ones[64:65, 0:64], recb[64:65, 0:512],
                    start=True, stop=True), deps=[dD, st.dOnes])
                if p0 == 0:
                    dO = c.op('S', lambda s, b=b: s.copy(out=Osb[0:64, :], in_=b.t[0:64, :]),
                              deps=[tO, prevA, dD])
                    b.consume(dO)
                else:
                    dOb = c.op('S', lambda s, b=b: s.copy(out=OsbB[0:64, :], in_=b.t[0:64, :]),
                               deps=[tO, prevA, prevSh, dD])
                    b.consume(dOb)
                    b7 = bk[7]
                    b7.guard()
                    tSh = c.op('T', lambda t, b7=b7: t.matmul(
                        b7.f[64:128, 0:512], st.identB[0:64, 0:64], OsbB[0:64, :],
                        start=True, stop=True), deps=[dOb, st.dIdB])
                    prevSh = tSh
                    dO = c.op('S', lambda s, b7=b7: s.copy(out=Osb[64:128, :], in_=b7.f[64:128, 0:512]),
                              deps=[tSh, prevA])
                    b7.consume(dO)
                dN = c.op('V', lambda v, p0=p0, b6=b6: v.tensor_tensor(
                    out=Osb[p0:p0 + 64, :], in0=Osb[p0:p0 + 64, :], in1=b6.t[p0:p0 + 64, :],
                    op=ALU.mult), deps=[tB5, dO])
                b6.consume(dN)
                dA = c.op('V', lambda v, h=h, ib2=ib2, p0=p0: v.tensor_tensor(
                    out=st.xF[p0:p0 + 64, h // 2, TC[ib2][0]:TC[ib2][1]],
                    in0=st.xF[p0:p0 + 64, h // 2, TC[ib2][0]:TC[ib2][1]],
                    in1=Osb[p0:p0 + 64, :], op=ALU.add), deps=[dN])
                prevB5 = tB5
                prevA = dA
    st.ar.pop()
    st.ar.pop()



def mk_glu_rings(c, st, inter, tag, with_up=True):
    r = {}
    r['tag'] = tag
    r['w1s'] = [st.ar.bf3(KC, 128) for _ in range(2)]
    r['w1r'] = [st.ar.bf(128) for _ in range(2)]
    r['w1t'] = [None, None]
    if with_up:
        r['w3s'] = [st.ar.bf3(KC, 128) for _ in range(2)]
        r['w3r'] = [st.ar.bf(128) for _ in range(2)]
        r['w3t'] = [None, None]
    r['w2s'] = [st.ar.bf3(inter // 128, 128) for _ in range(2)]
    r['w2t'] = [None, None]
    r['gsb'] = st.ar.bf(512)
    r['usb'] = st.ar.bf(512)
    r['mst'] = None
    r['mprev'] = None
    return r


def emit_glu(c, st, r, xsrc, xdep, w1ap, w3ap, w2ap, inter, tparts, wbv=None,
             dwb=None, extra_down=None, first_down_group=True,
             b1c=None, b3c=None, b2c=None, dBc=None):
    bk = st.banks
    nic = inter // 128
    tw = T // tparts
    ncc = tw // 512
    if r['mst'] is None:
        r['mst'] = st.ar.bf3(nic, tw)
    mst = r['mst']
    tag = r['tag']
    w1b = w1ap.shape[0] > 1024 and b1c is None
    w3b = w3ap is not None and w3ap.shape[0] > 1024 and b3c is None
    w2b = w2ap.shape[0] > inter and b2c is None
    mprev = r['mprev']
    for part in range(tparts):
        col0 = part * tw
        dm = mprev
        for ic in range(nic):
            sl = ic % 2
            ld1 = c.dma('G', f'c{tag}1_{sl}', lambda g, ic=ic, sl=sl: g.dma_start(
                out=r['w1s'][sl][:], in_=w1ap[0:1024, ic * 128:(ic + 1) * 128]
                .rearrange("(c p) m -> p c m", p=128)), deps=[r['w1t'][sl]])
            if w1b:
                ld1r = c.dma('G', f'c{tag}1r_{sl}', lambda g, ic=ic, sl=sl: g.dma_start(
                    out=r['w1r'][sl][0:1, :], in_=w1ap[1024:1025, ic * 128:(ic + 1) * 128]),
                    deps=[r['w1t'][sl]])
            if w3ap is not None:
                ld3 = c.dma('G', f'c{tag}3_{sl}', lambda g, ic=ic, sl=sl: g.dma_start(
                    out=r['w3s'][sl][:], in_=w3ap[0:1024, ic * 128:(ic + 1) * 128]
                    .rearrange("(c p) m -> p c m", p=128)), deps=[r['w3t'][sl]])
                if w3b:
                    ld3r = c.dma('G', f'c{tag}3r_{sl}', lambda g, ic=ic, sl=sl: g.dma_start(
                        out=r['w3r'][sl][0:1, :], in_=w3ap[1024:1025, ic * 128:(ic + 1) * 128]),
                        deps=[r['w3t'][sl]])
            for cc in range(ncc):
                lo = col0 + cc * 512
                ga = bk[cc % 2]
                ga.guard()
                for kc in range(KC):
                    c.op('T', lambda t, kc=kc, lo=lo, sl=sl, ga=ga: t.matmul(
                        ga.t[:, :], r['w1s'][sl][:, kc, :], xsrc[:, kc, lo:lo + 512],
                        start=(kc == 0), stop=(kc == KC - 1 and not w1b)),
                        deps=[ld1, xdep], inc=(0 if w1b else (1 if kc == KC - 1 else 0)))
                if w1b:
                    c.op('T', lambda t, sl=sl, ga=ga: t.matmul(
                        ga.t[:, :], r['w1r'][sl][0:1, :], st.ones[0:1, 0:512],
                        start=False, stop=True), deps=[ld1r, st.dOnes])
                tG = c.tip('T')
                if w3ap is not None:
                    ua = bk[2 + cc % 2]
                    ua.guard()
                    for kc in range(KC):
                        c.op('T', lambda t, kc=kc, lo=lo, sl=sl, ua=ua: t.matmul(
                            ua.t[:, :], r['w3s'][sl][:, kc, :], xsrc[:, kc, lo:lo + 512],
                            start=(kc == 0), stop=(kc == KC - 1 and not w3b)),
                            deps=[ld3, xdep], inc=(0 if w3b else (1 if kc == KC - 1 else 0)))
                    if w3b:
                        c.op('T', lambda t, sl=sl, ua=ua: t.matmul(
                            ua.t[:, :], r['w3r'][sl][0:1, :], st.ones[0:1, 0:512],
                            start=False, stop=True), deps=[ld3r, st.dOnes])
                    tU = c.tip('T')
                    if b1c is not None:
                        dSi = c.op('S', lambda s, ga=ga, ic=ic: s.activation(
                            out=r['gsb'][:, :], in_=ga.t[:, :], func=AF.Silu,
                            bias=b1c[:, ic:ic + 1]), deps=[tG, dm, dBc])
                    else:
                        dSi = c.op('S', lambda s, ga=ga: s.activation(
                            out=r['gsb'][:, :], in_=ga.t[:, :], func=AF.Silu), deps=[tG, dm])
                    ga.consume(dSi)
                    if wbv is not None:
                        if b3c is not None:
                            dT = c.op('V', lambda v, ua=ua, ic=ic: v.scalar_tensor_tensor(
                                out=r['usb'][:, :], in0=ua.t[:, :], scalar=b3c[:, ic:ic + 1],
                                in1=r['gsb'][:, :], op0=ALU.add, op1=ALU.mult),
                                deps=[dSi, tU, mprev, dBc])
                        else:
                            dT = c.op('V', lambda v, ua=ua: v.tensor_tensor(
                                out=r['usb'][:, :], in0=r['gsb'][:, :], in1=ua.t[:, :],
                                op=ALU.mult), deps=[dSi, tU, mprev])
                        ua.consume(dT)
                        dm = c.op('V', lambda v, ic=ic, cc=cc, lo=lo: v.tensor_tensor(
                            out=mst[:, ic, cc * 512:(cc + 1) * 512], in0=r['usb'][:, :],
                            in1=wbv[:, lo:lo + 512], op=ALU.mult), deps=[dT, dwb])
                    else:
                        if b3c is not None:
                            dm = c.op('V', lambda v, ic=ic, cc=cc, ua=ua: v.scalar_tensor_tensor(
                                out=mst[:, ic, cc * 512:(cc + 1) * 512], in0=ua.t[:, :],
                                scalar=b3c[:, ic:ic + 1], in1=r['gsb'][:, :],
                                op0=ALU.add, op1=ALU.mult), deps=[dSi, tU, mprev, dBc])
                        else:
                            dm = c.op('V', lambda v, ic=ic, cc=cc, ua=ua: v.tensor_tensor(
                                out=mst[:, ic, cc * 512:(cc + 1) * 512], in0=r['gsb'][:, :],
                                in1=ua.t[:, :], op=ALU.mult), deps=[dSi, tU, mprev])
                        ua.consume(dm)
                else:
                    dSi = c.op('S', lambda s, ga=ga: s.activation(
                        out=r['gsb'][:, :], in_=ga.t[:, :], func=AF.Relu), deps=[tG, dm])
                    ga.consume(dSi)
                    dm = c.op('V', lambda v, ic=ic, cc=cc: v.tensor_tensor(
                        out=mst[:, ic, cc * 512:(cc + 1) * 512], in0=r['gsb'][:, :],
                        in1=wbv[:, col0 + cc * 512:col0 + (cc + 1) * 512], op=ALU.mult),
                        deps=[dSi, dwb, mprev])
            r['w1t'][sl] = c.tip('T')
            if w3ap is not None:
                r['w3t'][sl] = c.tip('T')
        for dc in range(KC):
            sl = dc % 2
            ld2 = c.dma('G', f'c{tag}2_{sl}', lambda g, dc=dc, sl=sl: g.dma_start(
                out=r['w2s'][sl][:], in_=w2ap[0:inter, dc * 128:(dc + 1) * 128]
                .rearrange("(c p) m -> p c m", p=128)), deps=[r['w2t'][sl]])
            if w2b:
                ld2r = c.dma('G', f'c{tag}2r_{sl}', lambda g, dc=dc, sl=sl: g.dma_start(
                    out=r['w1r'][sl][0:1, :], in_=w2ap[inter:inter + 1, dc * 128:(dc + 1) * 128]),
                    deps=[r['w1t'][sl], r['w2t'][sl]])
            for cc in range(ncc):
                lo = col0 + cc * 512
                tc = lo // 512
                ob = bk[4 + cc % 2]
                ob.guard()
                needs_tail = w2b or (extra_down is not None)
                for ic in range(nic):
                    last = (ic == nic - 1) and not needs_tail
                    c.op('T', lambda t, ic=ic, cc=cc, sl=sl, ob=ob, last=last: t.matmul(
                        ob.t[:, :], r['w2s'][sl][:, ic, :], mst[:, ic, cc * 512:(cc + 1) * 512],
                        start=(ic == 0), stop=last), deps=[ld2, dm],
                        inc=(1 if last else 0))
                if w2b:
                    c.op('T', lambda t, sl=sl, ob=ob: t.matmul(
                        ob.t[:, :], r['w1r'][sl][0:1, :], st.ones[0:1, 0:512],
                        start=False, stop=(extra_down is None)),
                        deps=[ld2r, st.dOnes], inc=(1 if extra_down is None else 0))
                if extra_down is not None:
                    extra_down(ob, dc, tc)
                tD = c.tip('T')
                if b2c is not None:
                    dAdd = c.op('V', lambda v, dc=dc, tc=tc, ob=ob: v.scalar_tensor_tensor(
                        out=st.xF[:, dc, TC[tc][0]:TC[tc][1]], in0=ob.t[:, :],
                        scalar=b2c[:, dc:dc + 1], in1=st.xF[:, dc, TC[tc][0]:TC[tc][1]],
                        op0=ALU.add, op1=ALU.add), deps=[tD, dBc])
                else:
                    dAdd = c.op('V', lambda v, dc=dc, tc=tc, ob=ob: v.tensor_tensor(
                        out=st.xF[:, dc, TC[tc][0]:TC[tc][1]], in0=st.xF[:, dc, TC[tc][0]:TC[tc][1]],
                        in1=ob.t[:, :], op=ALU.add), deps=[tD])
                ob.consume(dAdd)
            r['w2t'][sl] = c.tip('T')
            if w2b:
                r['w1t'][sl] = c.tip('T')
        mprev = c.tip('T')
    r['mprev'] = mprev


def emit_l3_ffn(c, st, Wd):
    E = 8
    bk = st.banks
    c.phase()
    st.ar.push()
    Wt = st.ar.f32_3(NTOK, E)
    dWt = emit_routing(c, st, Wd, Wd["w_l3_gwt"], None, E, 'l3', Wt)
    WeT = st.ar.bf(T)
    dWeT = emit_wt_transpose(c, st, Wt, WeT, E, dWt)
    b2sb = st.ar.bf(1024)
    dB2 = c.dma('G', 'cmisc', lambda g: g.dma_start(out=b2sb[0:8, :], in_=Wd["w_l3_b2"][:, :]),
                deps=[dWeT])
    eye = st.ar.bf(1024)
    eyev = eye.rearrange("p (e m) -> p e m", e=8)
    dEye = c.dma('G', 'cmisc', lambda g: g.dma_start(
        out=eyev[0:8], in_=Wd["c_eyeb"][0:8, 0:1024].rearrange("e (f m) -> e f m", f=8)),
        deps=[dWeT])
    wb = st.ar.bf(T)
    b1c = st.ar.f32_3(8, 8)
    b3c = st.ar.f32_3(8, 8)
    sbc = st.ar.f32(24)
    dBc = c.dma('Y', 'cl3bc', lambda y: y.dma_start(
        out=b1c[:], in_=Wd["w_l3_b1c"].rearrange("e p i -> p e i")))
    dBc = c.dma('Y', 'cl3bc', lambda y: y.dma_start(
        out=b3c[:], in_=Wd["w_l3_b3c"].rearrange("e p i -> p e i")))
    dBc = c.dma('Y', 'cl3bc', lambda y: y.dma_start(out=sbc[:, 0:8], in_=Wd["w_l3_sb1c"][:, :]))
    dBc = c.dma('Y', 'cl3bc', lambda y: y.dma_start(out=sbc[:, 8:16], in_=Wd["w_l3_sb3c"][:, :]))
    dBc = c.dma('Y', 'cl3bc', lambda y: y.dma_start(out=sbc[:, 16:24], in_=Wd["w_l3_sb2c"][:, :]))
    dBc = ('cl3bc', c.semc['cl3bc'])
    rings = mk_glu_rings(c, st, 1024, 'l3f')
    for e in range(E):
        dwb = None
        for tc in range(NTC):
            bk[6].guard()
            tB = c.op('T', lambda t, tc=tc, e=e: t.matmul(
                bk[6].t[:, :], eyev[0:8, e, :], WeT[0:8, TC[tc][0]:TC[tc][1]],
                start=True, stop=True), deps=[dWeT, dEye])
            dwb = c.op('V', lambda v, tc=tc: v.tensor_copy(
                out=wb[:, TC[tc][0]:TC[tc][1]], in_=bk[6].t[:, :]), deps=[tB, dwb])
            bk[6].consume(dwb)
        if e == 0:
            def extra_down(ob, dc, tc):
                c.op('T', lambda t, dc=dc, tc=tc, ob=ob: t.matmul(
                    ob.t[:, :], b2sb[0:8, dc * 128:(dc + 1) * 128], WeT[0:8, TC[tc][0]:TC[tc][1]],
                    start=False, stop=True), deps=[dB2, dWeT])
        else:
            extra_down = None
        emit_glu(c, st, rings, st.xnb, st.xnbReady, Wd["w_l3_w1"][e], Wd["w_l3_w3"][e],
                 Wd["w_l3_w2"][e], 1024, 2, wbv=wb, dwb=dwb, extra_down=extra_down,
                 b1c=b1c[:, e, :], b3c=b3c[:, e, :], dBc=dBc)
    emit_glu(c, st, rings, st.xnb, st.xnbReady, Wd["w_l3_sw1"], Wd["w_l3_sw3"],
             Wd["w_l3_sw2"], 1024, 2, wbv=None, dwb=None, extra_down=None,
             b1c=sbc[:, 0:8], b3c=sbc[:, 8:16], b2c=sbc[:, 16:24], dBc=dBc)
    st.ar.pop()



L3F = {"mask": True, "wo": True, "rows": 67, "shift": True}


def emit_l3_attn(c, st, Wd):
    bk = st.banks
    c.phase()
    st.ar.push()
    Vt = st.ar.bf3(NTOK, 1040)
    st.ar.push()
    wv = st.ar.bf(KC * 1024).rearrange("p (c h m) -> p c h m", c=KC, h=16)
    dWv = None
    for h in range(16):
        dWv = c.dma('G', 'cwv', lambda g, h=h: g.dma_start(
            out=wv[:, :, h, :], in_=Wd["w_l3_wqkv"][:, 2048 + h * 64:2048 + (h + 1) * 64]
            .rearrange("(c p) m -> p c m", p=128)))
    dVm = c.op('V', lambda v: v.memset(Vt[:, :, :], 1.0))
    for jt in range(NTOK):
        for vc in range(2):
            b = bk[vc]
            b.guard()
            for kc in range(KC):
                c.op('T', lambda t, jt=jt, vc=vc, kc=kc, b=b: t.matmul(
                    b.t[:, :], st.xnb[:, kc, jt * 128:(jt + 1) * 128], wv[:, kc, vc * 8:(vc + 1) * 8, :],
                    start=(kc == 0), stop=(kc == KC - 1)),
                    deps=[dWv, st.xnbReady], inc=(1 if kc == KC - 1 else 0))
            tV = c.tip('T')
            dcp = None
            for h8 in range(8):
                hh = vc * 8 + h8
                dcp = c.op('V', lambda v, jt=jt, hh=hh, h8=h8, b=b: v.tensor_copy(
                    out=Vt[:, jt, hh * 65:hh * 65 + 64], in_=b.t[:, h8 * 64:(h8 + 1) * 64]),
                    deps=[tV, dVm, dcp])
                b.consume(dcp)
    tVdone = c.tip('T')
    st.ar.pop()
    st.ar.push()
    qkraw = st.ar.bf(KC * 256)
    qk = qkraw.rearrange("p (c w m) -> p c w m", c=KC, w=4)
    wos = qkraw[:, 0:1024]
    Qs = [st.ar.bf(T), st.ar.bf(T)]
    Ks = [st.ar.bf(T), st.ar.bf(T)]
    Ptd = [st.ar.bf3(8, 1024), st.ar.bf3(8, 1024)]
    qsq = st.ar.bf(T)
    siRow = qsq
    rowD = st.ar.f32(512)
    k2m = st.ar.f32(8)
    Osb = rowD.bitcast(BF16)[:, 0:512]
    OsbB = st.ar.bf(512)
    recb = OsbB
    l3m = st.ar.bf(896)
    OnG = st.ar.bf(T)
    dM = c.dma('G', 'cmisc2', lambda g: g.dma_start(out=l3m[:, :], in_=Wd["c_l3mask"][:, :]),
               deps=[tVdone])
    tPrevScores = None
    prevB5 = None
    prevA = None
    prevSh = None
    dOn = None
    tWoRead = None
    dBs = [None, None]
    dRows = [None, None]
    for hg in range(8):
        dWqkL = []
        for w in range(4):
            col0 = [0, 64, 1024, 1024 + 64][w] + hg * 128
            dWqkL.append(c.dma('G', f'cqk_{w}', lambda g, col0=col0, w=w: g.dma_start(
                out=qk[:, :, w, :], in_=Wd["w_l3_wqkv"][:, col0:col0 + 64]
                .rearrange("(c p) m -> p c m", p=128)), deps=[tPrevScores, tVdone, tWoRead]))
        for hh in range(2):
            h = hg * 2 + hh
            dQe = dKe = None
            for tc in range(NTC):
                b = bk[tc % 2]
                b.guard()
                for kc in range(KC):
                    c.op('T', lambda t, kc=kc, tc=tc, hh=hh, b=b: t.matmul(
                        b.t[0:64, :], qk[:, kc, hh, :], st.xnb[:, kc, TC[tc][0]:TC[tc][1]],
                        start=(kc == 0), stop=(kc == KC - 1)),
                        deps=[dWqkL[hh], st.xnbReady], inc=(1 if kc == KC - 1 else 0))
                tQ = c.tip('T')
                dQe = c.op('S', lambda s, tc=tc, hh=hh, b=b: s.activation(
                    out=Qs[hh][0:64, TC[tc][0]:TC[tc][1]], in_=b.t[0:64, :],
                    func=AF.Copy, scale=0.125), deps=[tQ, dQe])
                b.consume(dQe)
                b2_ = bk[2 + tc % 2]
                b2_.guard()
                for kc in range(KC):
                    c.op('T', lambda t, kc=kc, tc=tc, hh=hh, b2_=b2_: t.matmul(
                        b2_.t[0:64, :], qk[:, kc, 2 + hh, :],
                        st.xnb[:, kc, TC[tc][0]:TC[tc][1]],
                        start=(kc == 0), stop=(kc == KC - 1)),
                        deps=[dWqkL[2 + hh]], inc=(1 if kc == KC - 1 else 0))
                tK = c.tip('T')
                dKe = c.op('S', lambda s, tc=tc, hh=hh, b2_=b2_: s.copy(
                    out=Ks[hh][0:64, TC[tc][0]:TC[tc][1]], in_=b2_.t[0:64, :]), deps=[tK, dKe])
                b2_.consume(dKe)
            dKone = c.op('V', lambda v, hh=hh: v.memset(Ks[hh][64:65, :], 1.0), deps=[tVdone])
            dIot = c.dma('G', f'crow0_{hh}', lambda g, hh=hh: g.dma_start(
                out=Ks[hh][65:66, :], in_=Wd["c_iota128"][:, :]), deps=[tPrevScores])
            dIot = c.dma('G', f'crow3_{hh}', lambda g, hh=hh: g.dma_start(
                out=Ks[hh][66:67, :], in_=Wd["c_itile"][:, :]), deps=[tPrevScores, dIot])
            dSlp = c.dma('G', f'crow1_{hh}', lambda g, hh=hh, h=h: g.dma_start(
                out=Qs[hh][65:66, :], in_=Wd["c_l3slope"][h:h + 1, :]), deps=[tPrevScores])
            dSlp = c.dma('G', f'crow4_{hh}', lambda g, hh=hh, h=h: g.dma_start(
                out=Qs[hh][66:67, :], in_=Wd["c_l3slope"][h:h + 1, :]), deps=[tPrevScores, dSlp])
            dSi = c.dma('G', f'crow2_{hh}', lambda g, h=h: g.dma_start(
                out=siRow[64:65, :], in_=Wd["c_l3si"][h:h + 1, :]),
                deps=[tPrevScores, prevA, dBs[1 - hh]])
            dSqQ = c.op('S', lambda s, hh=hh: s.activation(
                out=qsq[0:64, :], in_=Qs[hh][0:64, :], func=AF.Square), deps=[dQe])
            dRowQ = None
            for tc in range(NTC):
                b = bk[4]
                b.guard()
                tR = c.op('T', lambda t, tc=tc, b=b: t.matmul(
                    b.t[0:65, :], st.cselB[:, :], qsq[0:64, TC[tc][0]:TC[tc][1]],
                    start=True, stop=True), deps=[dSqQ, st.dCsel, tPrevScores])
                dRowQ = c.op('S', lambda s, tc=tc, hh=hh, b=b: s.copy(
                    out=Qs[hh][64:65, TC[tc][0]:TC[tc][1]], in_=b.t[64:65, :]),
                    deps=[tR, dRowQ, prevA])
                b.consume(dRowQ)
            dSqK = c.op('S', lambda s, hh=hh: s.activation(
                out=qsq[0:64, :], in_=Ks[hh][0:64, :], func=AF.Square), deps=[dKe, dRowQ])
            dKm = None
            for tc in range(NTC):
                b = bk[4]
                b.guard()
                tR = c.op('T', lambda t, tc=tc, b=b: t.matmul(
                    b.t[0:65, :], st.cselB[:, :], qsq[0:64, TC[tc][0]:TC[tc][1]],
                    start=True, stop=True), deps=[dSqK])
                dKm = c.op('V', lambda v, tc=tc, b=b: v.tensor_reduce(
                    out=k2m[64:65, tc:tc + 1], in_=b.t[64:65, :], axis=AX.X, op=ALU.max),
                    deps=[tR, dKm])
                b.consume(dKm)
            dKm = c.op('V', lambda v: v.tensor_reduce(
                out=k2m[64:65, 4:5], in_=k2m[64:65, 0:4], axis=AX.X, op=ALU.max), deps=[dKm])
            dB = c.op('V', lambda v, hh=hh: v.tensor_scalar(
                out=Qs[hh][64:65, :], in0=Qs[hh][64:65, :], scalar1=k2m[64:65, 4:5], scalar2=None,
                op0=ALU.mult), deps=[dKm, dRowQ])
            dB = c.op('S', lambda s, hh=hh: s.activation(
                out=Qs[hh][64:65, :], in_=Qs[hh][64:65, :], func=AF.Sqrt), deps=[dB])
            dB = c.op('V', lambda v, hh=hh: v.tensor_tensor(
                out=Qs[hh][64:65, :], in0=Qs[hh][64:65, :], in1=siRow[64:65, :], op=ALU.add),
                deps=[dB, dSi])
            dB = c.op('V', lambda v, hh=hh: v.tensor_scalar(
                out=Qs[hh][64:65, :], in0=Qs[hh][64:65, :], scalar1=-1.0, scalar2=None,
                op0=ALU.mult), deps=[dB])
            dBs[hh] = dB
            dRows[hh] = (dIot, dSlp)
        units = [(hh, ib) for ib in range(NTC) for hh in range(2)]
        NU = len(units)
        dExpU = [None] * NU
        tPVU = [None] * NU
        dE = None
        for step in range(NU + 1):
            if step < NU:
                hh, ib = units[step]
                par = step % 2
                for jp in range(2 * ib, 8):
                    P = st.pairs[jp % 2]
                    P.guard()
                    tS = None
                    for hf in range(2):
                        jt = 2 * jp + hf
                        tS = c.op('T', lambda t, jt=jt, ib=ib, hh=hh, P=P, hf=hf: t.matmul(
                            P.half[hf].t[:, :], Ks[hh][0:67, jt * 128:(jt + 1) * 128],
                            Qs[hh][0:67, TC[ib][0]:TC[ib][1]],
                            start=True, stop=True),
                            deps=[dBs[hh], dKone, dRows[hh][0], dRows[hh][1], tPrevScores,
                                  tPVU[step - 2] if step >= 2 else None])
                    if jp < 2 * ib + 2:
                        for hf in range(2):
                            jt = 2 * jp + hf
                            delta = jt * 128 - ib * 512
                            tS = c.op('V', lambda v, P=P, hf=hf, delta=delta: v.tensor_tensor(
                                out=P.half[hf].t[:, :], in0=P.half[hf].t[:, :],
                                in1=l3m[:, 384 - delta:896 - delta], op=ALU.add),
                                deps=[tS, dM])
                    dE = c.op('S', lambda s, jp=jp, par=par, P=P: s.activation(
                        out=Ptd[par][:, jp, :], in_=P.t[:, :], func=AF.Exp),
                        deps=[tS, dE, tPVU[step - 2] if step >= 2 else None])
                    P.consume(dE)
                dExpU[step] = dE
            if step > 0:
                hh2, ib2 = units[step - 1]
                h = hg * 2 + hh2
                par2 = (step - 1) % 2
                b = bk[4 + (step - 1) % 2]
                b.guard()
                njt = NTOK - ib2 * 4
                for idx, jt in enumerate(range(ib2 * 4, NTOK)):
                    c.op('T', lambda t, jt=jt, h=h, b=b, par2=par2, idx=idx, njt=njt: t.matmul(
                        b.t[0:65, :], Vt[:, jt, h * 65:(h + 1) * 65],
                        Ptd[par2][:, jt // 2, (jt % 2) * 512:(jt % 2) * 512 + 512],
                        start=(idx == 0), stop=(idx == njt - 1)),
                        deps=[dExpU[step - 1]], inc=(1 if idx == njt - 1 else 0))
                tO = c.tip('T')
                tPVU[step - 1] = tO
                tPrevScores = tO
                dD = c.op('V', lambda v, b=b: v.tensor_scalar(
                    out=rowD[64:65, 0:512], in0=b.t[64:65, :], scalar1=1e-35, scalar2=None,
                    op0=ALU.add), deps=[tO, dBs[hh2]])
                b.consume(dD)
                dD = c.op('V', lambda v: v.reciprocal(
                    out=rowD[64:65, 0:512], in_=rowD[64:65, 0:512]), deps=[dD])
                dD = c.op('V', lambda v: v.tensor_copy(
                    out=recb[64:65, :], in_=rowD[64:65, 0:512]), deps=[dD, prevB5])
                p0 = (h % 2) * 64
                b6 = bk[6]
                b6.guard()
                tB5 = c.op('T', lambda t, p0=p0, b6=b6: t.matmul(
                    b6.t[p0:p0 + 64, :], st.ones[64:65, 0:64], recb[64:65, 0:512],
                    start=True, stop=True), deps=[dD, st.dOnes])
                if p0 == 0:
                    dO = c.op('S', lambda s, b=b: s.copy(out=Osb[0:64, :], in_=b.t[0:64, :]),
                              deps=[tO, prevA, dD])
                    b.consume(dO)
                else:
                    dOb = c.op('S', lambda s, b=b: s.copy(out=OsbB[0:64, :], in_=b.t[0:64, :]),
                               deps=[tO, prevA, prevSh, dD])
                    b.consume(dOb)
                    b7 = bk[7]
                    b7.guard()
                    tSh = c.op('T', lambda t, b7=b7: t.matmul(
                        b7.f[64:128, 0:512], st.identB[0:64, 0:64], OsbB[0:64, :],
                        start=True, stop=True), deps=[dOb, st.dIdB])
                    prevSh = tSh
                    dO = c.op('S', lambda s, b7=b7: s.copy(out=Osb[64:128, :], in_=b7.f[64:128, 0:512]),
                              deps=[tSh, prevA])
                    b7.consume(dO)
                dOn = c.op('V', lambda v, p0=p0, ib2=ib2, b6=b6: v.tensor_tensor(
                    out=OnG[p0:p0 + 64, TC[ib2][0]:TC[ib2][1]], in0=Osb[p0:p0 + 64, :],
                    in1=b6.t[p0:p0 + 64, :], op=ALU.mult), deps=[tB5, dO, tWoRead, dOn])
                b6.consume(dOn)
                prevB5 = tB5
                prevA = dOn
        if not L3F["wo"]:
            tWoRead = c.tip('T')
            continue
        dWo = c.dma('G', 'cwo', lambda g, hg=hg: g.dma_start(
            out=wos[:, :], in_=Wd["w_l3_wout"][hg * 128:(hg + 1) * 128, :]),
            deps=[tWoRead, c.tip('T')])
        for mc in range(KC):
            for tc in range(NTC):
                ob = bk[6]
                ob.guard()
                c.op('T', lambda t, mc=mc, tc=tc, ob=ob: t.matmul(
                    ob.t[:, :], wos[:, mc * 128:(mc + 1) * 128],
                    OnG[:, TC[tc][0]:TC[tc][1]],
                    start=True, stop=True), deps=[dWo, dOn])
                tD = c.tip('T')
                dAdd = c.op('V', lambda v, mc=mc, tc=tc, ob=ob: v.tensor_tensor(
                    out=st.xF[:, mc, TC[tc][0]:TC[tc][1]], in0=st.xF[:, mc, TC[tc][0]:TC[tc][1]],
                    in1=ob.t[:, :], op=ALU.add), deps=[tD])
                ob.consume(dAdd)
        tWoRead = c.tip('T')
    st.ar.pop()
    st.ar.pop()



def emit_l2_ffn(c, st, Wd):
    E = 8
    bk = st.banks
    c.phase()
    st.ar.push()
    Wt = st.ar.f32_3(NTOK, E)
    dWt = emit_routing(c, st, Wd, Wd["w_l2_rwt"], Wd["w_l2_rb"], E, 'l2', Wt)
    acc = st.ar.f32_3(NTOK, 1024)
    ews = [st.ar.bf3(KC, 512) for _ in range(2)]
    ewr = [st.ar.bf(512) for _ in range(2)]
    ewt = [None, None]
    accDep = dWt
    for e in range(E):
        for oc in range(2):
            sl = (e * 2 + oc) % 2
            ld = c.dma('G', f'cew_{sl}', lambda g, e=e, oc=oc, sl=sl: g.dma_start(
                out=ews[sl][:], in_=Wd["w_l2_ew"][e, 0:1024, oc * 512:(oc + 1) * 512]
                .rearrange("(c p) m -> p c m", p=128)), deps=[ewt[sl]])
            ldr = c.dma('G', f'cewr_{sl}', lambda g, e=e, oc=oc, sl=sl: g.dma_start(
                out=ewr[sl][0:1, :], in_=Wd["w_l2_ew"][e, 1024:1025, oc * 512:(oc + 1) * 512]),
                deps=[ewt[sl]])
            for tok in range(NTOK):
                ob = bk[tok % 2]
                ob.guard()
                for kc in range(KC):
                    c.op('T', lambda t, kc=kc, tok=tok, sl=sl, ob=ob: t.matmul(
                        ob.t[:, :], st.xnb[:, kc, tok * 128:(tok + 1) * 128], ews[sl][:, kc, :],
                        start=(kc == 0), stop=False), deps=[ld, st.xnbReady], inc=0)
                c.op('T', lambda t, sl=sl, ob=ob: t.matmul(
                    ob.t[:, :], st.ones[0:1, 0:128], ewr[sl][0:1, :],
                    start=False, stop=True), deps=[ldr, st.dOnes])
                tY = c.tip('T')
                if e == 0:
                    accDep = c.op('V', lambda v, tok=tok, oc=oc, e=e, ob=ob: v.tensor_scalar(
                        out=acc[:, tok, oc * 512:(oc + 1) * 512], in0=ob.t[:, :],
                        scalar1=Wt[:, tok, e:e + 1], scalar2=None, op0=ALU.mult),
                        deps=[tY, dWt, accDep])
                else:
                    accDep = c.op('V', lambda v, tok=tok, oc=oc, e=e, ob=ob: v.scalar_tensor_tensor(
                        out=acc[:, tok, oc * 512:(oc + 1) * 512], in0=ob.t[:, :],
                        scalar=Wt[:, tok, e:e + 1], in1=acc[:, tok, oc * 512:(oc + 1) * 512],
                        op0=ALU.mult, op1=ALU.add), deps=[tY, accDep])
                ob.consume(accDep)
            ewt[sl] = c.tip('T')
    tb = st.ar.bf(128)
    P = bk[7]
    dCast = None
    for tok in range(NTOK):
        for dc in range(KC):
            dCast = c.op('V', lambda v, tok=tok, dc=dc: v.tensor_copy(
                out=tb[:, :], in_=acc[:, tok, dc * 128:(dc + 1) * 128]), deps=[accDep, dCast])
            P.guard()
            tT = c.op('T', lambda t: t.transpose(P.t[:, 0:128], tb[:, :], st.identB[:]),
                      deps=[dCast, st.dIdB])
            dAdd = c.op('V', lambda v, tok=tok, dc=dc: v.tensor_tensor(
                out=st.xF[:, dc, tok * 128:(tok + 1) * 128],
                in0=st.xF[:, dc, tok * 128:(tok + 1) * 128], in1=P.t[:, 0:128], op=ALU.add),
                deps=[tT])
            P.consume(dAdd)
            dCast = dAdd
    st.ar.pop()


L2F = {"inter": True, "state": True, "wo": True, "nheads": 16, "core": True, "tmaj": True,
       "phi": True, "bias": True}


def emit_l2_attn(c, st, Wd):
    bk = st.banks
    c.phase()
    st.ar.push()
    phiq = [st.ar.bf(T), st.ar.bf(T)]
    phik = [st.ar.bf(T), st.ar.bf(T)]
    ktp = [st.ar.bf3(NTOK, 64), st.ar.bf3(NTOK, 64)]
    vtp = [st.ar.bf3(NTOK, 65), st.ar.bf3(NTOK, 65)]
    Sbf = st.ar.bf(160)
    OnG = st.ar.bf(T)
    Oraw = [st.ar.bf(T), st.ar.bf(T)]
    rowD2 = st.ar.f32(T)
    recb2 = st.ar.bf(T)
    Osb2 = st.ar.bf(512)
    wsl = [st.ar.bf3(KC, 64), st.ar.bf3(KC, 64)]
    wos = st.ar.bf(1024)
    worow = st.ar.bf(1024)
    vrow = st.ar.bf(64)
    rsc = st.ar.bf(512)
    msc = [st.ar.bf(512), st.ar.bf(512)]
    bqkc = st.ar.f32(32)
    zc64 = st.ar.f32(1)
    dZc = c.op('V', lambda v: v.memset(zc64[:, 0:1], 0.0))
    dBqk = c.dma('Y', 'cl2bc', lambda y: y.dma_start(out=bqkc[0:64, 0:16], in_=Wd["w_l2_bqc"][:, :]))
    dBqk = c.dma('Y', 'cl2bc', lambda y: y.dma_start(out=bqkc[0:64, 16:32], in_=Wd["w_l2_bkc"][:, :]))
    dBqk = ('cl2bc', c.semc['cl2bc'])
    dWoR = c.dma('G', 'cworow', lambda g: g.dma_start(
        out=worow[0:1, :], in_=Wd["w_l2_wo"][1024:1025, :]))
    tPrev = None
    tWoRead = None
    dOn = None
    prevPhi = None
    dD0 = None
    prevB6 = None
    for pr in range(8):
        dPq = [None, None]
        dPk = [None, None]
        dKt = [None, None]
        dVt = [None, None]
        for hh in range(2):
            h = 2 * pr + hh
            def proj_f(wkey, dst, dPrev, bcol, h=h, hh=hh):
                dSlab = c.dma('G', f'cl2w{hh}', lambda g, wkey=wkey, h=h, hh=hh: g.dma_start(
                    out=wsl[hh][:], in_=Wd[wkey][0:1024, h * 64:(h + 1) * 64]
                    .rearrange("(c p) m -> p c m", p=128)), deps=[c.tip('T')])
                dE = dPrev
                for tc in range(NTC):
                    b = bk[tc % 2]
                    b.guard()
                    for kc in range(KC):
                        c.op('T', lambda t, kc=kc, tc=tc, b=b, hh=hh: t.matmul(
                            b.t[0:64, :], wsl[hh][:, kc, :], st.xnb[:, kc, TC[tc][0]:TC[tc][1]],
                            start=(kc == 0), stop=(kc == KC - 1)),
                            deps=[dSlab, st.xnbReady], inc=(1 if kc == KC - 1 else 0))
                    tU = c.tip('T')
                    dR = c.op('S', lambda s, b=b, bcol=bcol: s.activation(
                        out=rsc[0:64, :], in_=b.t[0:64, :], func=AF.Relu,
                        bias=bcol), deps=[tU, dE, dBqk])
                    dMn = c.op('V', lambda v, b=b, bcol=bcol, hh=hh: v.tensor_scalar(
                        out=msc[hh][0:64, :], in0=b.t[0:64, :], scalar1=bcol, scalar2=zc64[0:64, 0:1],
                        op0=ALU.add, op1=ALU.min), deps=[tU, dE, dR, dBqk, dZc])
                    b.consume(dR)
                    b.consume(dMn)
                    dX = c.op('S', lambda s, hh=hh: s.activation(
                        out=msc[hh][0:64, :], in_=msc[hh][0:64, :], func=AF.Exp), deps=[dMn, dR])
                    dE = c.op('V', lambda v, tc=tc, dst=dst, hh=hh: v.tensor_tensor(
                        out=dst[0:64, TC[tc][0]:TC[tc][1]], in0=rsc[0:64, :], in1=msc[hh][0:64, :],
                        op=ALU.add), deps=[dX])
                return dE
            dPq[hh] = proj_f("w_l2_wq", phiq[hh], prevPhi if hh == 0 else dPk[0],
                             bqkc[0:64, h:h + 1])
            dPk[hh] = proj_f("w_l2_wk", phik[hh], dPq[hh], bqkc[0:64, 16 + h:17 + h])
            dKt[hh] = dPk[hh]
            b7 = bk[7]
            for ct in range(NTOK):
                b7.guard()
                tT = c.op('T', lambda t, ct=ct, b7=b7, hh=hh: t.transpose(
                    b7.t[0:128, 0:64], phik[hh][0:64, ct * 128:(ct + 1) * 128],
                    st.identB[0:64, 0:64]), deps=[dPk[hh], st.dIdB])
                dKt[hh] = c.op('V', lambda v, ct=ct, b7=b7, hh=hh: v.tensor_copy(
                    out=ktp[hh][:, ct, :], in_=b7.t[0:128, 0:64]), deps=[tT, dKt[hh]])
                b7.consume(dKt[hh])
            dSlab = c.dma('G', f'cl2w{hh}', lambda g, h=h, hh=hh: g.dma_start(
                out=wsl[hh][:], in_=Wd["w_l2_wv"][0:1024, h * 64:(h + 1) * 64]
                .rearrange("(c p) m -> p c m", p=128)), deps=[c.tip('T'), dKt[hh]])
            dRow2 = c.dma('G', f'cl2wr{hh}', lambda g, h=h, hh=hh: g.dma_start(
                out=vrow[0:1, :], in_=Wd["w_l2_wv"][1024:1025, h * 64:(h + 1) * 64]),
                deps=[c.tip('T')])
            dVm = c.op('V', lambda v, hh=hh: v.memset(vtp[hh][:, :, :], 1.0),
                       deps=[dKt[hh], tPrev])
            dVt[hh] = dVm
            for jt in range(NTOK):
                b = bk[2 + jt % 2]
                b.guard()
                for kc in range(KC):
                    c.op('T', lambda t, kc=kc, jt=jt, b=b, hh=hh: t.matmul(
                        b.t[:, 0:64], st.xnb[:, kc, jt * 128:(jt + 1) * 128], wsl[hh][:, kc, :],
                        start=(kc == 0), stop=False), deps=[dSlab, st.xnbReady], inc=0)
                c.op('T', lambda t, b=b: t.matmul(
                    b.t[:, 0:64], st.ones[0:1, 0:128], vrow[0:1, :],
                    start=False, stop=True), deps=[dRow2, st.dOnes])
                tU = c.tip('T')
                dVt[hh] = c.op('V', lambda v, jt=jt, b=b, hh=hh: v.tensor_copy(
                    out=vtp[hh][:, jt, 0:64], in_=b.t[:, 0:64]), deps=[tU, dVm, dVt[hh]])
                b.consume(dVt[hh])
        SOFF = [0, 80]
        ABANK = [bk[0], bk[1]]
        OBANK = [bk[2], bk[4]]
        SBANK = [bk[5], bk[3]]
        dS0 = [None, None]
        dSe = [None, None]
        prevO = [None, None]
        dOh = [None, None]
        for hh in range(2):
            dS0[hh] = c.op('V', lambda v, hh=hh: v.memset(Sbf[0:64, SOFF[hh]:SOFF[hh] + 65], 0.0),
                           deps=[tPrev, dVt[hh], dPq[hh], dD0])
            dSe[hh] = dS0[hh]
        for ct in range(NTOK):
            tA = [None, None]
            dAm = [None, None]
            tO = [None, None]
            tS = [None, None]
            for hh in range(2):
                ba = ABANK[hh]
                ba.guard()
                tA[hh] = c.op('T', lambda t, ct=ct, ba=ba, hh=hh: t.matmul(
                    ba.t[:, 0:128], phik[hh][0:64, ct * 128:(ct + 1) * 128],
                    phiq[hh][0:64, ct * 128:(ct + 1) * 128],
                    start=True, stop=True), deps=[dPq[hh], dPk[hh], dVt[hh]])
            for hh in range(2):
                dAm[hh] = c.op('V', lambda v, ba=ABANK[hh], hh=hh: v.tensor_tensor(
                    out=msc[hh][:, 0:128], in0=ba.t[:, 0:128], in1=st.triuB[:, :],
                    op=ALU.mult), deps=[tA[hh], st.dTril, prevO[hh]])
                ABANK[hh].consume(dAm[hh])
            for hh in range(2):
                bo = OBANK[hh]
                bo.guard()
                c.op('T', lambda t, ct=ct, bo=bo, hh=hh: t.matmul(
                    bo.t[0:65, 0:128], vtp[hh][:, ct, :], msc[hh][:, 0:128],
                    start=True, stop=(ct == 0)), deps=[dAm[hh]], inc=(1 if ct == 0 else 0))
                if ct > 0:
                    c.op('T', lambda t, ct=ct, bo=bo, hh=hh: t.matmul(
                        bo.t[0:65, 0:128], Sbf[0:64, SOFF[hh]:SOFF[hh] + 65],
                        phiq[hh][0:64, ct * 128:(ct + 1) * 128],
                        start=False, stop=True), deps=[dSe[hh]])
                tO[hh] = c.tip('T')
            for hh in range(2):
                bs = SBANK[hh]
                if ct == 0:
                    bs.guard()
                tS[hh] = c.op('T', lambda t, ct=ct, bs=bs, hh=hh: t.matmul(
                    bs.t[0:64, 0:65], ktp[hh][:, ct, :], vtp[hh][:, ct, :],
                    start=(ct == 0), stop=True, skip_group_check=True),
                    deps=[dKt[hh], dVt[hh], dSe[hh]])
            for hh in range(2):
                if ct < NTOK - 1:
                    dSe[hh] = c.op('V', lambda v, bs=SBANK[hh], hh=hh: v.tensor_copy(
                        out=Sbf[0:64, SOFF[hh]:SOFF[hh] + 65], in_=bs.t[0:64, 0:65]),
                        deps=[tS[hh], dSe[hh]])
                    SBANK[hh].consume(dSe[hh])
                dOh[hh] = c.op('S', lambda s, ct=ct, bo=OBANK[hh], hh=hh: s.copy(
                    out=Oraw[hh][0:65, ct * 128:(ct + 1) * 128], in_=bo.t[0:65, 0:128]),
                    deps=[tO[hh], prevO[hh], dD0])
                OBANK[hh].consume(dOh[hh])
                prevO[hh] = dOh[hh]
        for hh in range(2):
            p0 = hh * 64
            dD0 = c.op('V', lambda v, hh=hh: v.tensor_scalar(
                out=rowD2[64:65, :], in0=Oraw[hh][64:65, :], scalar1=st.cst[0:1, 1:2],
                scalar2=None, op0=ALU.add), deps=[prevO[hh], st.dCst, dD0])
            dD0 = c.op('V', lambda v: v.reciprocal(
                out=rowD2[64:65, :], in_=rowD2[64:65, :]), deps=[dD0])
            dD0 = c.op('V', lambda v: v.tensor_copy(
                out=recb2[64:65, :], in_=rowD2[64:65, :]), deps=[dD0, prevB6])
            for tc in range(NTC):
                b6 = bk[6]
                b6.guard()
                tB6 = c.op('T', lambda t, tc=tc, p0=p0, b6=b6: t.matmul(
                    b6.t[p0:p0 + 64, :], st.ones[64:65, 0:64], recb2[64:65, TC[tc][0]:TC[tc][1]],
                    start=True, stop=True), deps=[dD0, st.dOnes])
                prevB6 = tB6
                if p0 == 0:
                    dOn = c.op('V', lambda v, tc=tc, b6=b6, hh=hh: v.tensor_tensor(
                        out=OnG[0:64, TC[tc][0]:TC[tc][1]], in0=Oraw[hh][0:64, TC[tc][0]:TC[tc][1]],
                        in1=b6.t[0:64, :], op=ALU.mult), deps=[tB6, tWoRead, dOn, prevO[hh]])
                    b6.consume(dOn)
                else:
                    b3 = bk[3]
                    b3.guard()
                    tSh = c.op('T', lambda t, tc=tc, b3=b3, hh=hh: t.matmul(
                        b3.t[64:128, :], st.identB[0:64, 0:64], Oraw[hh][0:64, TC[tc][0]:TC[tc][1]],
                        start=True, stop=True), deps=[prevO[hh], st.dIdB])
                    dSh = c.op('S', lambda s, tc=tc, b3=b3: s.copy(
                        out=Osb2[64:128, :], in_=b3.t[64:128, :]), deps=[tSh, dOn])
                    b3.consume(dSh)
                    dOn = c.op('V', lambda v, tc=tc, b6=b6: v.tensor_tensor(
                        out=OnG[64:128, TC[tc][0]:TC[tc][1]], in0=Osb2[64:128, :],
                        in1=b6.t[64:128, :], op=ALU.mult), deps=[tB6, dSh, tWoRead, dOn])
                    b6.consume(dOn)
        tPrev = c.tip('T')
        prevPhi = dOn
        dWo = c.dma('G', 'cwo2', lambda g, pr=pr: g.dma_start(
            out=wos[:, :], in_=Wd["w_l2_wo"][pr * 128:(pr + 1) * 128, :]),
            deps=[tWoRead, c.tip('T')])
        for mc in range(KC):
            for tc in range(NTC):
                ob = bk[4]
                ob.guard()
                c.op('T', lambda t, mc=mc, tc=tc, ob=ob, pr=pr: t.matmul(
                    ob.t[:, :], wos[:, mc * 128:(mc + 1) * 128], OnG[:, TC[tc][0]:TC[tc][1]],
                    start=True, stop=(pr != 0)), deps=[dWo, dOn],
                    inc=(1 if pr != 0 else 0))
                if pr == 0:
                    c.op('T', lambda t, mc=mc, tc=tc, ob=ob: t.matmul(
                        ob.t[:, :], worow[0:1, mc * 128:(mc + 1) * 128], st.ones[0:1, 0:512],
                        start=False, stop=True), deps=[dWoR, st.dOnes])
                tD = c.tip('T')
                dAdd = c.op('V', lambda v, mc=mc, tc=tc, ob=ob: v.tensor_tensor(
                    out=st.xF[:, mc, TC[tc][0]:TC[tc][1]], in0=st.xF[:, mc, TC[tc][0]:TC[tc][1]],
                    in1=ob.t[:, :], op=ALU.add), deps=[tD])
                ob.consume(dAdd)
        tWoRead = c.tip('T')
        tPrev = tWoRead
    st.ar.pop()


def emit_rms(c, st, chunks, inv_n, prevDep, wdep=None, scr=None):
    bk = st.banks
    if scr is None:
        st.ar.push()
        sq = [st.ar.bf(T), st.ar.bf(T)]
        r32 = st.ar.f32(T)[0:1, :]
        rbf = st.ar.bf(T)[0:1, :]
    else:
        sq = [scr[:, 0:2048], scr[:, 2048:4096]]
        r32 = scr[:, 4096:8192].bitcast(F32)[0:1, :]
        rbf = scr[:, 0:2048][0:1, :]
    for tc in range(NTC):
        bk[tc].guard()
    tRd = [None, None]
    nch = len(chunks)
    for i, (srcv, dstv, wcol) in enumerate(chunks):
        b = i % 2
        dSq = c.op('S', lambda s, srcv=srcv, b=b: s.activation(
            out=sq[b][:], in_=srcv, func=AF.Square), deps=[prevDep, tRd[b]])
        for tc in range(NTC):
            c.op('T', lambda t, tc=tc, b=b, i=i: t.matmul(
                bk[tc].t[0:1, :], st.ones[:, 0:1], sq[b][:, TC[tc][0]:TC[tc][1]],
                start=(i == 0), stop=(i == nch - 1)),
                deps=[dSq, st.dOnes], inc=(1 if tc == NTC - 1 else 0))
        tRd[b] = c.tip('T')
    tVar = c.tip('T')
    dR = None
    for tc in range(NTC):
        dR = c.op('S', lambda s, tc=tc: s.activation(
            out=r32[0:1, TC[tc][0]:TC[tc][1]], in_=bk[tc].t[0:1, :],
            func=AF.Sqrt, scale=inv_n, bias=st.cst[0:1, 0:1]), deps=[tVar, st.dCst])
        bk[tc].consume(dR)
    dRec = c.op('V', lambda v: v.reciprocal(out=r32[:, :], in_=r32[:, :]), deps=[dR])
    dRb = c.op('V', lambda v: v.tensor_copy(out=rbf[:, :], in_=r32[:, :]), deps=[dRec])
    out_dep = None
    for tc in range(NTC):
        bk[4].guard()
        tB = c.op('T', lambda t, tc=tc: t.matmul(
            bk[4].t[:, :], st.ones[0:1, 0:128], rbf[0:1, TC[tc][0]:TC[tc][1]],
            start=True, stop=True), deps=[dRb])
        for i, (srcv, dstv, wcol) in enumerate(chunks):
            d = c.op('V', lambda v, tc=tc, srcv=srcv, dstv=dstv, wcol=wcol: v.scalar_tensor_tensor(
                out=dstv[:, TC[tc][0]:TC[tc][1]], in0=srcv[:, TC[tc][0]:TC[tc][1]],
                scalar=wcol, in1=bk[4].t[:, :], op0=ALU.mult, op1=ALU.mult),
                deps=[tB, wdep, c.tip('S')])
            bk[4].consume(d)
            out_dep = d
    if scr is None:
        st.ar.pop()
    return out_dep


def emit_l1_ffn(c, st, Wd):
    E = 32
    bk = st.banks
    c.phase()
    st.ar.push()
    Wt = st.ar.f32_3(NTOK, E)
    dWt = emit_routing(c, st, Wd, Wd["w_l1_rwt"], Wd["w_l1_rb"], E, 'l1', Wt)
    WeT = st.ar.bf(T)
    dWeT = emit_wt_transpose(c, st, Wt, WeT, E, dWt)
    eye = st.ar.bf(4096)
    eyev = eye.rearrange("p (e m) -> p e m", e=32)
    dEye = c.dma('G', 'cmisc', lambda g: g.dma_start(
        out=eyev[0:32], in_=Wd["c_eyeb"][:, :].rearrange("e (f m) -> e f m", f=32)),
        deps=[dWeT])
    wb = st.ar.bf(T)
    st.ar.push()
    rs = mk_glu_rings(c, st, 1024, 'l1s')
    emit_glu(c, st, rs, st.xnb, st.xnbReady, Wd["w_l1_sgw"], Wd["w_l1_suw"],
             Wd["w_l1_sdw"], 1024, 1)
    st.ar.pop()
    c.phase()
    st.ar.push()
    re_ = mk_glu_rings(c, st, 512, 'l1e')
    for e in range(E):
        dwb = None
        for tc in range(NTC):
            bk[6].guard()
            tB = c.op('T', lambda t, tc=tc, e=e: t.matmul(
                bk[6].t[:, :], eyev[0:32, e, :], WeT[0:32, TC[tc][0]:TC[tc][1]],
                start=True, stop=True), deps=[dWeT, dEye])
            dwb = c.op('V', lambda v, tc=tc: v.tensor_copy(
                out=wb[:, TC[tc][0]:TC[tc][1]], in_=bk[6].t[:, :]), deps=[tB, dwb])
            bk[6].consume(dwb)
        emit_glu(c, st, re_, st.xnb, st.xnbReady, Wd["w_l1_gw"][e], Wd["w_l1_uw"][e],
                 Wd["w_l1_dw"][e], 512, 1, wbv=wb, dwb=dwb)
    st.ar.pop()
    st.ar.pop()


def emit_l1_attn(c, st, Wd):
    bk = st.banks
    c.phase()
    st.ar.push()
    awcol = st.ar.f32(8)
    mwcol = st.ar.f32(8)
    owcol = st.ar.f32(1)
    dAw = c.dma('Y', 'cnw', lambda y: y.dma_start(out=awcol[:, :], in_=Wd["w_l1_anw"][:, :]))
    dMw = c.dma('Y', 'cnw', lambda y: y.dma_start(out=mwcol[:, :], in_=Wd["w_l1_mnw"][:, :]))
    dOw = c.dma('Y', 'cnw', lambda y: y.dma_start(out=owcol[:, :], in_=Wd["w_l1_onw"][:, :]))
    dNw = ('cnw', c.semc['cnw'])
    h1 = st.ar.bf3(KC, T)
    h1v = [h1[:, kc, :] for kc in range(KC)]
    dH1 = emit_rms(c, st, [(st.xnb[:, kc, :], h1v[kc], awcol[:, kc:kc + 1]) for kc in range(KC)],
                   1.0 / 1024.0, st.xnbReady, wdep=dNw)
    st.ar.push()
    x1b = st.ar.bf3(KC, T)
    OnP = st.ar.bf3(2, T)
    pk4 = st.ar.bf(8192)
    qf = pk4[:, 0:2048]
    kf = pk4[:, 2048:4096]
    kt = pk4[:, 4096:6144].rearrange("p (a b) -> p a b", a=NTOK)
    vt = pk4[:, 6144:8192].rearrange("p (a b) -> p a b", a=NTOK)
    wsl = [st.ar.bf3(KC, 128) for _ in range(2)]
    wop = st.ar.bf(2048)
    wopv = wop.rearrange("p (w m) -> p w m", w=2)
    Sbf = st.ar.bf(128)
    Am = st.ar.bf(128)
    dX1 = None
    for kc in range(KC):
        dX1 = c.op('V', lambda v, kc=kc: v.tensor_copy(
            out=x1b[:, kc, :], in_=st.xnb[:, kc, :]), deps=[st.xnbReady, dX1, c.tip('T')])
    wslT = [None, None]
    dOnP = None
    tWoRead = None
    for h in range(8):
        pp = h % 2
        dSl = c.dma('G', 'cl1w_0', lambda g, h=h: g.dma_start(
            out=wsl[0][:], in_=Wd["w_l1_wq"][:, h * 128:(h + 1) * 128]
            .rearrange("(c p) m -> p c m", p=128)), deps=[wslT[0]])
        dQf = None
        for tc in range(NTC):
            b = bk[tc % 2]
            b.guard()
            for kc in range(KC):
                c.op('T', lambda t, kc=kc, tc=tc, b=b: t.matmul(
                    b.t[:, :], wsl[0][:, kc, :], h1[:, kc, TC[tc][0]:TC[tc][1]],
                    start=(kc == 0), stop=(kc == KC - 1)),
                    deps=[dSl, dH1], inc=(1 if kc == KC - 1 else 0))
            tQ = c.tip('T')
            dQf = c.op('S', lambda s, tc=tc, b=b: s.copy(
                out=qf[:, TC[tc][0]:TC[tc][1]], in_=b.t[:, :]), deps=[tQ, dQf])
            b.consume(dQf)
        wslT[0] = c.tip('T')
        dSk = c.dma('G', 'cl1w_1', lambda g, h=h: g.dma_start(
            out=wsl[1][:], in_=Wd["w_l1_wk"][:, h * 128:(h + 1) * 128]
            .rearrange("(c p) m -> p c m", p=128)), deps=[wslT[1]])
        dKf = None
        for tc in range(NTC):
            b = bk[2 + tc % 2]
            b.guard()
            for kc in range(KC):
                c.op('T', lambda t, kc=kc, tc=tc, b=b: t.matmul(
                    b.t[:, :], wsl[1][:, kc, :], h1[:, kc, TC[tc][0]:TC[tc][1]],
                    start=(kc == 0), stop=(kc == KC - 1)),
                    deps=[dSk, dH1], inc=(1 if kc == KC - 1 else 0))
            tK = c.tip('T')
            dKf = c.op('S', lambda s, tc=tc, b=b: s.copy(
                out=kf[:, TC[tc][0]:TC[tc][1]], in_=b.t[:, :]), deps=[tK, dKf])
            b.consume(dKf)
        wslT[1] = c.tip('T')
        dSkt = c.dma('G', 'cl1w_0', lambda g, h=h: g.dma_start(
            out=wsl[0][:], in_=Wd["w_l1_wk"][:, h * 128:(h + 1) * 128]
            .rearrange("(c p) m -> p c m", p=128)), deps=[wslT[0]])
        dKt = None
        for jt in range(NTOK):
            b = bk[jt % 2]
            b.guard()
            for kc in range(KC):
                c.op('T', lambda t, kc=kc, jt=jt, b=b: t.matmul(
                    b.t[:, 0:128], h1[:, kc, jt * 128:(jt + 1) * 128], wsl[0][:, kc, :],
                    start=(kc == 0), stop=(kc == KC - 1)),
                    deps=[dSkt, dH1], inc=(1 if kc == KC - 1 else 0))
            tU = c.tip('T')
            dKt = c.op('V', lambda v, jt=jt, b=b: v.tensor_copy(
                out=kt[:, jt, :], in_=b.t[:, 0:128]), deps=[tU, dKt])
            b.consume(dKt)
        wslT[0] = c.tip('T')
        dSv = c.dma('G', 'cl1w_1', lambda g, h=h: g.dma_start(
            out=wsl[1][:], in_=Wd["w_l1_wv"][:, h * 128:(h + 1) * 128]
            .rearrange("(c p) m -> p c m", p=128)), deps=[wslT[1]])
        dVt = None
        for jt in range(NTOK):
            b = bk[2 + jt % 2]
            b.guard()
            for kc in range(KC):
                c.op('T', lambda t, kc=kc, jt=jt, b=b: t.matmul(
                    b.t[:, 0:128], h1[:, kc, jt * 128:(jt + 1) * 128], wsl[1][:, kc, :],
                    start=(kc == 0), stop=(kc == KC - 1)),
                    deps=[dSv, dH1], inc=(1 if kc == KC - 1 else 0))
            tU = c.tip('T')
            dVt = c.op('V', lambda v, jt=jt, b=b: v.tensor_copy(
                out=vt[:, jt, :], in_=b.t[:, 0:128]), deps=[tU, dVt])
            b.consume(dVt)
        wslT[1] = c.tip('T')
        dS0 = c.op('V', lambda v: v.memset(Sbf[:, :], 0.0), deps=[c.tip('T'), dOnP])
        dSe = dS0
        prevO = None
        dOh = None
        for ct in range(NTOK):
            ba = bk[ct % 2]
            ba.guard()
            tA = c.op('T', lambda t, ct=ct, ba=ba: t.matmul(
                ba.t[:, 0:128], kf[:, ct * 128:(ct + 1) * 128], qf[:, ct * 128:(ct + 1) * 128],
                start=True, stop=True), deps=[dQf, dKf, dVt, dKt])
            dAm = c.op('V', lambda v, ba=ba: v.tensor_tensor(
                out=Am[:, :], in0=ba.t[:, 0:128], in1=st.triuB[:, :],
                op=ALU.mult), deps=[tA, st.dTril, prevO])
            ba.consume(dAm)
            bo = bk[4]
            bo.guard()
            c.op('T', lambda t, ct=ct, bo=bo: t.matmul(
                bo.t[:, 0:128], vt[:, ct, :], Am[:, :],
                start=True, stop=(ct == 0)), deps=[dAm], inc=(1 if ct == 0 else 0))
            if ct > 0:
                c.op('T', lambda t, ct=ct, bo=bo: t.matmul(
                    bo.t[:, 0:128], Sbf[:, :], qf[:, ct * 128:(ct + 1) * 128],
                    start=False, stop=True), deps=[dSe])
            tO = c.tip('T')
            bs = bk[5]
            if ct == 0:
                bs.guard()
            tS = c.op('T', lambda t, ct=ct, bs=bs: t.matmul(
                bs.t[:, 0:128], kt[:, ct, :], vt[:, ct, :],
                start=(ct == 0), stop=True, skip_group_check=True), deps=[dKt, dVt, dSe])
            if ct < NTOK - 1:
                dSe = c.op('V', lambda v, bs=bs: v.tensor_copy(
                    out=Sbf[:, :], in_=bs.t[:, 0:128]), deps=[tS, dSe])
                bs.consume(dSe)
            dOh = c.op('S', lambda s, ct=ct, pp=pp, bo=bo: s.copy(
                out=OnP[:, pp, ct * 128:(ct + 1) * 128], in_=bo.t[:, 0:128]),
                deps=[tO, dOh, tWoRead, dOnP])
            bo.consume(dOh)
            prevO = dOh
        dOnP = emit_rms(c, st, [(OnP[:, pp, :], OnP[:, pp, :], owcol[:, 0:1])],
                        1.0 / 128.0, dOh, wdep=dNw, scr=pk4)
        if pp == 1:
            pr = h // 2
            dWo = c.dma('G', 'cl1wo', lambda g, pr=pr: g.dma_start(
                out=wopv[:], in_=Wd["w_l1_wo"][pr * 256:(pr + 1) * 256, :]
                .rearrange("(w p) m -> p w m", p=128)), deps=[tWoRead, c.tip('T')])
            for mc in range(KC):
                for tc in range(NTC):
                    ob = bk[6]
                    ob.guard()
                    for w in range(2):
                        c.op('T', lambda t, w=w, mc=mc, tc=tc, ob=ob: t.matmul(
                            ob.t[:, :], wopv[:, w, mc * 128:(mc + 1) * 128],
                            OnP[:, w, TC[tc][0]:TC[tc][1]],
                            start=(w == 0), stop=(w == 1)),
                            deps=[dWo, dOnP], inc=(1 if w == 1 else 0))
                    tD = c.tip('T')
                    dAdd = c.op('V', lambda v, mc=mc, tc=tc, ob=ob: v.tensor_tensor(
                        out=x1b[:, mc, TC[tc][0]:TC[tc][1]], in0=x1b[:, mc, TC[tc][0]:TC[tc][1]],
                        in1=ob.t[:, :], op=ALU.add), deps=[tD, dX1])
                    ob.consume(dAdd)
                    dX1 = dAdd
            tWoRead = c.tip('T')
    dXF = None
    for kc in range(KC):
        for tc in range(NTC):
            dXF = c.op('V', lambda v, kc=kc, tc=tc: v.tensor_tensor(
                out=st.xF[:, kc, TC[tc][0]:TC[tc][1]], in0=st.xF[:, kc, TC[tc][0]:TC[tc][1]],
                in1=x1b[:, kc, TC[tc][0]:TC[tc][1]], op=ALU.add), deps=[dX1, dXF])
    dH2 = emit_rms(c, st, [(x1b[:, kc, :], h1v[kc], mwcol[:, kc:kc + 1]) for kc in range(KC)],
                   1.0 / 1024.0, dXF, wdep=dNw, scr=pk4)
    st.ar.pop()
    c.phase()
    st.ar.push()
    rm = mk_glu_rings(c, st, 4096, 'l1m')
    emit_glu(c, st, rm, h1, dH2, Wd["w_l1_wgate"], Wd["w_l1_wup"],
             Wd["w_l1_wdown"], 4096, 4)
    st.ar.pop()
    st.ar.pop()


BLOCK_EMITTERS = {}


def emit_rows(c, st):
    st.ar.push()
    st.mu32 = st.ar.f32(T)[0:1, :]
    st.rstd32 = st.ar.f32(T)[0:1, :]
    st.mu_bf = st.ar.bf(T)[0:1, :]
    st.rstd_bf = st.ar.bf(T)[0:1, :]


def emit_rows_pop(c, st):
    st.ar.pop()


def emit_block0(c, st, Wd):
    emit_rows(c, st)
    emit_ln(c, st)
    emit_l0_ffn(c, st, Wd)
    emit_rows_pop(c, st)
    emit_l0_attn(c, st, Wd)


BLOCK_EMITTERS[0] = (emit_block0, ("w_l0_",))


def emit_block3(c, st, Wd):
    emit_rows(c, st)
    emit_ln(c, st)
    emit_l3_ffn(c, st, Wd)
    emit_rows_pop(c, st)
    emit_l3_attn(c, st, Wd)


BLOCK_EMITTERS[3] = (emit_block3, ("w_l3_",))


def emit_block2(c, st, Wd):
    emit_rows(c, st)
    emit_ln(c, st)
    emit_l2_ffn(c, st, Wd)
    emit_rows_pop(c, st)
    emit_l2_attn(c, st, Wd)


def emit_block2f(c, st, Wd):
    emit_rows(c, st)
    emit_ln(c, st)
    emit_l2_ffn(c, st, Wd)
    emit_rows_pop(c, st)


def emit_block2a(c, st, Wd):
    emit_rows(c, st)
    emit_ln(c, st)
    emit_rows_pop(c, st)
    emit_l2_attn(c, st, Wd)


BLOCK_EMITTERS[2] = (emit_block2, ("w_l2_",))


def emit_block1(c, st, Wd):
    emit_rows(c, st)
    emit_ln(c, st)
    emit_l1_ffn(c, st, Wd)
    emit_rows_pop(c, st)
    emit_l1_attn(c, st, Wd)


def emit_block1f(c, st, Wd):
    emit_rows(c, st)
    emit_ln(c, st)
    emit_l1_ffn(c, st, Wd)
    emit_rows_pop(c, st)


def emit_block1a(c, st, Wd):
    emit_rows(c, st)
    emit_ln(c, st)
    emit_rows_pop(c, st)
    emit_l1_attn(c, st, Wd)


BLOCK_EMITTERS[1] = (emit_block1, ("w_l1_",))
BLOCK_EMITTERS[10] = (emit_block1f, ("w_l1_",))
BLOCK_EMITTERS[11] = (emit_block1a, ("w_l1_",))
BLOCK_EMITTERS[20] = (emit_block2f, ("w_l2_",))
BLOCK_EMITTERS[21] = (emit_block2a, ("w_l2_",))


def emit_block3f(c, st, Wd):
    emit_rows(c, st)
    emit_ln(c, st)
    emit_l3_ffn(c, st, Wd)
    emit_rows_pop(c, st)


def emit_block3a(c, st, Wd):
    emit_rows(c, st)
    emit_ln(c, st)
    emit_rows_pop(c, st)
    emit_l3_attn(c, st, Wd)


BLOCK_EMITTERS[30] = (emit_block3f, ("w_l3_",))
BLOCK_EMITTERS[31] = (emit_block3a, ("w_l3_",))


def build_program(prep, consts, blocks):
    nc = bass.Bass("TRN2", target_bir_lowering=False)
    x_in = nc.dram_tensor("x_in", [D, T], F32, kind="ExternalInput").ap()
    y_out = nc.dram_tensor("y_out", [D, T], F32, kind="ExternalOutput").ap()
    Wd = {}
    need = set()
    for b in blocks:
        need.update(BLOCK_EMITTERS[b][1])
    for k, arr in list(consts.items()) + list(prep.items()):
        if k.startswith("c_") or any(k.startswith(p) for p in need):
            Wd[k] = nc.dram_tensor(k, list(arr.shape), F32, kind="ExternalInput").ap()
    c = Ctx(nc)
    st = St()
    st.xF = c.sb("xF", [128, KC, T], F32)
    st.xnb = c.sb("xnb", [128, KC, T], BF16)
    st.ones = c.sb("ones", [128, 512], BF16)
    st.onesF = c.sb("onesF", [128, 128], F32)
    st.identB = c.sb("identB", [128, 128], BF16)
    st.identF = c.sb("identF", [128, 128], F32)
    st.cselB = c.sb("cselB", [64, 65], BF16)
    st.trilB = c.sb("trilB", [128, 128], BF16)
    st.triuB = c.sb("triuB", [128, 128], BF16)
    st.cst = c.sb("cst", [1, 16], F32)
    st.pairs = [Pair(c, j) for j in range(3)]
    st.banks = [st.pairs[0].half[0], st.pairs[0].half[1],
                st.pairs[1].half[0], st.pairs[1].half[1],
                st.pairs[2].half[0], st.pairs[2].half[1],
                Bank(c, 6), Bank(c, 7, bf16=True)]
    st.ar = Arena(c, ARCOLS)
    emit_consts(c, st, Wd)
    emit_load_x(c, st, x_in)
    for b in blocks:
        BLOCK_EMITTERS[b][0](c, st, Wd)
    emit_store_x(c, st, y_out)
    return c.finish()


def _in_maps(prep, consts, blocks, xs):
    need = set()
    for b in blocks:
        need.update(BLOCK_EMITTERS[b][1])
    base = {}
    for k, arr in list(consts.items()) + list(prep.items()):
        if k.startswith("c_") or any(k.startswith(p) for p in need):
            base[k] = arr
    return [dict(base, x_in=x) for x in xs]


STAGES = [[0, 1, 2, 3]]


def kernel(**inputs):
    z = inputs
    prep = host_prep(z)
    consts = host_consts()
    x = np.asarray(z["L0_attn_x"], np.float32)
    B = x.shape[0]
    xs = [np.ascontiguousarray(x[b].T) for b in range(B)]
    for blocks in STAGES:
        nc = build_program(prep, consts, blocks)
        res = run_bass_kernel_spmd(nc, _in_maps(prep, consts, blocks, xs),
                                   core_ids=list(range(B)))
        xs = [np.asarray(r["y_out"], np.float32) for r in res.results]
    out = np.stack([xf.T for xf in xs], axis=0)
    return np.ascontiguousarray(out)
